# version 01
# baseline (speedup 1.0000x reference)
import contextlib
import numpy as np
import ml_dtypes

import concourse.bass as bass
import concourse.mybir as mybir
from concourse.bass_utils import run_bass_kernel_spmd

F32 = mybir.dt.float32
BF16 = mybir.dt.bfloat16
I32 = mybir.dt.int32
AF = mybir.ActivationFunctionType
ALU = mybir.AluOpType
AX = mybir.AxisListType
BF = ml_dtypes.bfloat16

D = 1024
NCH = 8
TILE = 512
NTILE = 8
M = TILE * NTILE
H = 16
NM = TILE // 128


class Sem:
    def __init__(self, stack, nc, name):
        self.h = stack.enter_context(nc.semaphore(name))
        self.n = 0

    def inc(self, instr, k=1):
        instr.then_inc(self.h, k)
        self.n += k
        return (self, self.n)


def _wait(eng, tok):
    if tok is None:
        return
    sem, val = tok
    if val > 0:
        eng.wait_ge(sem.h, val)


def _later(*toks):
    toks = [t for t in toks if t is not None]
    if not toks:
        return None
    sems = {id(t[0]) for t in toks}
    assert len(sems) == 1, "mixed-semaphore _later; pass tokens separately"
    return max(toks, key=lambda t: t[1])


def _latest(*toks):
    best = {}
    for t in toks:
        if t is None:
            continue
        k = id(t[0])
        if k not in best or t[1] > best[k][1]:
            best[k] = t
    return tuple(best.values())


class Bank:
    __slots__ = ("ps", "free")

    def __init__(self, ps):
        self.ps = ps
        self.free = None


SERIAL = [False]


def build_program(n_tiles=NTILE, taps=(), upto=None):
    nc = bass.Bass("TRN2", target_bir_lowering=False, debug=False)
    tapset = set(taps)

    def din(name, shape, dt=F32):
        return nc.dram_tensor(name, shape, dt, kind="ExternalInput").ap()

    x_d = din("x", [D, M])
    xh_d = din("xh", [D, 128])
    qkvw_d = din("qkv_w", [D, 3 * D])
    rqw_d = din("rq_w", [D, 256])
    rkvw_d = din("rkv_w", [D, 512])
    routw_d = din("rout_w", [256, D])
    outw_d = din("out_w", [D, D])
    glw_d = din("gl_w", [64, 1])
    grw_d = din("gr_w", [64, 1])
    w1_d = din("w1", [4096, D])
    w2_d = din("w2", [D, 4096])
    wlam_d = din("W_lambda", [16, D])
    wbm_d = din("W_B", [16, D])
    wc_d = din("W_C", [D, 16])
    wgate_d = din("W_gate", [D, D])
    wgq_d = din("wg", [3072, D], I32)
    wuq_d = din("wu", [3072, D], I32)
    wdq_d = din("wd", [D, 3072], I32)
    bqkv_d = din("b_qkv", [128, 24])
    brq_d = din("b_rq", [128, 2])
    brkv_d = din("b_rkv", [128, 4])
    brout_d = din("b_rout", [128, 8])
    bout_d = din("b_out", [128, 8])
    b1_d = din("b_1", [128, 32])
    b2_d = din("b_2", [128, 8])
    blam_d = din("b_lam", [16, 1])
    rmsw_d = din("rmsw", [128, 8])
    vbrow_d = din("vb_row", [1, D])
    rvrow_d = din("rv_row", [1, 256])
    ident_d = din("c_ident", [128, 128], BF16)
    mask_d = din("c_mask", [128, 256], BF16)
    halom_d = din("c_halom", [128, 128], BF16)
    latm_d = din("c_latm", [128, M], BF16)
    oh16_d = din("c_oh16", [128, 16, 16], BF16)
    ohlat_d = din("c_ohlat", [128, 2, 4], BF16)
    bc2_d = din("c_bc2", [64, 8, 128], BF16)
    bclat_d = din("c_bclat", [4, 2, 128], BF16)
    ones1_d = din("c_ones1", [1, 128], BF16)
    onesf_d = din("c_onesf", [128, 1])
    selH_d = din("c_selH", [8, 1])
    selA_d = din("c_selA", [128, 8])
    selC_d = din("c_selC", [128, 8])
    padm_d = din("c_padm", [128, 128], BF16)

    out_d = nc.dram_tensor("out", [D, M], F32, kind="ExternalOutput").ap()

    TAPSPECS = {
        "z0": ([128, NCH, TILE], BF16), "qT": ([128, NCH, TILE], BF16),
        "kT": ([128, NCH, TILE], BF16), "vtok": ([128, NCH, TILE], BF16),
        "localu": ([128, NCH, TILE], BF16), "lsum": ([16, TILE], F32),
        "mixed": ([128, NCH, TILE], BF16), "remote": ([128, NCH, TILE], BF16),
        "X1": ([128, NCH, TILE], F32), "X2": ([128, NCH, TILE], F32),
        "hft": ([16, 1], F32), "initv": ([16, 1], F32),
        "rkall": ([128, 2, 64], BF16), "rvlm": ([128, 256], BF16),
        "latn": ([128, 2, TILE], BF16), "grow": ([16, TILE], F32),
        "hsl": ([16, TILE], F32), "acum": ([16, TILE], F32),
        "X3": ([128, NCH, TILE], F32), "z1": ([128, NCH, TILE], BF16),
        "z2": ([128, NCH, TILE], BF16), "z3": ([128, NCH, TILE], BF16),
        "hffn": ([128, 32, TILE], BF16), "pooled": ([128, NCH, 128], BF16),
        "alpha1": ([128, 32], F32), "lam": ([16, TILE], F32),
        "bx": ([16, TILE], F32), "yv": ([128, NCH, TILE], BF16),
        "gate1": ([128, NCH, TILE], BF16), "rmsrow": ([1, TILE], F32),
        "latu": ([128, 2, TILE], BF16), "latsum": ([4, TILE], F32),
        "rqT": ([128, 2, TILE], BF16), "ktail": ([128, NCH, 128], BF16),
        "vtail": ([128, D], BF16), "hz": ([128, NCH, 128], BF16), "cx2": ([128, NCH, TILE], F32), "cz2": ([128, NCH, TILE], BF16), "chsl": ([16, TILE], F32),
    }
    tap_d = {tn: nc.dram_tensor("tap_" + tn, sh, dt, kind="ExternalOutput").ap()
             for tn, (sh, dt) in TAPSPECS.items() if tn in tapset}

    def scr(name, shape, dt=BF16):
        return nc.dram_tensor(name, shape, dt, kind="Internal").ap()

    qkvB = scr("qkvB", [D, 3 * D])
    rqB = scr("rqB", [D, 256])
    rkvB = scr("rkvB", [D, 512])
    routB = scr("routB", [256, D])
    outwB = scr("outwB", [D, D])
    w1sB = scr("w1sB", [4096, D])
    w2sB = scr("w2sB", [D, 4096])
    WgB = scr("WgB", [D, D])
    wgB = scr("wgB", [3072, D])
    wuB = scr("wuB", [3072, D])
    wdB = scr("wdB", [D, 3072])
    x2d = scr("x2d", [D, M], F32)
    z2d = scr("z2d", [D, M], BF16)
    hsd = scr("hsd", [16, M], F32)
    acd = scr("acd", [16, M], F32)
    agkv_in = scr("agkv_in", [3, 128, 128], BF16)
    agkv_out = scr("agkv_out", [8, 3, 128, 128], BF16)
    agh_in = scr("agh_in", [16, 1], F32)
    agh_out = scr("agh_out", [128, 1], F32)

    stack = contextlib.ExitStack()
    with stack:
        def sbuf(name, shape, dt):
            return stack.enter_context(nc.sbuf_tensor(name, shape, dt))

        FA = sbuf("FA", [128, 10240], F32)
        BA = sbuf("BA", [128, 32768], BF16)

        def carve(arena, off, shape):
            n = 1
            for s_ in shape[1:]:
                n *= s_
            ap = arena[0:shape[0], off:off + n]
            if len(shape) == 3:
                ap = ap.rearrange("p (a b) -> p a b", a=shape[1])
            return ap

        X = carve(FA, 0, [128, NCH, TILE])
        sqs = [carve(FA, 4096 + i * TILE, [128, TILE]) for i in range(2)]
        srow = carve(FA, 5120, [1, 3, TILE])
        wtmp = [carve(FA, 0, [128, 4096]),
                carve(FA, 4096, [128, 4096])]
        lsum = carve(FA, 6656, [16, TILE])
        latsum = carve(FA, 7168, [4, TILE])
        grow = carve(FA, 7680, [16, TILE])
        gtmp = carve(FA, 8192, [16, TILE])
        lamrow = carve(FA, 8704, [16, TILE])
        bxrow = carve(FA, 9216, [16, TILE])
        zer16 = carve(FA, 9728, [16, TILE])
        FA2 = sbuf("FA2", [16, 2 * TILE], F32)
        hslb = FA2[0:16, 0:TILE]
        acumb = FA2[0:16, TILE:2 * TILE]

        z = carve(BA, 0, [128, NCH, TILE])
        qT = carve(BA, 4096, [128, NCH, TILE])
        kT = carve(BA, 8192, [128, NCH, TILE])
        vtok = carve(BA, 12288, [128, NCH, TILE])
        bag1 = carve(BA, 16384, [128, NCH, TILE])
        bag2 = carve(BA, 20480, [128, NCH, TILE])
        big1 = carve(BA, 4096, [128, 32, TILE])
        wsgn = [carve(BA, 4096, [128, 4096]),
                carve(BA, 8192, [128, 4096])]
        agkv_s = carve(BA, 12288, [128, 24, 128])
        rqTb = carve(BA, 24576, [128, 2, TILE])
        lexp = carve(BA, 25600, [128, 2, TILE])
        latu = carve(BA, 26624, [128, 2, TILE])
        latnb = carve(BA, 24576, [128, 2, TILE])
        etile = carve(BA, 27648, [128, 2, 256])
        hz = carve(BA, 28160, [128, NCH, 128])
        mtmp = carve(BA, 29184, [128, TILE])
        srowb = carve(BA, 29696, [1, 2, TILE])
        srowb_rms = carve(BA, 30720, [1, TILE])
        latsb = carve(BA, 31232, [4, TILE])
        ghb = carve(BA, 31744, [16, TILE])
        gm1b = carve(BA, 32256, [16, TILE])

        ident = sbuf("ident", [128, 128], BF16)
        mask01 = sbuf("mask01", [128, 256], BF16)
        halom = sbuf("halom", [128, 128], BF16)
        latm = sbuf("latm_s", [128, M], BF16)
        oh16 = sbuf("oh16", [128, 16, 16], BF16)
        ohlat = sbuf("ohlat", [128, 2, 4], BF16)
        bc2 = sbuf("bc2", [64, 8, 128], BF16)
        bclat = sbuf("bclat", [4, 2, 128], BF16)
        ones1 = sbuf("ones1", [1, 128], BF16)
        onesf = sbuf("onesf", [128, 1], F32)
        ones1b = sbuf("ones1b", [128, 1], BF16)
        selHs = sbuf("selHs", [8, 1], F32)
        selAs = sbuf("selAs", [128, 8], F32)
        selCs = sbuf("selCs", [128, 8], F32)
        padm = sbuf("padm", [128, 128], BF16)
        bqkv = sbuf("bqkv", [128, 24], F32)
        brq = sbuf("brq", [128, 2], F32)
        brkv = sbuf("brkv", [128, 4], F32)
        brout = sbuf("brout", [128, 8], F32)
        bout = sbuf("bout", [128, 8], F32)
        b1v = sbuf("b1v", [128, 32], F32)
        b2v = sbuf("b2v", [128, 8], F32)
        blam = sbuf("blam", [16, 1], F32)
        rmsw = sbuf("rmsw_s", [128, 8], F32)
        vb_bc = sbuf("vb_bc", [128, D], BF16)
        alpha1 = sbuf("alpha1", [128, 32], F32)
        alpha2 = sbuf("alpha2", [128, 8], F32)
        lamTw = sbuf("lamTw", [128, NCH, 16], BF16)
        wbTw = sbuf("wbTw", [128, NCH, 16], BF16)
        wcT = sbuf("wcT", [16, NCH, 128], BF16)
        glgr = sbuf("glgr", [128, 2, 8, 16], BF16)
        rkallT = sbuf("rkallT", [128, 2, 64], BF16)
        rvlm = sbuf("rvlm", [128, 256], BF16)
        smu0 = sbuf("smu0", [1, NTILE, TILE], BF16)
        sinv0 = sbuf("sinv0", [1, NTILE, TILE], BF16)
        hmu = sbuf("hmu", [1, 128], BF16)
        hinv = sbuf("hinv", [1, 128], BF16)
        ktail = sbuf("ktail", [128, NCH, 128], BF16)
        vtail = sbuf("vtail", [128, D], BF16)
        scarry = sbuf("scarry", [16, 1], F32)
        ccarry = sbuf("ccarry", [16, 1], F32)
        initv = sbuf("initv", [16, 1], F32)
        pooledT = sbuf("pooledT", [128, NCH, 128], BF16)
        glw_s = sbuf("glw_s", [64, 1], F32)
        grw_s = sbuf("grw_s", [64, 1], F32)
        rk_con = sbuf("rk_con", [128, 2, 64], BF16)
        rv_con = sbuf("rv_con", [128, 256], BF16)
        hsbf = sbuf("hsbf", [16, TILE], BF16)
        pooledF = sbuf("pooledF", [128, NCH, 32], F32)
        agh_s = sbuf("agh_s", [8, 16], F32)
        vbrow_s = sbuf("vbrow_s", [1, D], F32)
        vbrow_b = sbuf("vbrow_b", [1, D], BF16)
        rvrow_s = sbuf("rvrow_s", [1, 256], F32)
        rvrow_b = sbuf("rvrow_b", [1, 256], BF16)
        wlam_s = sbuf("wlam_s", [16, D], F32)
        wlam_b = sbuf("wlam_b", [16, D], BF16)
        wbm_s = sbuf("wbm_s", [16, D], F32)
        wbm_b = sbuf("wbm_b", [16, D], BF16)
        wc_s = sbuf("wc_s", [128, NCH, 16], F32)
        wc_b = sbuf("wc_b", [128, NCH, 16], BF16)
        wsl = [sbuf(f"wsl{i}", [128, 8192], BF16) for i in range(2)]

        def wslv(i, nkc, cols):
            return wsl[i][:, 0:nkc * cols].rearrange("p (a b) -> p a b", a=nkc)

        banks = [Bank(stack.enter_context(nc.psum_tensor(f"pb{i}", [128, 512], F32)))
                 for i in range(8)]

        block = stack.enter_context(nc.Block())
        qS = Sem(stack, nc, "qS")
        qG = Sem(stack, nc, "qG")
        qA = Sem(stack, nc, "qA")
        sPE = Sem(stack, nc, "sPE")
        sDVE = Sem(stack, nc, "sDVE")
        sACT = Sem(stack, nc, "sACT")
        sGP = Sem(stack, nc, "sGP")

        allsems = [qS, qG, qA, sPE, sDVE, sACT, sGP]

        def _serial_waits(e):
            if SERIAL[0]:
                for s_ in allsems:
                    if s_.n > 0:
                        e.wait_ge(s_.h, s_.n)

        def dmas(qname, pairs, waits=(), transpose=False):
            sec, sem = {"S": (block.sync, qS), "G": (block.gpsimd, qG),
                        "A": (block.scalar, qA)}[qname]
            box = {}

            def f(e):
                _serial_waits(e)
                for w in waits:
                    _wait(e, w)
                for out, in_ in pairs:
                    i = e.dma_start(out=out, in_=in_, transpose=transpose)
                    box["t"] = sem.inc(i, 16)
            sec(f)
            return box["t"]

        def dma(qname, out, in_, waits=(), transpose=False):
            return dmas(qname, [(out, in_)], waits=waits, transpose=transpose)

        def pe(ops, waits=()):
            box = {}

            def f(t):
                _serial_waits(t)
                for w in waits:
                    _wait(t, w)
                last = None
                for o in ops:
                    kw = {}
                    if o.get("tp") is not None:
                        kw["tile_position"] = o["tp"]
                    last = t.matmul(o["out"], o["lhsT"], o["rhs"],
                                    start=o.get("start", True),
                                    stop=o.get("stop", True), **kw)
                box["t"] = sPE.inc(last)
            block.tensor(f)
            return box["t"]

        def tr(out_ps, in_ap, waits=(), idn=128):
            return pe([{"out": out_ps, "lhsT": in_ap, "rhs": ident[0:idn, 0:idn]}],
                      waits=waits)

        def ve(fn, waits=()):
            box = {}

            def f(v):
                _serial_waits(v)
                for w in waits:
                    _wait(v, w)
                box["t"] = sDVE.inc(fn(v))
            block.vector(f)
            return box["t"]

        def ae(fn, waits=()):
            box = {}

            def f(a):
                _serial_waits(a)
                for w in waits:
                    _wait(a, w)
                box["t"] = sACT.inc(fn(a))
            block.scalar(f)
            return box["t"]

        def ge(fn, waits=()):
            box = {}

            def f(g):
                _serial_waits(g)
                for w in waits:
                    _wait(g, w)
                box["t"] = sGP.inc(fn(g))
            block.gpsimd(f)
            return box["t"]

        def tap(name, src_ap, waits=()):
            if name in tap_d:
                return dma("A", tap_d[name], src_ap, waits=waits)
            return None

        dummy = sbuf("dummy_jp", [1, 1], F32)

        def joinpoint():
            toks = [(s_, s_.n) for s_ in (qS, qG, qA, sPE, sDVE, sACT, sGP) if s_.n > 0]
            return ve(lambda v: v.memset(dummy[:, :], 0.0), waits=tuple(toks))

        def finish():
            def f(g):
                _wait(g, (qS, qS.n))
                _wait(g, (qA, qA.n))
                _wait(g, (qG, qG.n))
            block.gpsimd(f)
            return nc

        t_consts = dmas("S", [
            (ident[:], ident_d), (mask01[:], mask_d), (halom[:], halom_d),
            (latm[:], latm_d), (oh16[:], oh16_d), (ohlat[:], ohlat_d),
            (bc2[:], bc2_d), (bclat[:], bclat_d), (ones1[:], ones1_d),
            (onesf[:], onesf_d), (selHs[:], selH_d), (selAs[:], selA_d),
            (selCs[:], selC_d), (padm[:], padm_d), (bqkv[:], bqkv_d),
            (brq[:], brq_d), (brkv[:], brkv_d), (brout[:], brout_d),
            (bout[:], bout_d), (b1v[:], b1_d), (b2v[:], b2_d),
            (blam[:], blam_d), (rmsw[:], rmsw_d), (vbrow_s[:], vbrow_d),
            (rvrow_s[:], rvrow_d),
            (glw_s[:], glw_d), (grw_s[:], grw_d), (wlam_s[:], wlam_d),
            (wbm_s[:], wbm_d),
            (wc_s[:], wc_d.rearrange("(c p) n -> p c n", p=128)),
        ])

        t_qkvB = dma("G", qkvB, qkvw_d)
        t_rkvB = dma("G", rkvB, rkvw_d)
        t_rqB = dma("G", rqB, rqw_d)
        t_routB = dma("G", routB, routw_d)
        t_outwB = dma("G", outwB, outw_d)
        t_WgB = dma("G", WgB, wgate_d)
        t_wgB = dma("G", wgB, wgq_d)
        t_wuB = dma("G", wuB, wuq_d)
        t_wdB = dma("G", wdB, wdq_d)

        def prep_misc(v):
            v.memset(scarry[:], 0.0)
            v.memset(ccarry[:], 1.0)
            v.memset(ones1b[:], 1.0)
            v.memset(glgr[:], 0.0)
            v.tensor_copy(vbrow_b[:], vbrow_s[:])
            v.tensor_copy(rvrow_b[:], rvrow_s[:])
            v.tensor_copy(wlam_b[:], wlam_s[:])
            v.tensor_copy(wbm_b[:], wbm_s[:])
            last = v.tensor_copy(wc_b[:], wc_s[:])
            for c in range(8):
                v.tensor_copy(glgr[0:64, 0, c, 2 * c:2 * c + 1], glw_s[:, :])
                v.tensor_copy(glgr[64:128, 0, c, 2 * c + 1:2 * c + 2], glw_s[:, :])
                v.tensor_copy(glgr[0:64, 1, c, 2 * c:2 * c + 1], grw_s[:, :])
                last = v.tensor_copy(glgr[64:128, 1, c, 2 * c + 1:2 * c + 2], grw_s[:, :])
            return last
        t_misc = ve(prep_misc, waits=(t_consts,))

        tbv = pe([{"out": banks[0].ps[:, 0:512], "lhsT": ones1[:, :], "rhs": vbrow_b[:, 0:512]},
                  {"out": banks[1].ps[:, 0:512], "lhsT": ones1[:, :], "rhs": vbrow_b[:, 512:1024]}],
                 waits=(t_misc,))

        def f_vb(v):
            v.tensor_copy(vb_bc[:, 0:512], banks[0].ps[:, 0:512])
            return v.tensor_copy(vb_bc[:, 512:1024], banks[1].ps[:, 0:512])
        t_vb = ve(f_vb, waits=(tbv,))
        banks[0].free = t_vb
        banks[1].free = t_vb

        for cch in range(NCH):
            bk = banks[cch % 2]
            tr(bk.ps[:, 0:16], wlam_b[:, cch * 128:(cch + 1) * 128],
               waits=(t_misc, bk.free), idn=16)
            tb_ = tr(bk.ps[:, 16:32], wbm_b[:, cch * 128:(cch + 1) * 128], idn=16)

            def f(v, bk=bk, cch=cch):
                v.tensor_copy(lamTw[:, cch, :], bk.ps[:, 0:16])
                return v.tensor_copy(wbTw[:, cch, :], bk.ps[:, 16:32])
            bk.free = ve(f, waits=(tb_,))
        for cch in range(NCH):
            bk = banks[cch % 2]
            ta = tr(bk.ps[0:16, 0:128], wc_b[:, cch, :], waits=(t_misc, bk.free))
            bk.free = ve(lambda v, bk=bk, cch=cch:
                         v.tensor_copy(wcT[:, cch, :], bk.ps[0:16, 0:128]),
                         waits=(ta,))

        wtok = [None, None]
        rtok = [None, None]
        for it in range(32):
            pb = it % 2
            t_l = dma("S", wtmp[pb][:, 0:1024], w1_d[it * 128:(it + 1) * 128, :],
                      waits=tuple(w for w in (wtok[pb], rtok[pb]) if w))
            t_s = ae(lambda a, pb=pb: a.activation(out=wsgn[pb][:, 0:1024],
                                                   in_=wtmp[pb][:, 0:1024], func=AF.Sign),
                     waits=(t_l,))
            rtok[pb] = ve(lambda v, pb=pb, it=it: v.reduce_sum(
                alpha1[:, it:it + 1], wtmp[pb][:, 0:1024],
                axis=AX.X, apply_absolute_value=True), waits=(t_l,))
            wtok[pb] = dma("G", w1sB[it * 128:(it + 1) * 128, :], wsgn[pb][:, 0:1024],
                           waits=(t_s,))
        wtok2 = [None, None]
        for it in range(8):
            pb = it % 2
            t_l = dma("S", wtmp[pb][:, :], w2_d[it * 128:(it + 1) * 128, :],
                      waits=tuple(w for w in (wtok2[pb] or wtok[pb], rtok[pb]) if w))
            t_s = ae(lambda a, pb=pb: a.activation(out=wsgn[pb][:, :], in_=wtmp[pb][:, :],
                                                   func=AF.Sign),
                     waits=(t_l,))
            rtok[pb] = ve(lambda v, pb=pb, it=it: v.reduce_sum(
                alpha2[:, it:it + 1], wtmp[pb][:, :],
                axis=AX.X, apply_absolute_value=True), waits=(t_l,))
            wtok2[pb] = dma("G", w2sB[it * 128:(it + 1) * 128, :], wsgn[pb][:, :],
                            waits=(t_s,))

        def f_alpha(v):
            v.memset(zer16[:, :], 0.0)
            v.tensor_scalar_mul(alpha1[:], alpha1[:], 1.0 / 1024.0)
            return v.tensor_scalar_mul(alpha2[:], alpha2[:], 1.0 / 4096.0)
        t_alpha = ve(f_alpha)
        t_signs = _latest(*wtok, *wtok2, t_alpha, *rtok)
        tap("alpha1", alpha1[:], waits=(t_alpha,))

        def ln_stats(zget, T_, mu_out, inv_out, eps, extra=(), srcf32=True):
            b6, b7 = banks[6], banks[7]
            onesz = onesf if srcf32 else ones1b
            ops = []
            for c in range(NCH):
                ops.append({"out": b6.ps[0:1, 0:T_], "lhsT": onesz[:, :],
                            "rhs": zget(c)[:, 0:T_], "start": c == 0, "stop": c == NCH - 1})
            tsum = pe(ops, waits=(b6.free,) + tuple(extra))
            tsq_mm = None
            prev = [None, None]
            for c in range(NCH):
                pb = c % 2
                wl = [prev[pb]] + (list(extra) if c < 2 else [])
                t_sq = ae(lambda a, c=c, pb=pb: a.activation(
                    out=sqs[pb][:, 0:T_], in_=zget(c)[:, 0:T_], func=AF.Square),
                    waits=tuple(w for w in wl if w))
                tsq_mm = pe([{"out": b7.ps[0:1, 0:T_], "lhsT": onesf[:, :],
                              "rhs": sqs[pb][:, 0:T_], "start": c == 0, "stop": c == NCH - 1}],
                            waits=(t_sq, b7.free) if c == 0 else (t_sq,))
                prev[pb] = tsq_mm

            def fin(v):
                v.tensor_scalar_mul(srow[:, 0, 0:T_], b6.ps[0:1, 0:T_], 1.0 / D)
                v.tensor_scalar_mul(srow[:, 1, 0:T_], b7.ps[0:1, 0:T_], 1.0 / D)
                v.tensor_mul(srow[:, 2, 0:T_], srow[:, 0, 0:T_], srow[:, 0, 0:T_])
                v.tensor_sub(srow[:, 1, 0:T_], srow[:, 1, 0:T_], srow[:, 2, 0:T_])
                return v.tensor_scalar_add(srow[:, 1, 0:T_], srow[:, 1, 0:T_], float(eps))
            t_fin = ve(fin, waits=(tsum, tsq_mm))
            t_sq2 = ae(lambda a: a.activation(out=srow[:, 2, 0:T_], in_=srow[:, 1, 0:T_],
                                              func=AF.Sqrt),
                       waits=(t_fin,))

            def fin2(v):
                v.reciprocal(srow[:, 2, 0:T_], srow[:, 2, 0:T_])
                v.tensor_copy(mu_out, srow[:, 0, 0:T_])
                return v.tensor_copy(inv_out, srow[:, 2, 0:T_])
            t2 = ve(fin2, waits=(t_sq2,))
            b6.free = t2
            b7.free = t2
            return t2

        def ln_bcast(mu_row, inv_row, T_, extra=()):
            bkA, bkB = banks[4], banks[5]
            return pe([{"out": bkA.ps[:, 0:T_], "lhsT": ones1[:, :], "rhs": mu_row},
                       {"out": bkB.ps[:, 0:T_], "lhsT": ones1[:, :], "rhs": inv_row}],
                      waits=tuple(w for w in (bkA.free, bkB.free) + tuple(extra) if w))

        def ln_apply(xget, zget, T_, t_bc):
            bkA, bkB = banks[4], banks[5]
            tk = None
            for c in range(NCH):
                def f(v, c=c):
                    v.tensor_sub(zget(c)[:, 0:T_], xget(c)[:, 0:T_], bkA.ps[:, 0:T_])
                    return v.tensor_mul(zget(c)[:, 0:T_], zget(c)[:, 0:T_], bkB.ps[:, 0:T_])
                tk = ve(f, waits=(t_bc,) if c == 0 else ())
            bkA.free = tk
            bkB.free = tk
            return tk

        def full_ln(xget, zget, T_, eps, extra=(), srcf32=True,
                    mu_out=None, inv_out=None, stats=True):
            mu_out = srowb[:, 0, 0:T_] if mu_out is None else mu_out
            inv_out = srowb[:, 1, 0:T_] if inv_out is None else inv_out
            if stats:
                t_st = ln_stats(xget, T_, mu_out, inv_out, eps, extra=extra,
                                srcf32=srcf32)
                t_bc = ln_bcast(mu_out, inv_out, T_, extra=(t_st,))
            else:
                t_bc = ln_bcast(mu_out, inv_out, T_, extra=_latest(*extra))
            return ln_apply(xget, zget, T_, t_bc)

        t_pool = None
        t_z0 = None
        t_ap = None
        for tt in list(range(1, n_tiles)) + [0]:
            t_x = dma("S", X[:],
                      x_d.rearrange("(c p) n -> p c n", p=128)[:, :, tt * TILE:(tt + 1) * TILE],
                      waits=t_signs + ((t_ap,) if t_ap else ()))
            t_ap = full_ln(lambda c: X[:, c, :], lambda c: z[:, c, :], TILE, 1e-5,
                           extra=(t_x,) + t_signs, mu_out=smu0[:, tt, :],
                           inv_out=sinv0[:, tt, :])
            for c in range(NCH):
                t_pool = ve(lambda v, c=c, tt=tt: v.reduce_sum(
                    pooledF[:, c, tt * 4:(tt + 1) * 4],
                    z[:, c, :].rearrange("p (g w) -> p g w", w=128), axis=AX.X),
                    waits=(t_ap,) if c == 0 else ())
            if tt == 0:
                t_z0 = t_ap
        tap("z0", z[:], waits=(t_z0,))

        t_xh = dma("G", hz[:], xh_d.rearrange("(c p) n -> p c n", p=128),
                   waits=t_signs)
        t_hst = ln_stats(lambda c: hz[:, c, :], 128, hmu[:, :], hinv[:, :], 1e-5,
                         extra=(t_xh,), srcf32=False)
        t_hbc = ln_bcast(hmu[:, :], hinv[:, :], 128, extra=(t_hst,))
        t_hz = ln_apply(lambda c: hz[:, c, :], lambda c: hz[:, c, :], 128, t_hbc)
        tap("hz", hz[:], waits=(t_hz,))

        def f_pad(v):
            last = None
            for c in range(NCH):
                for reg in range(4):
                    v.tensor_copy(pooledT[:, c, reg * 32:(reg + 1) * 32], pooledF[:, c, :])
                last = v.tensor_mul(pooledT[:, c, :], pooledT[:, c, :], padm[:, :])
            return last
        t_pad = ve(f_pad, waits=(t_pool, t_misc))
        tap("pooled", pooledT[:], waits=(t_pad,))

        t_rkvl = dma("S", wslv(0, 8, 512)[:, :, :],
                     rkvB.rearrange("(c p) n -> p c n", p=128), waits=(t_rkvB,))
        wsl_busy = [None, None]
        for oc in range(2):
            bk = banks[oc]
            ops = [{"out": bk.ps[:, 0:64], "lhsT": wslv(0, 8, 512)[:, kc, oc * 128:(oc + 1) * 128],
                    "rhs": pooledT[:, kc, 0:64], "start": kc == 0, "stop": kc == NCH - 1}
                   for kc in range(NCH)]
            t_mm = pe(ops, waits=(t_rkvl, t_pad, bk.free))
            bk.free = ve(lambda v, bk=bk, oc=oc: v.tensor_copy(rk_con[:, oc, :], bk.ps[:, 0:64]),
                         waits=(t_mm,))
        bkv = banks[2]
        ops = [{"out": bkv.ps[:, 0:256], "lhsT": pooledT[:, kc, :],
                "rhs": wslv(0, 8, 512)[:, kc, 256:512], "start": kc == 0, "stop": kc == NCH - 1}
               for kc in range(NCH)]
        t_rvmm = pe(ops, waits=(t_rkvl, t_pad, bkv.free))
        wsl_busy[0] = t_rvmm
        bkv.free = ve(lambda v: v.tensor_copy(rv_con[:, :], bkv.ps[:, 0:256]),
                      waits=(t_rvmm,))

        t_agw = dmas("G", [(agkv_in[0], rk_con[:].rearrange("p a b -> p (a b)")),
                           (agkv_in[1], rv_con[:, 0:128]),
                           (agkv_in[2], rv_con[:, 128:256])],
                     waits=(banks[0].free, banks[1].free, bkv.free))
        box1 = {}

        def f_cc1(g):
            _wait(g, t_agw)
            i = g.collective_compute("AllGather", ALU.bypass,
                                     replica_groups=[[0, 1, 2, 3, 4, 5, 6, 7]],
                                     ins=[agkv_in], outs=[agkv_out])
            box1["t"] = sGP.inc(i)
        block.gpsimd(f_cc1)
        t_agr = dmas("S", [(agkv_s[:, s_ * 8:(s_ + 1) * 8, :],
                            agkv_out[:, s_, :, :].rearrange("r p n -> p r n"))
                           for s_ in range(3)],
                     waits=(box1["t"],))

        def f_comb(v):
            last = None
            for s_ in range(3):
                dst = (rkallT[:].rearrange("p a b -> p (a b)") if s_ == 0
                       else rvlm[:, (s_ - 1) * 128:s_ * 128])
                v.tensor_scalar(dst, agkv_s[:, s_ * 8, :], selAs[:, 0:1], None, ALU.mult)
                for r in range(1, 8):
                    v.scalar_tensor_tensor(dst, agkv_s[:, s_ * 8 + r, :], selAs[:, r:r + 1],
                                           dst, ALU.mult, ALU.add)
                for r in range(8):
                    last = v.scalar_tensor_tensor(dst, agkv_s[:, s_ * 8 + r, :], selCs[:, r:r + 1],
                                                  dst, ALU.mult, ALU.add)
            return last
        t_comb = ve(f_comb, waits=(t_agr,))

        t_rvbb = pe([{"out": banks[3].ps[:, 0:256], "lhsT": ones1[:, :],
                      "rhs": rvrow_b[:, 0:256]}], waits=(banks[3].free, t_misc))

        def f_bias(v):
            v.tensor_scalar(rkallT[:, 0, :], rkallT[:, 0, :], brkv[:, 0:1], None, ALU.add)
            v.tensor_scalar(rkallT[:, 1, :], rkallT[:, 1, :], brkv[:, 1:2], None, ALU.add)
            return v.tensor_add(rvlm[:, :], rvlm[:, :], banks[3].ps[:, 0:256])
        t_rkrv = ve(f_bias, waits=(t_comb, t_rvbb))
        banks[3].free = t_rkrv
        tap("rkall", rkallT[:], waits=(t_rkrv,))
        tap("rvlm", rvlm[:], waits=(t_rkrv,))

        if upto == "prep":
            return finish()

        wsl_busy = [None, None]

        def proj(dramB, K, NOUT, rhs_get, evac, transposed=False, t_src=(),
                 slab_cols=512):
            nkc = K // 128
            nslab = NOUT // slab_cols
            socc = slab_cols // 128
            slab_tok = [None, None]
            mm_tok = [wsl_busy[0], wsl_busy[1]]
            for j in range(nslab + 1):
                pb = j % 2
                if j < nslab:
                    w0 = tuple(w for w in (slab_tok[pb], mm_tok[pb]) if w)
                    w0 = w0 + (tuple(t_src) if j == 0 else ())
                    if not transposed:
                        slab_tok[pb] = dma(
                            "S", wslv(pb, nkc, slab_cols)[:, :, :],
                            dramB.rearrange("(c p) n -> p c n", p=128)[
                                :, :, j * slab_cols:(j + 1) * slab_cols],
                            waits=w0)
                    else:
                        prs = [(wslv(pb, nkc, slab_cols)[:, kc, :],
                                dramB[j * slab_cols:(j + 1) * slab_cols,
                                      kc * 128:(kc + 1) * 128])
                               for kc in range(nkc)]
                        slab_tok[pb] = dmas("S", prs, transpose=True, waits=w0)
                if j == 0:
                    continue
                jj = j - 1
                pbc = jj % 2
                for oc in range(socc):
                    og = jj * socc + oc
                    bk = banks[og % 2]
                    ops = [{"out": bk.ps[:, 0:TILE],
                            "lhsT": wslv(pbc, nkc, slab_cols)[:, kc, oc * 128:(oc + 1) * 128],
                            "rhs": rhs_get(kc),
                            "start": kc == 0, "stop": kc == nkc - 1}
                           for kc in range(nkc)]
                    t_mm = pe(ops, waits=tuple(w for w in (slab_tok[pbc], bk.free) if w))
                    mm_tok[pbc] = t_mm
                    bk.free = evac(bk, og, t_mm)
            wsl_busy[0] = mm_tok[0]
            wsl_busy[1] = mm_tok[1]
            return _later(banks[0].free, banks[1].free)

        def z_get(buf):
            return lambda kc: buf[:, kc, :]

        t_x2w = None
        t_z2w = None
        t_scan = None
        t_tails = None
        t_hsw = None
        for tt in range(n_tiles):
            first = tt == 0
            if not first:
                t_x = dma("S", X[:],
                          x_d.rearrange("(c p) n -> p c n", p=128)[:, :, tt * TILE:(tt + 1) * TILE],
                          waits=(t_x2w,) if t_x2w else ())
                t_z = full_ln(lambda c: X[:, c, :], lambda c: z[:, c, :], TILE, 1e-5,
                              extra=(t_x, t_scan, t_z2w), mu_out=smu0[:, tt, :],
                              inv_out=sinv0[:, tt, :], stats=False)
            else:
                t_z = t_z0

            def ev_qk(bk, og, tok):
                dstb = qT if og < 8 else kT
                c = og % 8

                def f(v):
                    return v.tensor_scalar(dstb[:, c, :], bk.ps[:, 0:TILE],
                                           bqkv[:, og:og + 1], None, ALU.add)
                return ve(f, waits=(tok,))
            proj(qkvB, D, 2048, z_get(z), ev_qk, t_src=(t_qkvB, t_z, t_comb))

            t_vw0 = dma("S", wslv(0, 8, 512)[:, :, :],
                        qkvB.rearrange("(c p) n -> p c n", p=128)[:, :, 2048:2560],
                        waits=tuple(w for w in (wsl_busy[0],) if w))
            t_vw1 = dma("S", wslv(1, 8, 512)[:, :, :],
                        qkvB.rearrange("(c p) n -> p c n", p=128)[:, :, 2560:3072],
                        waits=tuple(w for w in (wsl_busy[1],) if w))
            for tb in range(NM):
                for vf in range(2):
                    bk = banks[(tb * 2 + vf) % 2]
                    ops = [{"out": bk.ps[:, 0:512],
                            "lhsT": z[:, kc, tb * 128:(tb + 1) * 128],
                            "rhs": wslv(vf, 8, 512)[:, kc, :],
                            "start": kc == 0, "stop": kc == NCH - 1}
                           for kc in range(NCH)]
                    t_mm = pe(ops, waits=((t_vw0, t_vw1)[vf], bk.free)
                              + ((t_z,) if tb == 0 else ()))
                    wsl_busy[vf] = t_mm

                    def f(v, bk=bk, tb=tb, vf=vf):
                        return v.tensor_add(vtok[:, 2 * tb + vf, :],
                                            bk.ps[:, 0:512], vb_bc[:, vf * 512:(vf + 1) * 512])
                    bk.free = ve(f, waits=(t_mm, t_vb))
            if first:
                t_hk0 = dma("S", wslv(0, 8, 512)[:, :, :],
                            qkvB.rearrange("(c p) n -> p c n", p=128)[:, :, 1024:1536],
                            waits=tuple(w for w in (wsl_busy[0],) if w))
                t_hk1 = dma("S", wslv(1, 8, 512)[:, :, :],
                            qkvB.rearrange("(c p) n -> p c n", p=128)[:, :, 1536:2048],
                            waits=tuple(w for w in (wsl_busy[1],) if w))
                for c in range(NCH):
                    bk = banks[2 + c % 2]
                    vvi = 0 if c < 4 else 1
                    ops = [{"out": bk.ps[:, 0:128],
                            "lhsT": wslv(vvi, 8, 512)[:, kc, (c % 4) * 128:(c % 4 + 1) * 128],
                            "rhs": hz[:, kc, :], "start": kc == 0, "stop": kc == NCH - 1}
                           for kc in range(NCH)]
                    t_mm = pe(ops, waits=((t_hk0, t_hk1)[vvi], t_hz, bk.free))
                    wsl_busy[vvi] = t_mm

                    def f(v, bk=bk, c=c):
                        return v.tensor_scalar(ktail[:, c, :], bk.ps[:, 0:128],
                                               bqkv[:, 8 + c:9 + c], None, ALU.add)
                    bk.free = ve(f, waits=(t_mm,))
                t_hv0 = dma("S", wslv(0, 8, 512)[:, :, :],
                            qkvB.rearrange("(c p) n -> p c n", p=128)[:, :, 2048:2560],
                            waits=tuple(w for w in (wsl_busy[0],) if w))
                t_hv1 = dma("S", wslv(1, 8, 512)[:, :, :],
                            qkvB.rearrange("(c p) n -> p c n", p=128)[:, :, 2560:3072],
                            waits=tuple(w for w in (wsl_busy[1],) if w))
                for vf in range(2):
                    bk = banks[4 + vf]
                    ops = [{"out": bk.ps[:, 0:512], "lhsT": hz[:, kc, :],
                            "rhs": wslv(vf, 8, 512)[:, kc, :],
                            "start": kc == 0, "stop": kc == NCH - 1}
                           for kc in range(NCH)]
                    t_mm = pe(ops, waits=((t_hv0, t_hv1)[vf], t_hz, bk.free))
                    wsl_busy[vf] = t_mm

                    def f(v, bk=bk, vf=vf):
                        return v.tensor_add(vtail[:, vf * 512:(vf + 1) * 512],
                                            bk.ps[:, 0:512], vb_bc[:, vf * 512:(vf + 1) * 512])
                    bk.free = ve(f, waits=(t_mm, t_vb))
                t_tails = _later(banks[2].free, banks[3].free, banks[4].free, banks[5].free)
            t_qkv_done = _later(*[b.free for b in banks])
            tap("qT", qT[:], waits=(t_qkv_done,))
            tap("kT", kT[:], waits=(t_qkv_done,))
            tap("vtok", vtok[:], waits=(t_qkv_done,))
            tap("ktail", ktail[:], waits=(t_tails,))
            tap("vtail", vtail[:], waits=(t_tails,))
            if upto == "qkv":
                return finish()

            sgrp_open = False
            t_lsum = None
            t_av_last = None
            for cpair in range(8):
                for s in range(-1, NM):
                    q0 = max(0, s * 128)
                    q1 = min(TILE, s * 128 + 256)
                    wq = q1 - q0
                    joff = q0 - s * 128
                    t_e = [None, None]
                    for hh in range(2):
                        hb = 64 * hh
                        bk = banks[2 + hh]
                        ksrc = (ktail[hb:hb + 64, cpair, 0:128] if s < 0
                                else kT[hb:hb + 64, cpair, s * 128:(s + 1) * 128])
                        wl = [bk.free]
                        if cpair == 0 and s == -1:
                            wl += [t_qkv_done, t_tails]
                        t_sc = pe([{"out": bk.ps[0:128, 0:wq], "lhsT": ksrc,
                                    "rhs": qT[hb:hb + 64, cpair, q0:q1]}],
                                  waits=tuple(w for w in wl if w))
                        t_e[hh] = ae(lambda a, bk=bk, hh=hh, wq=wq: a.activation(
                            out=etile[:, hh, 0:wq], in_=bk.ps[0:128, 0:wq],
                            func=AF.Exp, scale=0.125),
                            waits=tuple(w for w in (t_sc, t_av_last) if w))
                        bk.free = t_e[hh]
                    mref = halom[:, 0:128] if (first and s < 0) else mask01[:, joff:joff + wq]

                    def fm(g, mref=mref, wq=wq):
                        g.tensor_mul(etile[:, 0, 0:wq], etile[:, 0, 0:wq], mref)
                        return g.tensor_mul(etile[:, 1, 0:wq], etile[:, 1, 0:wq], mref)
                    t_m = ge(fm, waits=(t_e[0], t_e[1]))
                    for mpart in (s, s + 1):
                        if mpart < 0 or mpart >= NM:
                            continue
                        mo = mpart * 128
                        bkL = banks[4]
                        bkS = banks[6]
                        eq0 = mpart * 128 - q0
                        opening = mpart == s + 1
                        closing = mpart == s
                        ops = []
                        wl = [t_m]
                        for hh in range(2):
                            h = 2 * cpair + hh
                            vs = (vtail[:, h * 64:(h + 1) * 64] if s < 0
                                  else vtok[:, 2 * s + (h * 64) // 512,
                                            (h * 64) % 512:(h * 64) % 512 + 64])
                            if opening and hh == 0 and mpart == 0:
                                wl.append(bkL.free)
                            ops.append({"out": bkL.ps[64 * hh:64 * (hh + 1), mo:mo + 128],
                                        "lhsT": vs, "rhs": etile[:, hh, eq0:eq0 + 128],
                                        "start": bool(opening),
                                        "stop": bool(closing),
                                        "tp": (0, 64) if hh == 1 else None})
                            s_open = bool(opening and hh == 0 and not sgrp_open)
                            s_close = bool(closing and hh == 1 and cpair == 7
                                           and mpart == NM - 1)
                            if s_open:
                                wl.append(bkS.free)
                                sgrp_open = True
                            ops.append({"out": bkS.ps[0:16, mo:mo + 128],
                                        "lhsT": oh16[:, h, :],
                                        "rhs": etile[:, hh, eq0:eq0 + 128],
                                        "start": s_open, "stop": s_close})
                        t_av = pe(ops, waits=tuple(w for w in wl if w))
                        t_av_last = t_av
                        if closing and mpart == NM - 1:
                            def fev(v, cpair=cpair):
                                return v.tensor_copy(bag1[:, cpair, :], banks[4].ps[:, 0:512])
                            banks[4].free = ve(fev, waits=(t_av,))
                        if closing and cpair == 7 and mpart == NM - 1:
                            def fsv(v):
                                return v.tensor_copy(lsum[:, :], banks[6].ps[0:16, 0:512])
                            t_lsum = ve(fsv, waits=(t_av,))
                            banks[6].free = t_lsum
            t_localu = banks[4].free
            tap("localu", bag1[:], waits=(t_localu,))
            tap("lsum", lsum[:], waits=(t_lsum,))

            def f_lr(v):
                v.tensor_scalar_max(lsum[:, :], lsum[:, :], 1e-30)
                return v.reciprocal(lsum[:, :], lsum[:, :])
            t_lrec = ve(f_lr, waits=(t_lsum,))
            if tt < n_tiles - 1:
                def f_tails(v):
                    for c in range(NCH):
                        v.tensor_copy(ktail[:, c, :], kT[:, c, TILE - 128:TILE])
                    v.tensor_copy(vtail[:, 0:512], vtok[:, 2 * (NM - 1), :])
                    return v.tensor_copy(vtail[:, 512:1024], vtok[:, 2 * (NM - 1) + 1, :])
                t_tails = ve(f_tails, waits=(t_qkv_done, t_localu))
            if upto == "local":
                return finish()

            def ev_rq(bk, og, tok):
                def f(v):
                    return v.tensor_scalar(rqTb[:, og, :], bk.ps[:, 0:TILE],
                                           brq[:, og:og + 1], None, ALU.add)
                return ve(f, waits=(tok,))
            proj(rqB, D, 256, z_get(z), ev_rq, t_src=(t_rqB,), slab_cols=256)
            t_rq = _later(banks[0].free, banks[1].free)
            tap("rqT", rqTb[:], waits=(t_rq,))

            lat_last = None
            lsum_lat = None
            for cpair in range(2):
                bk = banks[2 + cpair]
                ops = []
                for hh in range(2):
                    hb = 64 * hh
                    ops.append({"out": bk.ps[hb:hb + 64, 0:512],
                                "lhsT": rkallT[hb:hb + 64, cpair, :],
                                "rhs": rqTb[hb:hb + 64, cpair, :],
                                "tp": (64, 64) if hh == 1 else None})
                wl = [bk.free] + ([t_rq, t_rkrv] if cpair == 0 else [])
                t_ls = pe(ops, waits=tuple(w for w in wl if w))
                t_le = ae(lambda a, bk=bk, cpair=cpair: a.activation(
                    out=lexp[:, cpair, :], in_=bk.ps[:, 0:512],
                    func=AF.Exp, scale=0.125), waits=(t_ls,))
                bk.free = t_le

                def fm(g, cpair=cpair, tt=tt):
                    return g.tensor_mul(lexp[:, cpair, :], lexp[:, cpair, :],
                                        latm[:, tt * TILE:(tt + 1) * TILE])
                t_lm = ge(fm, waits=(t_le,))
                bkL = banks[4] if cpair == 0 else banks[5]
                ops = []
                for hh in range(2):
                    hb = 64 * hh
                    lh = 2 * cpair + hh
                    ops.append({"out": bkL.ps[hb:hb + 64, 0:512],
                                "lhsT": rvlm[hb:hb + 64, lh * 64:(lh + 1) * 64],
                                "rhs": lexp[hb:hb + 64, cpair, :],
                                "tp": (64, 64) if hh == 1 else None})
                ops.append({"out": banks[6].ps[0:4, 0:512],
                            "lhsT": ohlat[:, cpair, :],
                            "rhs": lexp[:, cpair, :],
                            "start": cpair == 0, "stop": cpair == 1})
                wl = [t_lm, bkL.free] + ([banks[6].free] if cpair == 0 else [])
                t_lv = pe(ops, waits=tuple(w for w in wl if w))

                def flv(v, bkL=bkL, cpair=cpair):
                    return v.tensor_copy(latu[:, cpair, :], bkL.ps[:, 0:512])
                bkL.free = ve(flv, waits=(t_lv,))
                lat_last = _later(lat_last, bkL.free)
                if cpair == 1:
                    def fls(v):
                        return v.tensor_copy(latsum[:, :], banks[6].ps[0:4, 0:512])
                    lsum_lat = ve(fls, waits=(t_lv,))
                    banks[6].free = lsum_lat
            tap("latu", latu[:], waits=(lat_last,))
            tap("latsum", latsum[:], waits=(lsum_lat,))

            def f_latrec(v):
                v.tensor_scalar_max(latsum[:, :], latsum[:, :], 1e-30)
                v.reciprocal(latsum[:, :], latsum[:, :])
                return v.tensor_copy(latsb[:, :], latsum[:, :])
            t_latrec = ve(f_latrec, waits=(lsum_lat,))

            for cpair in range(2):
                bk = banks[2 + cpair]
                t_bc = pe([{"out": bk.ps[:, 0:512], "lhsT": bclat[:, cpair, :],
                            "rhs": latsb[:, :]}],
                          waits=(t_latrec, bk.free))

                def f(v, bk=bk, cpair=cpair):
                    return v.tensor_mul(latnb[:, cpair, :], latu[:, cpair, :],
                                        bk.ps[:, 0:512])
                bk.free = ve(f, waits=(t_bc, lat_last))
            t_latn = _later(banks[2].free, banks[3].free)
            tap("latn", latnb[:], waits=(t_latn,))

            t_rol = dma("S", wslv(0, 2, 1024)[:, :, :],
                        routB.rearrange("(c p) n -> p c n", p=128),
                        waits=tuple(w for w in (t_routB, wsl_busy[0]) if w))
            for og in range(8):
                bk = banks[og % 2]
                ops = [{"out": bk.ps[:, 0:TILE],
                        "lhsT": wslv(0, 2, 1024)[:, kc, og * 128:(og + 1) * 128],
                        "rhs": latnb[:, kc, :],
                        "start": kc == 0, "stop": kc == 1}
                       for kc in range(2)]
                wl = [t_rol, bk.free] + ([t_latn] if og == 0 else [])
                t_mm = pe(ops, waits=tuple(w for w in wl if w))
                wsl_busy[0] = t_mm

                def f(v, bk=bk, og=og):
                    return v.tensor_scalar(bag2[:, og, :], bk.ps[:, 0:TILE],
                                           brout[:, og:og + 1], None, ALU.add)
                bk.free = ve(f, waits=(t_mm,))
            t_remote = _later(banks[0].free, banks[1].free)
            tap("remote", bag2[:], waits=(t_remote,))

            bkl, bkr = banks[2], banks[3]
            opsl = [{"out": bkl.ps[0:16, 0:512], "lhsT": glgr[:, 0, c, :],
                     "rhs": bag1[:, c, :],
                     "start": c == 0, "stop": c == NCH - 1} for c in range(NCH)]
            opsr = [{"out": bkr.ps[0:16, 0:512], "lhsT": glgr[:, 1, c, :],
                     "rhs": bag2[:, c, :],
                     "start": c == 0, "stop": c == NCH - 1} for c in range(NCH)]
            t_gl = pe(opsl, waits=(bkl.free, t_localu))
            t_gr = pe(opsr, waits=(bkr.free, t_remote))

            def f_g(v):
                v.tensor_mul(gtmp[:, :], bkl.ps[0:16, 0:512], lsum[:, :])
                return v.tensor_add(gtmp[:, :], gtmp[:, :], bkr.ps[0:16, 0:512])
            t_gd = ve(f_g, waits=(t_gl, t_gr, t_lrec))
            t_sig = ae(lambda a: a.activation(out=grow[0:16, :], in_=gtmp[:, :],
                                              func=AF.Sigmoid), waits=(t_gd,))

            def f_g2(v):
                v.tensor_mul(ghb[:, :], grow[0:16, :], lsum[:, :])
                return v.tensor_scalar(gm1b[:, :], grow[0:16, :],
                                       -1.0, 1.0, ALU.mult, ALU.add)
            t_grow = ve(f_g2, waits=(t_sig,))
            bkl.free = t_grow
            bkr.free = t_grow
            tap("grow", grow[:], waits=(t_grow,))

            t_mix = None
            for cpair in range(8):
                bka = banks[2 + cpair % 2]
                bkb = banks[4 + cpair % 2]
                wl = [bka.free] + ([t_grow] if cpair == 0 else [])
                t_b1 = pe([{"out": bka.ps[:, 0:512], "lhsT": bc2[0:16, cpair, :],
                            "rhs": ghb[:, :]}],
                          waits=tuple(w for w in wl if w))
                t_b2 = pe([{"out": bkb.ps[:, 0:512], "lhsT": bc2[0:16, cpair, :],
                            "rhs": gm1b[:, :]}],
                          waits=(bkb.free,) if bkb.free else ())

                def f(v, bka=bka, bkb=bkb, cpair=cpair):
                    v.tensor_mul(z[:, cpair, :], bag1[:, cpair, :], bka.ps[:, 0:512])
                    v.tensor_mul(mtmp[:, :], bag2[:, cpair, :], bkb.ps[:, 0:512])
                    return v.tensor_add(z[:, cpair, :], z[:, cpair, :], mtmp[:, :])
                tk = ve(f, waits=(t_b1, t_b2))
                bka.free = tk
                bkb.free = tk
                t_mix = tk
            tap("mixed", z[:], waits=(t_mix,))
            if upto == "mix":
                return finish()

            def ev_out(bk, og, tok):
                def f(v):
                    return v.scalar_tensor_tensor(X[:, og, :], bk.ps[:, 0:TILE],
                                                  bout[:, og:og + 1], X[:, og, :],
                                                  ALU.add, ALU.add)
                return ve(f, waits=(tok,))
            proj(outwB, D, D, z_get(z), ev_out, t_src=(t_outwB, t_mix))
            t_x1 = _later(banks[0].free, banks[1].free)
            tap("X1", X[:], waits=(t_x1,))

            t_z1 = full_ln(lambda c: X[:, c, :], lambda c: z[:, c, :], TILE, 1e-5,
                           extra=(t_x1,))
            tap("z1", z[:], waits=(t_z1,))

            def ev_w1(bk, og, tok):
                def f(a):
                    return a.activation(out=big1[:, og, :], in_=bk.ps[:, 0:TILE],
                                        func=AF.Gelu, bias=b1v[:, og:og + 1],
                                        scale=alpha1[:, og:og + 1])
                return ae(f, waits=(tok,))
            proj(w1sB, D, 4096, z_get(z), ev_w1, transposed=True,
                 t_src=(t_z1, t_mix, t_grow, t_localu, t_qkv_done, t_tails))
            t_h0 = _later(banks[0].free, banks[1].free)
            tap("hffn", big1[:], waits=(t_h0,))

            def ev_w2(bk, og, tok):
                def f(v):
                    v.tensor_scalar(sqs[0][:, :], bk.ps[:, 0:TILE],
                                    alpha2[:, og:og + 1], b2v[:, og:og + 1],
                                    ALU.mult, ALU.add)
                    return v.tensor_add(X[:, og, :], X[:, og, :], sqs[0][:, :])
                return ve(f, waits=(tok,))
            proj(w2sB, 4096, D, z_get(big1), ev_w2, transposed=True,
                 t_src=(t_h0,), slab_cols=128)
            t_x2 = _later(banks[0].free, banks[1].free)
            tap("X2", X[:], waits=(t_x2,))

            t_z2 = full_ln(lambda c: X[:, c, :], lambda c: z[:, c, :], TILE, 1e-5,
                           extra=(t_x2,))
            tap("z2", z[:], waits=(t_z2,))
            t_x2w = dma("A", x2d.rearrange("(c p) n -> p c n", p=128)[:, :, tt * TILE:(tt + 1) * TILE],
                        X[:], waits=(t_x2,))
            t_z2w = dma("A", z2d.rearrange("(c p) n -> p c n", p=128)[:, :, tt * TILE:(tt + 1) * TILE],
                        z[:], waits=(t_z2,))

            bkl, bkb = banks[0], banks[1]
            opsl = [{"out": bkl.ps[0:16, 0:512], "lhsT": lamTw[:, kc, :],
                     "rhs": z[:, kc, :],
                     "start": kc == 0, "stop": kc == NCH - 1} for kc in range(NCH)]
            opsb = [{"out": bkb.ps[0:16, 0:512], "lhsT": wbTw[:, kc, :],
                     "rhs": z[:, kc, :],
                     "start": kc == 0, "stop": kc == NCH - 1} for kc in range(NCH)]
            t_l = pe(opsl, waits=(bkl.free, t_z2))
            t_b = pe(opsb, waits=(bkb.free,))
            t_sg = ae(lambda a: a.activation(out=lamrow[:, :], in_=bkl.ps[0:16, 0:512],
                                             func=AF.Sigmoid, bias=blam[:, :]),
                      waits=(t_l,))
            t_cb = ve(lambda v: v.tensor_copy(bxrow[:, :], bkb.ps[0:16, 0:512]),
                      waits=(t_b,))
            bkl.free = t_sg
            bkb.free = t_cb
            tap("lam", lamrow[:], waits=(t_sg,))
            tap("bx", bxrow[:], waits=(t_cb,))

            def f_scan(v):
                v.tensor_tensor_scan(hslb[:, :], lamrow[:, :], bxrow[:, :],
                                     scarry[:, :], ALU.mult, ALU.add)
                v.tensor_tensor_scan(acumb[:, :], lamrow[:, :], zer16[:, :],
                                     ccarry[:, :], ALU.mult, ALU.add)
                v.tensor_copy(gtmp[:, :], hslb[:, :])
                v.tensor_copy(scarry[:, :], FA2[0:16, TILE - 1:TILE])
                return v.tensor_copy(ccarry[:, :], FA2[0:16, 2 * TILE - 1:2 * TILE])
            t_scan = ve(f_scan, waits=(t_sg, t_cb, t_misc, t_hsw))
            banks[0].free = t_scan
            banks[1].free = t_scan
            t_hsw = dmas("A", [(hsd[:, tt * TILE:(tt + 1) * TILE], hslb[:, :]),
                               (acd[:, tt * TILE:(tt + 1) * TILE], acumb[:, :])],
                         waits=(t_scan,))
            tap("hsl", hslb[:, :], waits=(t_scan,))
            tap("acum", acumb[:, :], waits=(t_scan,))

        if upto == "phaseA":
            return finish()

        t_hw = dma("G", agh_in, scarry[:, :], waits=(t_scan,))
        box2 = {}

        def f_cc2(g):
            _wait(g, t_hw)
            i = g.collective_compute("AllGather", ALU.bypass,
                                     replica_groups=[[0, 1, 2, 3, 4, 5, 6, 7]],
                                     ins=[agh_in], outs=[agh_out])
            box2["t"] = sGP.inc(i)
        block.gpsimd(f_cc2)
        t_agh = dma("S", agh_s[:], agh_out.rearrange("(r k) one -> r (k one)", k=16),
                    waits=(box2["t"],))
        t_selmm = pe([{"out": banks[0].ps[0:16, 0:1], "lhsT": agh_s[:, :],
                       "rhs": selHs[:, :]}],
                     waits=tuple(w for w in (t_agh, banks[0].free) if w))
        t_init = ve(lambda v: v.tensor_copy(initv[:, :], banks[0].ps[0:16, 0:1]),
                    waits=(t_selmm,))
        banks[0].free = t_init
        tap("hft", scarry[:], waits=(t_hw,))
        tap("initv", initv[:], waits=(t_init,))
        if upto == "phaseB":
            return finish()

        t_h1 = None
        t_hs = None
        t_odma = None
        for tt in range(n_tiles):
            t_x2r = dma("S", X[:],
                        x2d.rearrange("(c p) n -> p c n", p=128)[:, :, tt * TILE:(tt + 1) * TILE],
                        waits=tuple(w for w in (t_x2w, t_odma) if w))
            t_z2r = dma("S", z[:],
                        z2d.rearrange("(c p) n -> p c n", p=128)[:, :, tt * TILE:(tt + 1) * TILE],
                        waits=tuple(w for w in (t_z2w, t_h1) if w))
            t_hsr = dmas("S", [(hslb[:, :], hsd[:, tt * TILE:(tt + 1) * TILE]),
                               (acumb[:, :], acd[:, tt * TILE:(tt + 1) * TILE])],
                         waits=tuple(w for w in (t_hsw, t_hs) if w))
            if tt == 0:
                tap("cx2", X[:], waits=(t_x2r,))
                tap("cz2", z[:], waits=(t_z2r,))
                tap("chsl", hslb[:, :], waits=(t_hsr,))

            if SERIAL[0]:
                joinpoint()

            def ev_wg(bk, og, tok):
                def f(a):
                    return a.activation(out=bag1[:, og, :], in_=bk.ps[:, 0:TILE],
                                        func=AF.Sigmoid)
                return ae(f, waits=(tok,))
            proj(WgB, D, D, z_get(z), ev_wg, transposed=True, t_src=(t_WgB, t_z2r))
            t_gate = _later(banks[0].free, banks[1].free)
            tap("gate1", bag1[:], waits=(t_gate,))
            if SERIAL[0]:
                joinpoint()

            def f_hs(v):
                v.scalar_tensor_tensor(gtmp[:, :], acumb[:, :], initv[:, 0:1],
                                       hslb[:, :], ALU.mult, ALU.add)
                return v.tensor_copy(hsbf[:, :], gtmp[:, :])
            t_hs = ve(f_hs, waits=(t_init, t_hsr))

            t_y = None
            for oc in range(NCH):
                bk = banks[2 + oc % 2]
                wl = [bk.free] + ([t_hs] if oc == 0 else [])
                t_mm = pe([{"out": bk.ps[:, 0:512], "lhsT": wcT[:, oc, :],
                            "rhs": hsbf[:, :]}],
                          waits=tuple(w for w in wl if w))

                def f(v, bk=bk, oc=oc):
                    return v.tensor_mul(qT[:, oc, :], bk.ps[:, 0:512], bag1[:, oc, :])
                bk.free = ve(f, waits=(t_mm, t_gate))
                t_y = bk.free
            tap("yv", qT[:], waits=(t_y,))
            if SERIAL[0]:
                joinpoint()

            b7 = banks[7]
            prev = [None, None]
            tsq_mm = None
            for c in range(NCH):
                pb = c % 2
                wl = [prev[pb]] + ([t_y] if c < 2 else [])
                t_sq = ae(lambda a, c=c, pb=pb: a.activation(
                    out=sqs[pb][:, :], in_=qT[:, c, :], func=AF.Square),
                    waits=tuple(w for w in wl if w))
                tsq_mm = pe([{"out": b7.ps[0:1, 0:512], "lhsT": onesf[:, :],
                              "rhs": sqs[pb][:, :], "start": c == 0, "stop": c == NCH - 1}],
                            waits=(t_sq, b7.free) if c == 0 else (t_sq,))
                prev[pb] = tsq_mm
            t_ms = ve(lambda v: v.tensor_scalar(srow[:, 0, 0:TILE], b7.ps[0:1, 0:TILE],
                                                1.0 / D, 1e-6, ALU.mult, ALU.add),
                      waits=(tsq_mm,))
            t_msq = ae(lambda a: a.activation(out=srow[:, 0, 0:TILE], in_=srow[:, 0, 0:TILE],
                                              func=AF.Sqrt), waits=(t_ms,))

            def f_rrec(v):
                v.reciprocal(srow[:, 0, 0:TILE], srow[:, 0, 0:TILE])
                return v.tensor_copy(srowb_rms[:, 0:TILE], srow[:, 0, 0:TILE])
            t_rrec = ve(f_rrec, waits=(t_msq,))
            b7.free = t_rrec
            tap("rmsrow", srow[:, 0, :], waits=(t_rrec,))

            bk5 = banks[5]
            t_bc = pe([{"out": bk5.ps[:, 0:512], "lhsT": ones1[:, :],
                        "rhs": srowb_rms[:, 0:TILE]}],
                      waits=tuple(w for w in (bk5.free, t_rrec) if w))
            t_x3 = None
            for c in range(NCH):
                def f(v, c=c):
                    v.scalar_tensor_tensor(sqs[0][:, :], qT[:, c, :],
                                           rmsw[:, c:c + 1], bk5.ps[:, 0:512],
                                           ALU.mult, ALU.mult)
                    return v.tensor_add(X[:, c, :], X[:, c, :], sqs[0][:, :])
                wl = [t_bc] + ([t_x2r] if c == 0 else [])
                t_x3 = ve(f, waits=tuple(w for w in wl if w))
            bk5.free = t_x3
            tap("X3", X[:], waits=(t_x3,))
            if SERIAL[0]:
                joinpoint()

            t_z3 = full_ln(lambda c: X[:, c, :], lambda c: z[:, c, :], TILE, 1e-5,
                           extra=(t_x3,))
            tap("z3", z[:], waits=(t_z3,))
            if SERIAL[0]:
                joinpoint()

            def ev_g(bk, og, tok):
                def f(a):
                    return a.activation(out=big1[:, og, :], in_=bk.ps[:, 0:TILE],
                                        func=AF.Silu, scale=1.0 / 127.0)
                return ae(f, waits=(tok,))
            proj(wgB, D, 3072, z_get(z), ev_g, transposed=True,
                 t_src=(t_z3, t_y, t_gate))
            t_sg1 = _later(banks[0].free, banks[1].free)
            if SERIAL[0]:
                joinpoint()

            def ev_u(bk, og, tok):
                def f(v):
                    return v.scalar_tensor_tensor(big1[:, og, :], bk.ps[:, 0:TILE],
                                                  1.0 / 127.0, big1[:, og, :],
                                                  ALU.mult, ALU.mult)
                return ve(f, waits=(tok, t_sg1))
            proj(wuB, D, 3072, z_get(z), ev_u, transposed=True)
            t_h1 = _later(banks[0].free, banks[1].free)
            if SERIAL[0]:
                joinpoint()

            def ev_d(bk, og, tok):
                def f(v):
                    return v.scalar_tensor_tensor(X[:, og, :], bk.ps[:, 0:TILE],
                                                  1.0 / 127.0, X[:, og, :],
                                                  ALU.mult, ALU.add)
                return ve(f, waits=(tok,))
            proj(wdB, 3072, D, z_get(big1), ev_d, transposed=True,
                 t_src=(t_h1,), slab_cols=256)
            t_out = _later(banks[0].free, banks[1].free)
            t_odma = dma("S", out_d.rearrange("(c p) n -> p c n", p=128)[:, :, tt * TILE:(tt + 1) * TILE],
                         X[:], waits=(t_out,))
            if SERIAL[0]:
                joinpoint()

        return finish()


def _mk_consts(core):
    half = core % 2
    consts = {}
    consts["c_ident"] = np.eye(128, dtype=BF)
    j = np.arange(256)[None, :]
    kl = np.arange(128)[:, None]
    consts["c_mask"] = ((1 <= j - kl) & (j - kl <= 128)).astype(BF)
    consts["c_halom"] = (np.ascontiguousarray(consts["c_mask"][:, 128:256]) if half == 1
                         else np.zeros((128, 128), BF))
    cb = half * 32
    lat = np.arange(64)[:, None]
    chunk = cb + np.arange(M)[None, :] // 128
    lm = (lat < chunk)
    consts["c_latm"] = np.concatenate([lm, lm], axis=0).astype(BF)
    oh = np.zeros((128, 16, 16), BF)
    for h in range(16):
        oh[:, h, h] = 1
    consts["c_oh16"] = oh
    ohl = np.zeros((128, 2, 4), BF)
    for c in range(2):
        ohl[0:64, c, 2 * c] = 1
        ohl[64:128, c, 2 * c + 1] = 1
    consts["c_ohlat"] = ohl
    bc2 = np.zeros((64, 8, 128), BF)
    for c in range(8):
        bc2[2 * c, c, 0:64] = 1
        bc2[2 * c + 1, c, 64:128] = 1
        bc2[32 + 2 * c, c, 0:64] = 1
        bc2[32 + 2 * c + 1, c, 64:128] = 1
    consts["c_bc2"] = bc2
    bcl = np.zeros((4, 2, 128), BF)
    for c in range(2):
        bcl[2 * c, c, 0:64] = 1
        bcl[2 * c + 1, c, 64:128] = 1
    consts["c_bclat"] = bcl
    consts["c_ones1"] = np.ones((1, 128), BF)
    consts["c_onesf"] = np.ones((128, 1), np.float32)
    selH = np.zeros((8, 1), np.float32)
    if half == 1:
        selH[core - 1, 0] = 1.0
    consts["c_selH"] = selH
    selA = np.zeros((128, 8), np.float32)
    selA[:, core - 1 if half == 1 else core] = 1.0
    consts["c_selA"] = selA
    selC = np.zeros((128, 8), np.float32)
    if half == 1:
        selC[:, core] = 1.0
    consts["c_selC"] = selC
    padmv = np.zeros((128, 128), BF)
    for reg in range(4):
        if (reg % 2) == half:
            padmv[:, reg * 32:(reg + 1) * 32] = np.float32(1.0 / 128.0)
    consts["c_padm"] = padmv
    return consts


def make_in_maps(inputs):
    x = np.asarray(inputs["L0_attn_x"])
    qkv_b = np.asarray(inputs["L0_attn_qkv_b"])
    in_maps = []
    for core in range(8):
        b, half = core // 2, core % 2
        g0 = half * M
        im = {}
        im["x"] = np.ascontiguousarray(x[b, g0:g0 + M, :].T)
        im["xh"] = (np.ascontiguousarray(x[b, g0 - 128:g0, :].T) if half == 1
                    else np.zeros((D, 128), np.float32))
        im["qkv_w"] = np.asarray(inputs["L0_attn_qkv_w"])
        im["rq_w"] = np.asarray(inputs["L0_attn_rq_w"])
        im["rkv_w"] = np.asarray(inputs["L0_attn_rkv_w"])
        im["rout_w"] = np.asarray(inputs["L0_attn_rout_w"])
        im["out_w"] = np.asarray(inputs["L0_attn_out_w"])
        im["gl_w"] = np.asarray(inputs["L0_attn_gl_w"])
        im["gr_w"] = np.asarray(inputs["L0_attn_gr_w"])
        im["w1"] = np.asarray(inputs["L0_ffn_w1"])
        im["w2"] = np.asarray(inputs["L0_ffn_w2"])
        im["W_lambda"] = np.asarray(inputs["L1_attn_W_lambda"])
        im["W_B"] = np.asarray(inputs["L1_attn_W_B"])
        im["W_C"] = np.asarray(inputs["L1_attn_W_C"])
        im["W_gate"] = np.asarray(inputs["L1_attn_W_gate"])
        im["wg"] = np.asarray(inputs["L1_ffn_w_gate"])
        im["wu"] = np.asarray(inputs["L1_ffn_w_up"])
        im["wd"] = np.asarray(inputs["L1_ffn_w_down"])
        im["b_qkv"] = np.ascontiguousarray(qkv_b.reshape(24, 128).T)
        im["b_rq"] = np.ascontiguousarray(np.asarray(inputs["L0_attn_rq_b"]).reshape(2, 128).T)
        im["b_rkv"] = np.ascontiguousarray(np.asarray(inputs["L0_attn_rkv_b"]).reshape(4, 128).T)
        im["b_rout"] = np.ascontiguousarray(np.asarray(inputs["L0_attn_rout_b"]).reshape(8, 128).T)
        im["b_out"] = np.ascontiguousarray(np.asarray(inputs["L0_attn_out_b"]).reshape(8, 128).T)
        im["b_1"] = np.ascontiguousarray(np.asarray(inputs["L0_ffn_b1"]).reshape(32, 128).T)
        im["b_2"] = np.ascontiguousarray(np.asarray(inputs["L0_ffn_b2"]).reshape(8, 128).T)
        im["b_lam"] = np.ascontiguousarray(np.asarray(inputs["L1_attn_b_lambda"]).reshape(16, 1))
        im["rmsw"] = np.ascontiguousarray(np.asarray(inputs["L1_attn_rms_weight"]).reshape(8, 128).T)
        im["vb_row"] = np.ascontiguousarray(qkv_b[2 * D:].reshape(1, D))
        im["rv_row"] = np.ascontiguousarray(np.asarray(inputs["L0_attn_rkv_b"])[256:].reshape(1, 256))
        im.update(_mk_consts(core))
        in_maps.append(im)
    return in_maps


def kernel(**inputs):
    SERIAL[0] = True
    nc = build_program()
    in_maps = make_in_maps(inputs)
    res = run_bass_kernel_spmd(nc, in_maps, core_ids=list(range(8)))
    out = np.empty((4, 2 * M, D), np.float32)
    for core in range(8):
        b, half = core // 2, core % 2
        out[b, half * M:(half + 1) * M, :] = res.results[core]["out"].T
    return out


# version 03
# speedup vs baseline: 1.0020x; 1.0020x over previous
import contextlib
import numpy as np
import ml_dtypes

import concourse.bass as bass
import concourse.mybir as mybir
from concourse.bass_utils import run_bass_kernel_spmd

F32 = mybir.dt.float32
BF16 = mybir.dt.bfloat16
I32 = mybir.dt.int32
AF = mybir.ActivationFunctionType
ALU = mybir.AluOpType
AX = mybir.AxisListType
BF = ml_dtypes.bfloat16

D = 1024
NCH = 8
TILE = 512
NTILE = 8
M = TILE * NTILE
H = 16
NM = TILE // 128


class Sem:
    def __init__(self, stack, nc, name):
        self.h = stack.enter_context(nc.semaphore(name))
        self.n = 0

    def inc(self, instr, k=1):
        instr.then_inc(self.h, k)
        self.n += k
        return (self, self.n)


def _wait(eng, tok):
    if tok is None:
        return
    sem, val = tok
    if val > 0:
        eng.wait_ge(sem.h, val)


def _later(*toks):
    toks = [t for t in toks if t is not None]
    if not toks:
        return None
    sems = {id(t[0]) for t in toks}
    assert len(sems) == 1, "mixed-semaphore _later; pass tokens separately"
    return max(toks, key=lambda t: t[1])


def _latest(*toks):
    best = {}
    for t in toks:
        if t is None:
            continue
        k = id(t[0])
        if k not in best or t[1] > best[k][1]:
            best[k] = t
    return tuple(best.values())


class Bank:
    __slots__ = ("ps", "free")

    def __init__(self, ps):
        self.ps = ps
        self.free = None


SERIAL = [False]
JOINS = [set([0,1,2,3])]


def build_program(n_tiles=NTILE, taps=(), upto=None):
    nc = bass.Bass("TRN2", target_bir_lowering=False, debug=False)
    tapset = set(taps)

    def din(name, shape, dt=F32):
        return nc.dram_tensor(name, shape, dt, kind="ExternalInput").ap()

    x_d = din("x", [D, M])
    xh_d = din("xh", [D, 128])
    qkvw_d = din("qkv_w", [D, 3 * D])
    rqw_d = din("rq_w", [D, 256])
    rkvw_d = din("rkv_w", [D, 512])
    routw_d = din("rout_w", [256, D])
    outw_d = din("out_w", [D, D])
    glw_d = din("gl_w", [64, 1])
    grw_d = din("gr_w", [64, 1])
    w1_d = din("w1", [4096, D])
    w2_d = din("w2", [D, 4096])
    wlam_d = din("W_lambda", [16, D])
    wbm_d = din("W_B", [16, D])
    wc_d = din("W_C", [D, 16])
    wgate_d = din("W_gate", [D, D])
    wgq_d = din("wg", [3072, D], I32)
    wuq_d = din("wu", [3072, D], I32)
    wdq_d = din("wd", [D, 3072], I32)
    bqkv_d = din("b_qkv", [128, 24])
    brq_d = din("b_rq", [128, 2])
    brkv_d = din("b_rkv", [128, 4])
    brout_d = din("b_rout", [128, 8])
    bout_d = din("b_out", [128, 8])
    b1_d = din("b_1", [128, 32])
    b2_d = din("b_2", [128, 8])
    blam_d = din("b_lam", [16, 1])
    rmsw_d = din("rmsw", [128, 8])
    vbrow_d = din("vb_row", [1, D])
    rvrow_d = din("rv_row", [1, 256])
    ident_d = din("c_ident", [128, 128], BF16)
    mask_d = din("c_mask", [128, 256], BF16)
    halom_d = din("c_halom", [128, 128], BF16)
    latm_d = din("c_latm", [128, M], BF16)
    oh16_d = din("c_oh16", [128, 16, 16], BF16)
    ohlat_d = din("c_ohlat", [128, 2, 4], BF16)
    bc2_d = din("c_bc2", [64, 8, 128], BF16)
    bclat_d = din("c_bclat", [4, 2, 128], BF16)
    ones1_d = din("c_ones1", [1, 128], BF16)
    onesf_d = din("c_onesf", [128, 1])
    selH_d = din("c_selH", [8, 1])
    selA_d = din("c_selA", [128, 8])
    selC_d = din("c_selC", [128, 8])
    padm_d = din("c_padm", [128, 128], BF16)

    out_d = nc.dram_tensor("out", [D, M], F32, kind="ExternalOutput").ap()

    TAPSPECS = {
        "z0": ([128, NCH, TILE], BF16), "qT": ([128, NCH, TILE], BF16),
        "kT": ([128, NCH, TILE], BF16), "vtok": ([128, NCH, TILE], BF16),
        "localu": ([128, NCH, TILE], BF16), "lsum": ([16, TILE], F32),
        "mixed": ([128, NCH, TILE], BF16), "remote": ([128, NCH, TILE], BF16),
        "X1": ([128, NCH, TILE], F32), "X2": ([128, NCH, TILE], F32),
        "hft": ([16, 1], F32), "initv": ([16, 1], F32),
        "rkall": ([128, 2, 64], BF16), "rvlm": ([128, 256], BF16),
        "latn": ([128, 2, TILE], BF16), "grow": ([16, TILE], F32),
        "hsl": ([16, TILE], F32), "acum": ([16, TILE], F32),
        "X3": ([128, NCH, TILE], F32), "z1": ([128, NCH, TILE], BF16),
        "z2": ([128, NCH, TILE], BF16), "z3": ([128, NCH, TILE], BF16),
        "hffn": ([128, 32, TILE], BF16), "pooled": ([128, NCH, 128], BF16),
        "alpha1": ([128, 32], F32), "lam": ([16, TILE], F32),
        "bx": ([16, TILE], F32), "yv": ([128, NCH, TILE], BF16),
        "gate1": ([128, NCH, TILE], BF16), "rmsrow": ([1, TILE], F32),
        "latu": ([128, 2, TILE], BF16), "latsum": ([4, TILE], F32),
        "rqT": ([128, 2, TILE], BF16), "ktail": ([128, NCH, 128], BF16),
        "vtail": ([128, D], BF16), "hz": ([128, NCH, 128], BF16), "cx2": ([128, NCH, TILE], F32), "cz2": ([128, NCH, TILE], BF16), "chsl": ([16, TILE], F32),
    }
    tap_d = {tn: nc.dram_tensor("tap_" + tn, sh, dt, kind="ExternalOutput").ap()
             for tn, (sh, dt) in TAPSPECS.items() if tn in tapset}

    def scr(name, shape, dt=BF16):
        return nc.dram_tensor(name, shape, dt, kind="Internal").ap()

    qkvB = scr("qkvB", [D, 3 * D])
    rqB = scr("rqB", [D, 256])
    rkvB = scr("rkvB", [D, 512])
    routB = scr("routB", [256, D])
    outwB = scr("outwB", [D, D])
    w1sB = scr("w1sB", [4096, D])
    w2sB = scr("w2sB", [D, 4096])
    WgB = scr("WgB", [D, D])
    wgB = scr("wgB", [3072, D])
    wuB = scr("wuB", [3072, D])
    wdB = scr("wdB", [D, 3072])
    x2d = scr("x2d", [D, M], F32)
    z2d = scr("z2d", [D, M], BF16)
    hsd = scr("hsd", [16, M], F32)
    acd = scr("acd", [16, M], F32)
    agkv_in = scr("agkv_in", [3, 128, 128], BF16)
    agkv_out = scr("agkv_out", [8, 3, 128, 128], BF16)
    agh_in = scr("agh_in", [16, 1], F32)
    agh_out = scr("agh_out", [128, 1], F32)

    stack = contextlib.ExitStack()
    with stack:
        def sbuf(name, shape, dt):
            return stack.enter_context(nc.sbuf_tensor(name, shape, dt))

        FA = sbuf("FA", [128, 10240], F32)
        BA = sbuf("BA", [128, 32768], BF16)

        def carve(arena, off, shape):
            n = 1
            for s_ in shape[1:]:
                n *= s_
            ap = arena[0:shape[0], off:off + n]
            if len(shape) == 3:
                ap = ap.rearrange("p (a b) -> p a b", a=shape[1])
            return ap

        X = carve(FA, 0, [128, NCH, TILE])
        sqs = [carve(FA, 4096 + i * TILE, [128, TILE]) for i in range(2)]
        srow = carve(FA, 5120, [1, 3, TILE])
        wtmp = [carve(FA, 0, [128, 4096]),
                carve(FA, 4096, [128, 4096])]
        lsum = carve(FA, 6656, [16, TILE])
        latsum = carve(FA, 7168, [4, TILE])
        grow = carve(FA, 7680, [16, TILE])
        gtmp = carve(FA, 8192, [16, TILE])
        lamrow = carve(FA, 8704, [16, TILE])
        bxrow = carve(FA, 9216, [16, TILE])
        zer16 = carve(FA, 9728, [16, TILE])
        FA2 = sbuf("FA2", [16, 2 * TILE], F32)
        hslb = FA2[0:16, 0:TILE]
        acumb = FA2[0:16, TILE:2 * TILE]

        z = carve(BA, 0, [128, NCH, TILE])
        qT = carve(BA, 4096, [128, NCH, TILE])
        kT = carve(BA, 8192, [128, NCH, TILE])
        vtok = carve(BA, 12288, [128, NCH, TILE])
        bag1 = carve(BA, 16384, [128, NCH, TILE])
        bag2 = carve(BA, 20480, [128, NCH, TILE])
        big1 = carve(BA, 4096, [128, 32, TILE])
        wsgn = [carve(BA, 4096, [128, 4096]),
                carve(BA, 8192, [128, 4096])]
        agkv_s = carve(BA, 12288, [128, 24, 128])
        rqTb = carve(BA, 24576, [128, 2, TILE])
        lexp = carve(BA, 25600, [128, 2, TILE])
        latu = carve(BA, 26624, [128, 2, TILE])
        latnb = carve(BA, 24576, [128, 2, TILE])
        etile = carve(BA, 27648, [128, 2, 256])
        hz = carve(BA, 28160, [128, NCH, 128])
        mtmp = carve(BA, 29184, [128, TILE])
        srowb = carve(BA, 29696, [1, 2, TILE])
        srowb_rms = carve(BA, 30720, [1, TILE])
        latsb = carve(BA, 31232, [4, TILE])
        ghb = carve(BA, 31744, [16, TILE])
        gm1b = carve(BA, 32256, [16, TILE])

        ident = sbuf("ident", [128, 128], BF16)
        mask01 = sbuf("mask01", [128, 256], BF16)
        halom = sbuf("halom", [128, 128], BF16)
        latm = sbuf("latm_s", [128, M], BF16)
        oh16 = sbuf("oh16", [128, 16, 16], BF16)
        ohlat = sbuf("ohlat", [128, 2, 4], BF16)
        bc2 = sbuf("bc2", [64, 8, 128], BF16)
        bclat = sbuf("bclat", [4, 2, 128], BF16)
        ones1 = sbuf("ones1", [1, 128], BF16)
        onesf = sbuf("onesf", [128, 1], F32)
        ones1b = sbuf("ones1b", [128, 1], BF16)
        selHs = sbuf("selHs", [8, 1], F32)
        selAs = sbuf("selAs", [128, 8], F32)
        selCs = sbuf("selCs", [128, 8], F32)
        padm = sbuf("padm", [128, 128], BF16)
        bqkv = sbuf("bqkv", [128, 24], F32)
        brq = sbuf("brq", [128, 2], F32)
        brkv = sbuf("brkv", [128, 4], F32)
        brout = sbuf("brout", [128, 8], F32)
        bout = sbuf("bout", [128, 8], F32)
        b1v = sbuf("b1v", [128, 32], F32)
        b2v = sbuf("b2v", [128, 8], F32)
        blam = sbuf("blam", [16, 1], F32)
        rmsw = sbuf("rmsw_s", [128, 8], F32)
        vb_bc = sbuf("vb_bc", [128, D], BF16)
        alpha1 = sbuf("alpha1", [128, 32], F32)
        alpha2 = sbuf("alpha2", [128, 8], F32)
        lamTw = sbuf("lamTw", [128, NCH, 16], BF16)
        wbTw = sbuf("wbTw", [128, NCH, 16], BF16)
        wcT = sbuf("wcT", [16, NCH, 128], BF16)
        glgr = sbuf("glgr", [128, 2, 8, 16], BF16)
        rkallT = sbuf("rkallT", [128, 2, 64], BF16)
        rvlm = sbuf("rvlm", [128, 256], BF16)
        smu0 = sbuf("smu0", [1, NTILE, TILE], BF16)
        sinv0 = sbuf("sinv0", [1, NTILE, TILE], BF16)
        hmu = sbuf("hmu", [1, 128], BF16)
        hinv = sbuf("hinv", [1, 128], BF16)
        ktail = sbuf("ktail", [128, NCH, 128], BF16)
        vtail = sbuf("vtail", [128, D], BF16)
        scarry = sbuf("scarry", [16, 1], F32)
        ccarry = sbuf("ccarry", [16, 1], F32)
        initv = sbuf("initv", [16, 1], F32)
        pooledT = sbuf("pooledT", [128, NCH, 128], BF16)
        glw_s = sbuf("glw_s", [64, 1], F32)
        grw_s = sbuf("grw_s", [64, 1], F32)
        rk_con = sbuf("rk_con", [128, 2, 64], BF16)
        rv_con = sbuf("rv_con", [128, 256], BF16)
        hsbf = sbuf("hsbf", [16, TILE], BF16)
        pooledF = sbuf("pooledF", [128, NCH, 32], F32)
        agh_s = sbuf("agh_s", [8, 16], F32)
        vbrow_s = sbuf("vbrow_s", [1, D], F32)
        vbrow_b = sbuf("vbrow_b", [1, D], BF16)
        rvrow_s = sbuf("rvrow_s", [1, 256], F32)
        rvrow_b = sbuf("rvrow_b", [1, 256], BF16)
        wlam_s = sbuf("wlam_s", [16, D], F32)
        wlam_b = sbuf("wlam_b", [16, D], BF16)
        wbm_s = sbuf("wbm_s", [16, D], F32)
        wbm_b = sbuf("wbm_b", [16, D], BF16)
        wc_s = sbuf("wc_s", [128, NCH, 16], F32)
        wc_b = sbuf("wc_b", [128, NCH, 16], BF16)
        wsl = [sbuf(f"wsl{i}", [128, 8192], BF16) for i in range(2)]

        def wslv(i, nkc, cols):
            return wsl[i][:, 0:nkc * cols].rearrange("p (a b) -> p a b", a=nkc)

        banks = [Bank(stack.enter_context(nc.psum_tensor(f"pb{i}", [128, 512], F32)))
                 for i in range(8)]

        block = stack.enter_context(nc.Block())
        qS = Sem(stack, nc, "qS")
        qG = Sem(stack, nc, "qG")
        qA = Sem(stack, nc, "qA")
        sPE = Sem(stack, nc, "sPE")
        sDVE = Sem(stack, nc, "sDVE")
        sACT = Sem(stack, nc, "sACT")
        sGP = Sem(stack, nc, "sGP")

        allsems = [qS, qG, qA, sPE, sDVE, sACT, sGP]

        def _serial_waits(e):
            if SERIAL[0] or 0 in JOINS[0]:
                for s_ in allsems:
                    if s_.n > 0:
                        e.wait_ge(s_.h, s_.n)

        def dmas(qname, pairs, waits=(), transpose=False):
            sec, sem = {"S": (block.sync, qS), "G": (block.gpsimd, qG),
                        "A": (block.scalar, qA)}[qname]
            box = {}

            def f(e):
                _serial_waits(e)
                for w in waits:
                    _wait(e, w)
                for out, in_ in pairs:
                    i = e.dma_start(out=out, in_=in_, transpose=transpose)
                    box["t"] = sem.inc(i, 16)
            sec(f)
            return box["t"]

        def dma(qname, out, in_, waits=(), transpose=False):
            return dmas(qname, [(out, in_)], waits=waits, transpose=transpose)

        def pe(ops, waits=()):
            box = {}

            def f(t):
                _serial_waits(t)
                for w in waits:
                    _wait(t, w)
                last = None
                for o in ops:
                    kw = {}
                    if o.get("tp") is not None:
                        kw["tile_position"] = o["tp"]
                    last = t.matmul(o["out"], o["lhsT"], o["rhs"],
                                    start=o.get("start", True),
                                    stop=o.get("stop", True), **kw)
                box["t"] = sPE.inc(last)
            block.tensor(f)
            return box["t"]

        def tr(out_ps, in_ap, waits=(), idn=128):
            return pe([{"out": out_ps, "lhsT": in_ap, "rhs": ident[0:idn, 0:idn]}],
                      waits=waits)

        def ve(fn, waits=()):
            box = {}

            def f(v):
                _serial_waits(v)
                for w in waits:
                    _wait(v, w)
                box["t"] = sDVE.inc(fn(v))
            block.vector(f)
            return box["t"]

        def ae(fn, waits=()):
            box = {}

            def f(a):
                _serial_waits(a)
                for w in waits:
                    _wait(a, w)
                box["t"] = sACT.inc(fn(a))
            block.scalar(f)
            return box["t"]

        def ge(fn, waits=()):
            box = {}

            def f(g):
                _serial_waits(g)
                for w in waits:
                    _wait(g, w)
                box["t"] = sGP.inc(fn(g))
            block.gpsimd(f)
            return box["t"]

        def tap(name, src_ap, waits=()):
            if name in tap_d:
                return dma("A", tap_d[name], src_ap, waits=waits)
            return None

        dummy = sbuf("dummy_jp", [1, 1], F32)

        def joinpoint():
            toks = [(s_, s_.n) for s_ in (qS, qG, qA, sPE, sDVE, sACT, sGP) if s_.n > 0]
            return ve(lambda v: v.memset(dummy[:, :], 0.0), waits=tuple(toks))

        def finish():
            def f(g):
                _wait(g, (qS, qS.n))
                _wait(g, (qA, qA.n))
                _wait(g, (qG, qG.n))
            block.gpsimd(f)
            return nc

        t_consts = dmas("S", [
            (ident[:], ident_d), (mask01[:], mask_d), (halom[:], halom_d),
            (latm[:], latm_d), (oh16[:], oh16_d), (ohlat[:], ohlat_d),
            (bc2[:], bc2_d), (bclat[:], bclat_d), (ones1[:], ones1_d),
            (onesf[:], onesf_d), (selHs[:], selH_d), (selAs[:], selA_d),
            (selCs[:], selC_d), (padm[:], padm_d), (bqkv[:], bqkv_d),
            (brq[:], brq_d), (brkv[:], brkv_d), (brout[:], brout_d),
            (bout[:], bout_d), (b1v[:], b1_d), (b2v[:], b2_d),
            (blam[:], blam_d), (rmsw[:], rmsw_d), (vbrow_s[:], vbrow_d),
            (rvrow_s[:], rvrow_d),
            (glw_s[:], glw_d), (grw_s[:], grw_d), (wlam_s[:], wlam_d),
            (wbm_s[:], wbm_d),
            (wc_s[:], wc_d.rearrange("(c p) n -> p c n", p=128)),
        ])

        t_qkvB = dma("G", qkvB, qkvw_d)
        t_rkvB = dma("G", rkvB, rkvw_d)
        t_rqB = dma("G", rqB, rqw_d)
        t_routB = dma("G", routB, routw_d)
        t_outwB = dma("G", outwB, outw_d)
        t_WgB = dma("G", WgB, wgate_d)
        t_wgB = dma("G", wgB, wgq_d)
        t_wuB = dma("G", wuB, wuq_d)
        t_wdB = dma("G", wdB, wdq_d)

        def prep_misc(v):
            v.memset(scarry[:], 0.0)
            v.memset(ccarry[:], 1.0)
            v.memset(ones1b[:], 1.0)
            v.memset(glgr[:], 0.0)
            v.tensor_copy(vbrow_b[:], vbrow_s[:])
            v.tensor_copy(rvrow_b[:], rvrow_s[:])
            v.tensor_copy(wlam_b[:], wlam_s[:])
            v.tensor_copy(wbm_b[:], wbm_s[:])
            last = v.tensor_copy(wc_b[:], wc_s[:])
            for c in range(8):
                v.tensor_copy(glgr[0:64, 0, c, 2 * c:2 * c + 1], glw_s[:, :])
                v.tensor_copy(glgr[64:128, 0, c, 2 * c + 1:2 * c + 2], glw_s[:, :])
                v.tensor_copy(glgr[0:64, 1, c, 2 * c:2 * c + 1], grw_s[:, :])
                last = v.tensor_copy(glgr[64:128, 1, c, 2 * c + 1:2 * c + 2], grw_s[:, :])
            return last
        t_misc = ve(prep_misc, waits=(t_consts,))

        tbv = pe([{"out": banks[0].ps[:, 0:512], "lhsT": ones1[:, :], "rhs": vbrow_b[:, 0:512]},
                  {"out": banks[1].ps[:, 0:512], "lhsT": ones1[:, :], "rhs": vbrow_b[:, 512:1024]}],
                 waits=(t_misc,))

        def f_vb(v):
            v.tensor_copy(vb_bc[:, 0:512], banks[0].ps[:, 0:512])
            return v.tensor_copy(vb_bc[:, 512:1024], banks[1].ps[:, 0:512])
        t_vb = ve(f_vb, waits=(tbv,))
        banks[0].free = t_vb
        banks[1].free = t_vb

        for cch in range(NCH):
            bk = banks[cch % 2]
            tr(bk.ps[:, 0:16], wlam_b[:, cch * 128:(cch + 1) * 128],
               waits=(t_misc, bk.free), idn=16)
            tb_ = tr(bk.ps[:, 16:32], wbm_b[:, cch * 128:(cch + 1) * 128], idn=16)

            def f(v, bk=bk, cch=cch):
                v.tensor_copy(lamTw[:, cch, :], bk.ps[:, 0:16])
                return v.tensor_copy(wbTw[:, cch, :], bk.ps[:, 16:32])
            bk.free = ve(f, waits=(tb_,))
        for cch in range(NCH):
            bk = banks[cch % 2]
            ta = tr(bk.ps[0:16, 0:128], wc_b[:, cch, :], waits=(t_misc, bk.free))
            bk.free = ve(lambda v, bk=bk, cch=cch:
                         v.tensor_copy(wcT[:, cch, :], bk.ps[0:16, 0:128]),
                         waits=(ta,))

        wtok = [None, None]
        rtok = [None, None]
        for it in range(32):
            pb = it % 2
            t_l = dma("S", wtmp[pb][:, 0:1024], w1_d[it * 128:(it + 1) * 128, :],
                      waits=tuple(w for w in (wtok[pb], rtok[pb]) if w))
            t_s = ae(lambda a, pb=pb: a.activation(out=wsgn[pb][:, 0:1024],
                                                   in_=wtmp[pb][:, 0:1024], func=AF.Sign),
                     waits=(t_l,))
            rtok[pb] = ve(lambda v, pb=pb, it=it: v.reduce_sum(
                alpha1[:, it:it + 1], wtmp[pb][:, 0:1024],
                axis=AX.X, apply_absolute_value=True), waits=(t_l,))
            wtok[pb] = dma("G", w1sB[it * 128:(it + 1) * 128, :], wsgn[pb][:, 0:1024],
                           waits=(t_s,))
        wtok2 = [None, None]
        for it in range(8):
            pb = it % 2
            t_l = dma("S", wtmp[pb][:, :], w2_d[it * 128:(it + 1) * 128, :],
                      waits=tuple(w for w in (wtok2[pb] or wtok[pb], rtok[pb]) if w))
            t_s = ae(lambda a, pb=pb: a.activation(out=wsgn[pb][:, :], in_=wtmp[pb][:, :],
                                                   func=AF.Sign),
                     waits=(t_l,))
            rtok[pb] = ve(lambda v, pb=pb, it=it: v.reduce_sum(
                alpha2[:, it:it + 1], wtmp[pb][:, :],
                axis=AX.X, apply_absolute_value=True), waits=(t_l,))
            wtok2[pb] = dma("G", w2sB[it * 128:(it + 1) * 128, :], wsgn[pb][:, :],
                            waits=(t_s,))

        def f_alpha(v):
            v.memset(zer16[:, :], 0.0)
            v.tensor_scalar_mul(alpha1[:], alpha1[:], 1.0 / 1024.0)
            return v.tensor_scalar_mul(alpha2[:], alpha2[:], 1.0 / 4096.0)
        t_alpha = ve(f_alpha)
        t_signs = _latest(*wtok, *wtok2, t_alpha, *rtok)
        tap("alpha1", alpha1[:], waits=(t_alpha,))

        def ln_stats(zget, T_, mu_out, inv_out, eps, extra=(), srcf32=True):
            b6, b7 = banks[6], banks[7]
            onesz = onesf if srcf32 else ones1b
            ops = []
            for c in range(NCH):
                ops.append({"out": b6.ps[0:1, 0:T_], "lhsT": onesz[:, :],
                            "rhs": zget(c)[:, 0:T_], "start": c == 0, "stop": c == NCH - 1})
            tsum = pe(ops, waits=(b6.free,) + tuple(extra))
            tsq_mm = None
            prev = [None, None]
            for c in range(NCH):
                pb = c % 2
                wl = [prev[pb]] + (list(extra) if c < 2 else [])
                t_sq = ae(lambda a, c=c, pb=pb: a.activation(
                    out=sqs[pb][:, 0:T_], in_=zget(c)[:, 0:T_], func=AF.Square),
                    waits=tuple(w for w in wl if w))
                tsq_mm = pe([{"out": b7.ps[0:1, 0:T_], "lhsT": onesf[:, :],
                              "rhs": sqs[pb][:, 0:T_], "start": c == 0, "stop": c == NCH - 1}],
                            waits=(t_sq, b7.free) if c == 0 else (t_sq,))
                prev[pb] = tsq_mm

            def fin(v):
                v.tensor_scalar_mul(srow[:, 0, 0:T_], b6.ps[0:1, 0:T_], 1.0 / D)
                v.tensor_scalar_mul(srow[:, 1, 0:T_], b7.ps[0:1, 0:T_], 1.0 / D)
                v.tensor_mul(srow[:, 2, 0:T_], srow[:, 0, 0:T_], srow[:, 0, 0:T_])
                v.tensor_sub(srow[:, 1, 0:T_], srow[:, 1, 0:T_], srow[:, 2, 0:T_])
                return v.tensor_scalar_add(srow[:, 1, 0:T_], srow[:, 1, 0:T_], float(eps))
            t_fin = ve(fin, waits=(tsum, tsq_mm))
            t_sq2 = ae(lambda a: a.activation(out=srow[:, 2, 0:T_], in_=srow[:, 1, 0:T_],
                                              func=AF.Sqrt),
                       waits=(t_fin,))

            def fin2(v):
                v.reciprocal(srow[:, 2, 0:T_], srow[:, 2, 0:T_])
                v.tensor_copy(mu_out, srow[:, 0, 0:T_])
                return v.tensor_copy(inv_out, srow[:, 2, 0:T_])
            t2 = ve(fin2, waits=tuple(w for w in (t_sq2, srowb_busy[0]) if w))
            b6.free = t2
            b7.free = t2
            return t2

        srowb_busy = [None]

        def ln_bcast(mu_row, inv_row, T_, extra=()):
            bkA, bkB = banks[4], banks[5]
            t = pe([{"out": bkA.ps[:, 0:T_], "lhsT": ones1[:, :], "rhs": mu_row},
                    {"out": bkB.ps[:, 0:T_], "lhsT": ones1[:, :], "rhs": inv_row}],
                   waits=tuple(w for w in (bkA.free, bkB.free) + tuple(extra) if w))
            srowb_busy[0] = t
            return t

        def ln_apply(xget, zget, T_, t_bc):
            bkA, bkB = banks[4], banks[5]
            tk = None
            for c in range(NCH):
                def f(v, c=c):
                    v.tensor_sub(zget(c)[:, 0:T_], xget(c)[:, 0:T_], bkA.ps[:, 0:T_])
                    return v.tensor_mul(zget(c)[:, 0:T_], zget(c)[:, 0:T_], bkB.ps[:, 0:T_])
                tk = ve(f, waits=(t_bc,) if c == 0 else ())
            bkA.free = tk
            bkB.free = tk
            return tk

        def full_ln(xget, zget, T_, eps, extra=(), srcf32=True,
                    mu_out=None, inv_out=None, stats=True):
            mu_out = srowb[:, 0, 0:T_] if mu_out is None else mu_out
            inv_out = srowb[:, 1, 0:T_] if inv_out is None else inv_out
            if stats:
                t_st = ln_stats(xget, T_, mu_out, inv_out, eps, extra=extra,
                                srcf32=srcf32)
                t_bc = ln_bcast(mu_out, inv_out, T_, extra=(t_st,))
            else:
                t_bc = ln_bcast(mu_out, inv_out, T_, extra=_latest(*extra))
            return ln_apply(xget, zget, T_, t_bc)

        t_pool = None
        t_z0 = None
        t_ap = None
        for tt in list(range(1, n_tiles)) + [0]:
            t_x = dma("S", X[:],
                      x_d.rearrange("(c p) n -> p c n", p=128)[:, :, tt * TILE:(tt + 1) * TILE],
                      waits=t_signs + ((t_ap,) if t_ap else ()))
            t_ap = full_ln(lambda c: X[:, c, :], lambda c: z[:, c, :], TILE, 1e-5,
                           extra=(t_x,) + t_signs, mu_out=smu0[:, tt, :],
                           inv_out=sinv0[:, tt, :])
            for c in range(NCH):
                t_pool = ve(lambda v, c=c, tt=tt: v.reduce_sum(
                    pooledF[:, c, tt * 4:(tt + 1) * 4],
                    z[:, c, :].rearrange("p (g w) -> p g w", w=128), axis=AX.X),
                    waits=(t_ap,) if c == 0 else ())
            if tt == 0:
                t_z0 = t_ap
        tap("z0", z[:], waits=(t_z0,))

        t_xh = dma("G", hz[:], xh_d.rearrange("(c p) n -> p c n", p=128),
                   waits=t_signs)
        t_hst = ln_stats(lambda c: hz[:, c, :], 128, hmu[:, :], hinv[:, :], 1e-5,
                         extra=(t_xh,), srcf32=False)
        t_hbc = ln_bcast(hmu[:, :], hinv[:, :], 128, extra=(t_hst,))
        t_hz = ln_apply(lambda c: hz[:, c, :], lambda c: hz[:, c, :], 128, t_hbc)
        tap("hz", hz[:], waits=(t_hz,))

        def f_pad(v):
            last = None
            for c in range(NCH):
                for reg in range(4):
                    v.tensor_copy(pooledT[:, c, reg * 32:(reg + 1) * 32], pooledF[:, c, :])
                last = v.tensor_mul(pooledT[:, c, :], pooledT[:, c, :], padm[:, :])
            return last
        t_pad = ve(f_pad, waits=(t_pool, t_misc))
        tap("pooled", pooledT[:], waits=(t_pad,))

        t_rkvl = dma("S", wslv(0, 8, 512)[:, :, :],
                     rkvB.rearrange("(c p) n -> p c n", p=128), waits=(t_rkvB,))
        wsl_busy = [None, None]
        for oc in range(2):
            bk = banks[oc]
            ops = [{"out": bk.ps[:, 0:64], "lhsT": wslv(0, 8, 512)[:, kc, oc * 128:(oc + 1) * 128],
                    "rhs": pooledT[:, kc, 0:64], "start": kc == 0, "stop": kc == NCH - 1}
                   for kc in range(NCH)]
            t_mm = pe(ops, waits=(t_rkvl, t_pad, bk.free))
            bk.free = ve(lambda v, bk=bk, oc=oc: v.tensor_copy(rk_con[:, oc, :], bk.ps[:, 0:64]),
                         waits=(t_mm,))
        bkv = banks[2]
        ops = [{"out": bkv.ps[:, 0:256], "lhsT": pooledT[:, kc, :],
                "rhs": wslv(0, 8, 512)[:, kc, 256:512], "start": kc == 0, "stop": kc == NCH - 1}
               for kc in range(NCH)]
        t_rvmm = pe(ops, waits=(t_rkvl, t_pad, bkv.free))
        wsl_busy[0] = t_rvmm
        bkv.free = ve(lambda v: v.tensor_copy(rv_con[:, :], bkv.ps[:, 0:256]),
                      waits=(t_rvmm,))

        t_agw = dmas("G", [(agkv_in[0], rk_con[:].rearrange("p a b -> p (a b)")),
                           (agkv_in[1], rv_con[:, 0:128]),
                           (agkv_in[2], rv_con[:, 128:256])],
                     waits=(banks[0].free, banks[1].free, bkv.free))
        box1 = {}

        def f_cc1(g):
            _wait(g, t_agw)
            i = g.collective_compute("AllGather", ALU.bypass,
                                     replica_groups=[[0, 1, 2, 3, 4, 5, 6, 7]],
                                     ins=[agkv_in], outs=[agkv_out])
            box1["t"] = sGP.inc(i)
        block.gpsimd(f_cc1)
        t_agr = dmas("S", [(agkv_s[:, s_ * 8:(s_ + 1) * 8, :],
                            agkv_out[:, s_, :, :].rearrange("r p n -> p r n"))
                           for s_ in range(3)],
                     waits=(box1["t"],))

        def f_comb(v):
            last = None
            for s_ in range(3):
                dst = (rkallT[:].rearrange("p a b -> p (a b)") if s_ == 0
                       else rvlm[:, (s_ - 1) * 128:s_ * 128])
                v.tensor_scalar(dst, agkv_s[:, s_ * 8, :], selAs[:, 0:1], None, ALU.mult)
                for r in range(1, 8):
                    v.scalar_tensor_tensor(dst, agkv_s[:, s_ * 8 + r, :], selAs[:, r:r + 1],
                                           dst, ALU.mult, ALU.add)
                for r in range(8):
                    last = v.scalar_tensor_tensor(dst, agkv_s[:, s_ * 8 + r, :], selCs[:, r:r + 1],
                                                  dst, ALU.mult, ALU.add)
            return last
        t_comb = ve(f_comb, waits=(t_agr,))

        t_rvbb = pe([{"out": banks[3].ps[:, 0:256], "lhsT": ones1[:, :],
                      "rhs": rvrow_b[:, 0:256]}], waits=(banks[3].free, t_misc))

        def f_bias(v):
            v.tensor_scalar(rkallT[:, 0, :], rkallT[:, 0, :], brkv[:, 0:1], None, ALU.add)
            v.tensor_scalar(rkallT[:, 1, :], rkallT[:, 1, :], brkv[:, 1:2], None, ALU.add)
            return v.tensor_add(rvlm[:, :], rvlm[:, :], banks[3].ps[:, 0:256])
        t_rkrv = ve(f_bias, waits=(t_comb, t_rvbb))
        banks[3].free = t_rkrv
        tap("rkall", rkallT[:], waits=(t_rkrv,))
        tap("rvlm", rvlm[:], waits=(t_rkrv,))

        if upto == "prep":
            return finish()

        wsl_busy = [None, None]

        def proj(dramB, K, NOUT, rhs_get, evac, transposed=False, t_src=(),
                 slab_cols=512):
            nkc = K // 128
            nslab = NOUT // slab_cols
            socc = slab_cols // 128
            slab_tok = [None, None]
            mm_tok = [wsl_busy[0], wsl_busy[1]]
            for j in range(nslab + 1):
                pb = j % 2
                if j < nslab:
                    w0 = tuple(w for w in (slab_tok[pb], mm_tok[pb]) if w)
                    w0 = w0 + (tuple(t_src) if j == 0 else ())
                    if not transposed:
                        slab_tok[pb] = dma(
                            "S", wslv(pb, nkc, slab_cols)[:, :, :],
                            dramB.rearrange("(c p) n -> p c n", p=128)[
                                :, :, j * slab_cols:(j + 1) * slab_cols],
                            waits=w0)
                    else:
                        prs = [(wslv(pb, nkc, slab_cols)[:, kc, :],
                                dramB[j * slab_cols:(j + 1) * slab_cols,
                                      kc * 128:(kc + 1) * 128])
                               for kc in range(nkc)]
                        slab_tok[pb] = dmas("S", prs, transpose=True, waits=w0)
                if j == 0:
                    continue
                jj = j - 1
                pbc = jj % 2
                for oc in range(socc):
                    og = jj * socc + oc
                    bk = banks[og % 2]
                    ops = [{"out": bk.ps[:, 0:TILE],
                            "lhsT": wslv(pbc, nkc, slab_cols)[:, kc, oc * 128:(oc + 1) * 128],
                            "rhs": rhs_get(kc),
                            "start": kc == 0, "stop": kc == nkc - 1}
                           for kc in range(nkc)]
                    t_mm = pe(ops, waits=tuple(w for w in (slab_tok[pbc], bk.free) if w))
                    mm_tok[pbc] = t_mm
                    bk.free = evac(bk, og, t_mm)
            wsl_busy[0] = mm_tok[0]
            wsl_busy[1] = mm_tok[1]
            return _later(banks[0].free, banks[1].free)

        def z_get(buf):
            return lambda kc: buf[:, kc, :]

        t_x2w = None
        t_z2w = None
        t_scan = None
        t_tails = None
        t_hsw = None
        for tt in range(n_tiles):
            first = tt == 0
            if not first:
                t_x = dma("S", X[:],
                          x_d.rearrange("(c p) n -> p c n", p=128)[:, :, tt * TILE:(tt + 1) * TILE],
                          waits=(t_x2w,) if t_x2w else ())
                t_z = full_ln(lambda c: X[:, c, :], lambda c: z[:, c, :], TILE, 1e-5,
                              extra=(t_x, t_scan, t_z2w), mu_out=smu0[:, tt, :],
                              inv_out=sinv0[:, tt, :], stats=False)
            else:
                t_z = t_z0

            def ev_qk(bk, og, tok):
                dstb = qT if og < 8 else kT
                c = og % 8

                def f(v):
                    return v.tensor_scalar(dstb[:, c, :], bk.ps[:, 0:TILE],
                                           bqkv[:, og:og + 1], None, ALU.add)
                return ve(f, waits=(tok,))
            proj(qkvB, D, 2048, z_get(z), ev_qk, t_src=(t_qkvB, t_z, t_comb))

            t_vw0 = dma("S", wslv(0, 8, 512)[:, :, :],
                        qkvB.rearrange("(c p) n -> p c n", p=128)[:, :, 2048:2560],
                        waits=tuple(w for w in (wsl_busy[0],) if w))
            t_vw1 = dma("S", wslv(1, 8, 512)[:, :, :],
                        qkvB.rearrange("(c p) n -> p c n", p=128)[:, :, 2560:3072],
                        waits=tuple(w for w in (wsl_busy[1],) if w))
            for tb in range(NM):
                for vf in range(2):
                    bk = banks[(tb * 2 + vf) % 2]
                    ops = [{"out": bk.ps[:, 0:512],
                            "lhsT": z[:, kc, tb * 128:(tb + 1) * 128],
                            "rhs": wslv(vf, 8, 512)[:, kc, :],
                            "start": kc == 0, "stop": kc == NCH - 1}
                           for kc in range(NCH)]
                    t_mm = pe(ops, waits=((t_vw0, t_vw1)[vf], bk.free)
                              + ((t_z,) if tb == 0 else ()))
                    wsl_busy[vf] = t_mm

                    def f(v, bk=bk, tb=tb, vf=vf):
                        return v.tensor_add(vtok[:, 2 * tb + vf, :],
                                            bk.ps[:, 0:512], vb_bc[:, vf * 512:(vf + 1) * 512])
                    bk.free = ve(f, waits=(t_mm, t_vb))
            if first:
                t_hk0 = dma("S", wslv(0, 8, 512)[:, :, :],
                            qkvB.rearrange("(c p) n -> p c n", p=128)[:, :, 1024:1536],
                            waits=tuple(w for w in (wsl_busy[0],) if w))
                t_hk1 = dma("S", wslv(1, 8, 512)[:, :, :],
                            qkvB.rearrange("(c p) n -> p c n", p=128)[:, :, 1536:2048],
                            waits=tuple(w for w in (wsl_busy[1],) if w))
                for c in range(NCH):
                    bk = banks[2 + c % 2]
                    vvi = 0 if c < 4 else 1
                    ops = [{"out": bk.ps[:, 0:128],
                            "lhsT": wslv(vvi, 8, 512)[:, kc, (c % 4) * 128:(c % 4 + 1) * 128],
                            "rhs": hz[:, kc, :], "start": kc == 0, "stop": kc == NCH - 1}
                           for kc in range(NCH)]
                    t_mm = pe(ops, waits=((t_hk0, t_hk1)[vvi], t_hz, bk.free))
                    wsl_busy[vvi] = t_mm

                    def f(v, bk=bk, c=c):
                        return v.tensor_scalar(ktail[:, c, :], bk.ps[:, 0:128],
                                               bqkv[:, 8 + c:9 + c], None, ALU.add)
                    bk.free = ve(f, waits=(t_mm,))
                t_hv0 = dma("S", wslv(0, 8, 512)[:, :, :],
                            qkvB.rearrange("(c p) n -> p c n", p=128)[:, :, 2048:2560],
                            waits=tuple(w for w in (wsl_busy[0],) if w))
                t_hv1 = dma("S", wslv(1, 8, 512)[:, :, :],
                            qkvB.rearrange("(c p) n -> p c n", p=128)[:, :, 2560:3072],
                            waits=tuple(w for w in (wsl_busy[1],) if w))
                for vf in range(2):
                    bk = banks[4 + vf]
                    ops = [{"out": bk.ps[:, 0:512], "lhsT": hz[:, kc, :],
                            "rhs": wslv(vf, 8, 512)[:, kc, :],
                            "start": kc == 0, "stop": kc == NCH - 1}
                           for kc in range(NCH)]
                    t_mm = pe(ops, waits=((t_hv0, t_hv1)[vf], t_hz, bk.free))
                    wsl_busy[vf] = t_mm

                    def f(v, bk=bk, vf=vf):
                        return v.tensor_add(vtail[:, vf * 512:(vf + 1) * 512],
                                            bk.ps[:, 0:512], vb_bc[:, vf * 512:(vf + 1) * 512])
                    bk.free = ve(f, waits=(t_mm, t_vb))
                t_tails = _later(banks[2].free, banks[3].free, banks[4].free, banks[5].free)
            t_qkv_done = _later(*[b.free for b in banks])
            tap("qT", qT[:], waits=(t_qkv_done,))
            tap("kT", kT[:], waits=(t_qkv_done,))
            tap("vtok", vtok[:], waits=(t_qkv_done,))
            tap("ktail", ktail[:], waits=(t_tails,))
            tap("vtail", vtail[:], waits=(t_tails,))
            if upto == "qkv":
                return finish()

            sgrp_open = False
            t_lsum = None
            t_av_last = None
            for cpair in range(8):
                for s in range(-1, NM):
                    q0 = max(0, s * 128)
                    q1 = min(TILE, s * 128 + 256)
                    wq = q1 - q0
                    joff = q0 - s * 128
                    t_e = [None, None]
                    for hh in range(2):
                        hb = 64 * hh
                        bk = banks[2 + hh]
                        ksrc = (ktail[hb:hb + 64, cpair, 0:128] if s < 0
                                else kT[hb:hb + 64, cpair, s * 128:(s + 1) * 128])
                        wl = [bk.free]
                        if cpair == 0 and s == -1:
                            wl += [t_qkv_done, t_tails]
                        t_sc = pe([{"out": bk.ps[0:128, 0:wq], "lhsT": ksrc,
                                    "rhs": qT[hb:hb + 64, cpair, q0:q1]}],
                                  waits=tuple(w for w in wl if w))
                        t_e[hh] = ae(lambda a, bk=bk, hh=hh, wq=wq: a.activation(
                            out=etile[:, hh, 0:wq], in_=bk.ps[0:128, 0:wq],
                            func=AF.Exp, scale=0.125),
                            waits=tuple(w for w in (t_sc, t_av_last) if w))
                        bk.free = t_e[hh]
                    mref = halom[:, 0:128] if (first and s < 0) else mask01[:, joff:joff + wq]

                    def fm(g, mref=mref, wq=wq):
                        g.tensor_mul(etile[:, 0, 0:wq], etile[:, 0, 0:wq], mref)
                        return g.tensor_mul(etile[:, 1, 0:wq], etile[:, 1, 0:wq], mref)
                    t_m = ge(fm, waits=(t_e[0], t_e[1]))
                    for mpart in (s, s + 1):
                        if mpart < 0 or mpart >= NM:
                            continue
                        mo = mpart * 128
                        bkL = banks[4]
                        bkS = banks[6]
                        eq0 = mpart * 128 - q0
                        opening = mpart == s + 1
                        closing = mpart == s
                        ops = []
                        wl = [t_m]
                        for hh in range(2):
                            h = 2 * cpair + hh
                            vs = (vtail[:, h * 64:(h + 1) * 64] if s < 0
                                  else vtok[:, 2 * s + (h * 64) // 512,
                                            (h * 64) % 512:(h * 64) % 512 + 64])
                            if opening and hh == 0 and mpart == 0:
                                wl.append(bkL.free)
                            ops.append({"out": bkL.ps[64 * hh:64 * (hh + 1), mo:mo + 128],
                                        "lhsT": vs, "rhs": etile[:, hh, eq0:eq0 + 128],
                                        "start": bool(opening),
                                        "stop": bool(closing),
                                        "tp": (0, 64) if hh == 1 else None})
                            s_open = bool(opening and hh == 0 and not sgrp_open)
                            s_close = bool(closing and hh == 1 and cpair == 7
                                           and mpart == NM - 1)
                            if s_open:
                                wl.append(bkS.free)
                                sgrp_open = True
                            ops.append({"out": bkS.ps[0:16, mo:mo + 128],
                                        "lhsT": oh16[:, h, :],
                                        "rhs": etile[:, hh, eq0:eq0 + 128],
                                        "start": s_open, "stop": s_close})
                        t_av = pe(ops, waits=tuple(w for w in wl if w))
                        t_av_last = t_av
                        if closing and mpart == NM - 1:
                            def fev(v, cpair=cpair):
                                return v.tensor_copy(bag1[:, cpair, :], banks[4].ps[:, 0:512])
                            banks[4].free = ve(fev, waits=(t_av,))
                        if closing and cpair == 7 and mpart == NM - 1:
                            def fsv(v):
                                return v.tensor_copy(lsum[:, :], banks[6].ps[0:16, 0:512])
                            t_lsum = ve(fsv, waits=(t_av,))
                            banks[6].free = t_lsum
            t_localu = banks[4].free
            tap("localu", bag1[:], waits=(t_localu,))
            tap("lsum", lsum[:], waits=(t_lsum,))

            def f_lr(v):
                v.tensor_scalar_max(lsum[:, :], lsum[:, :], 1e-30)
                return v.reciprocal(lsum[:, :], lsum[:, :])
            t_lrec = ve(f_lr, waits=(t_lsum,))
            if tt < n_tiles - 1:
                def f_tails(v):
                    for c in range(NCH):
                        v.tensor_copy(ktail[:, c, :], kT[:, c, TILE - 128:TILE])
                    v.tensor_copy(vtail[:, 0:512], vtok[:, 2 * (NM - 1), :])
                    return v.tensor_copy(vtail[:, 512:1024], vtok[:, 2 * (NM - 1) + 1, :])
                t_tails = ve(f_tails, waits=(t_qkv_done, t_localu))
            if upto == "local":
                return finish()

            def ev_rq(bk, og, tok):
                def f(v):
                    return v.tensor_scalar(rqTb[:, og, :], bk.ps[:, 0:TILE],
                                           brq[:, og:og + 1], None, ALU.add)
                return ve(f, waits=(tok,))
            proj(rqB, D, 256, z_get(z), ev_rq, t_src=(t_rqB,), slab_cols=256)
            t_rq = _later(banks[0].free, banks[1].free)
            tap("rqT", rqTb[:], waits=(t_rq,))

            lat_last = None
            lsum_lat = None
            for cpair in range(2):
                bk = banks[2 + cpair]
                ops = []
                for hh in range(2):
                    hb = 64 * hh
                    ops.append({"out": bk.ps[hb:hb + 64, 0:512],
                                "lhsT": rkallT[hb:hb + 64, cpair, :],
                                "rhs": rqTb[hb:hb + 64, cpair, :],
                                "tp": (64, 64) if hh == 1 else None})
                wl = [bk.free] + ([t_rq, t_rkrv] if cpair == 0 else [])
                t_ls = pe(ops, waits=tuple(w for w in wl if w))
                t_le = ae(lambda a, bk=bk, cpair=cpair: a.activation(
                    out=lexp[:, cpair, :], in_=bk.ps[:, 0:512],
                    func=AF.Exp, scale=0.125), waits=(t_ls,))
                bk.free = t_le

                def fm(g, cpair=cpair, tt=tt):
                    return g.tensor_mul(lexp[:, cpair, :], lexp[:, cpair, :],
                                        latm[:, tt * TILE:(tt + 1) * TILE])
                t_lm = ge(fm, waits=(t_le,))
                bkL = banks[4] if cpair == 0 else banks[5]
                ops = []
                for hh in range(2):
                    hb = 64 * hh
                    lh = 2 * cpair + hh
                    ops.append({"out": bkL.ps[hb:hb + 64, 0:512],
                                "lhsT": rvlm[hb:hb + 64, lh * 64:(lh + 1) * 64],
                                "rhs": lexp[hb:hb + 64, cpair, :],
                                "tp": (64, 64) if hh == 1 else None})
                ops.append({"out": banks[6].ps[0:4, 0:512],
                            "lhsT": ohlat[:, cpair, :],
                            "rhs": lexp[:, cpair, :],
                            "start": cpair == 0, "stop": cpair == 1})
                wl = [t_lm, bkL.free] + ([banks[6].free] if cpair == 0 else [])
                t_lv = pe(ops, waits=tuple(w for w in wl if w))

                def flv(v, bkL=bkL, cpair=cpair):
                    return v.tensor_copy(latu[:, cpair, :], bkL.ps[:, 0:512])
                bkL.free = ve(flv, waits=(t_lv,))
                lat_last = _later(lat_last, bkL.free)
                if cpair == 1:
                    def fls(v):
                        return v.tensor_copy(latsum[:, :], banks[6].ps[0:4, 0:512])
                    lsum_lat = ve(fls, waits=(t_lv,))
                    banks[6].free = lsum_lat
            tap("latu", latu[:], waits=(lat_last,))
            tap("latsum", latsum[:], waits=(lsum_lat,))

            def f_latrec(v):
                v.tensor_scalar_max(latsum[:, :], latsum[:, :], 1e-30)
                v.reciprocal(latsum[:, :], latsum[:, :])
                return v.tensor_copy(latsb[:, :], latsum[:, :])
            t_latrec = ve(f_latrec, waits=(lsum_lat,))

            for cpair in range(2):
                bk = banks[2 + cpair]
                t_bc = pe([{"out": bk.ps[:, 0:512], "lhsT": bclat[:, cpair, :],
                            "rhs": latsb[:, :]}],
                          waits=(t_latrec, bk.free))

                def f(v, bk=bk, cpair=cpair):
                    return v.tensor_mul(latnb[:, cpair, :], latu[:, cpair, :],
                                        bk.ps[:, 0:512])
                bk.free = ve(f, waits=(t_bc, lat_last))
            t_latn = _later(banks[2].free, banks[3].free)
            tap("latn", latnb[:], waits=(t_latn,))

            t_rol = dma("S", wslv(0, 2, 1024)[:, :, :],
                        routB.rearrange("(c p) n -> p c n", p=128),
                        waits=tuple(w for w in (t_routB, wsl_busy[0]) if w))
            for og in range(8):
                bk = banks[og % 2]
                ops = [{"out": bk.ps[:, 0:TILE],
                        "lhsT": wslv(0, 2, 1024)[:, kc, og * 128:(og + 1) * 128],
                        "rhs": latnb[:, kc, :],
                        "start": kc == 0, "stop": kc == 1}
                       for kc in range(2)]
                wl = [t_rol, bk.free] + ([t_latn] if og == 0 else [])
                t_mm = pe(ops, waits=tuple(w for w in wl if w))
                wsl_busy[0] = t_mm

                def f(v, bk=bk, og=og):
                    return v.tensor_scalar(bag2[:, og, :], bk.ps[:, 0:TILE],
                                           brout[:, og:og + 1], None, ALU.add)
                bk.free = ve(f, waits=(t_mm,))
            t_remote = _later(banks[0].free, banks[1].free)
            tap("remote", bag2[:], waits=(t_remote,))

            bkl, bkr = banks[2], banks[3]
            opsl = [{"out": bkl.ps[0:16, 0:512], "lhsT": glgr[:, 0, c, :],
                     "rhs": bag1[:, c, :],
                     "start": c == 0, "stop": c == NCH - 1} for c in range(NCH)]
            opsr = [{"out": bkr.ps[0:16, 0:512], "lhsT": glgr[:, 1, c, :],
                     "rhs": bag2[:, c, :],
                     "start": c == 0, "stop": c == NCH - 1} for c in range(NCH)]
            t_gl = pe(opsl, waits=(bkl.free, t_localu))
            t_gr = pe(opsr, waits=(bkr.free, t_remote))

            def f_g(v):
                v.tensor_mul(gtmp[:, :], bkl.ps[0:16, 0:512], lsum[:, :])
                return v.tensor_add(gtmp[:, :], gtmp[:, :], bkr.ps[0:16, 0:512])
            t_gd = ve(f_g, waits=(t_gl, t_gr, t_lrec))
            t_sig = ae(lambda a: a.activation(out=grow[0:16, :], in_=gtmp[:, :],
                                              func=AF.Sigmoid), waits=(t_gd,))

            def f_g2(v):
                v.tensor_mul(ghb[:, :], grow[0:16, :], lsum[:, :])
                return v.tensor_scalar(gm1b[:, :], grow[0:16, :],
                                       -1.0, 1.0, ALU.mult, ALU.add)
            t_grow = ve(f_g2, waits=(t_sig,))
            bkl.free = t_grow
            bkr.free = t_grow
            tap("grow", grow[:], waits=(t_grow,))

            t_mix = None
            for cpair in range(8):
                bka = banks[2 + cpair % 2]
                bkb = banks[4 + cpair % 2]
                wl = [bka.free] + ([t_grow] if cpair == 0 else [])
                t_b1 = pe([{"out": bka.ps[:, 0:512], "lhsT": bc2[0:16, cpair, :],
                            "rhs": ghb[:, :]}],
                          waits=tuple(w for w in wl if w))
                t_b2 = pe([{"out": bkb.ps[:, 0:512], "lhsT": bc2[0:16, cpair, :],
                            "rhs": gm1b[:, :]}],
                          waits=(bkb.free,) if bkb.free else ())

                def f(v, bka=bka, bkb=bkb, cpair=cpair):
                    v.tensor_mul(z[:, cpair, :], bag1[:, cpair, :], bka.ps[:, 0:512])
                    v.tensor_mul(mtmp[:, :], bag2[:, cpair, :], bkb.ps[:, 0:512])
                    return v.tensor_add(z[:, cpair, :], z[:, cpair, :], mtmp[:, :])
                tk = ve(f, waits=(t_b1, t_b2))
                bka.free = tk
                bkb.free = tk
                t_mix = tk
            tap("mixed", z[:], waits=(t_mix,))
            if upto == "mix":
                return finish()

            def ev_out(bk, og, tok):
                def f(v):
                    return v.scalar_tensor_tensor(X[:, og, :], bk.ps[:, 0:TILE],
                                                  bout[:, og:og + 1], X[:, og, :],
                                                  ALU.add, ALU.add)
                return ve(f, waits=(tok,))
            proj(outwB, D, D, z_get(z), ev_out, t_src=(t_outwB, t_mix))
            t_x1 = _later(banks[0].free, banks[1].free)
            tap("X1", X[:], waits=(t_x1,))

            t_z1 = full_ln(lambda c: X[:, c, :], lambda c: z[:, c, :], TILE, 1e-5,
                           extra=(t_x1,))
            tap("z1", z[:], waits=(t_z1,))

            def ev_w1(bk, og, tok):
                def f(a):
                    return a.activation(out=big1[:, og, :], in_=bk.ps[:, 0:TILE],
                                        func=AF.Gelu, bias=b1v[:, og:og + 1],
                                        scale=alpha1[:, og:og + 1])
                return ae(f, waits=(tok,))
            proj(w1sB, D, 4096, z_get(z), ev_w1, transposed=True,
                 t_src=(t_z1, t_mix, t_grow, t_localu, t_qkv_done, t_tails))
            t_h0 = _later(banks[0].free, banks[1].free)
            tap("hffn", big1[:], waits=(t_h0,))

            def ev_w2(bk, og, tok):
                def f(v):
                    v.tensor_scalar(sqs[0][:, :], bk.ps[:, 0:TILE],
                                    alpha2[:, og:og + 1], b2v[:, og:og + 1],
                                    ALU.mult, ALU.add)
                    return v.tensor_add(X[:, og, :], X[:, og, :], sqs[0][:, :])
                return ve(f, waits=(tok,))
            proj(w2sB, 4096, D, z_get(big1), ev_w2, transposed=True,
                 t_src=(t_h0,), slab_cols=128)
            t_x2 = _later(banks[0].free, banks[1].free)
            tap("X2", X[:], waits=(t_x2,))

            t_z2 = full_ln(lambda c: X[:, c, :], lambda c: z[:, c, :], TILE, 1e-5,
                           extra=(t_x2,))
            tap("z2", z[:], waits=(t_z2,))
            t_x2w = dma("A", x2d.rearrange("(c p) n -> p c n", p=128)[:, :, tt * TILE:(tt + 1) * TILE],
                        X[:], waits=(t_x2,))
            t_z2w = dma("A", z2d.rearrange("(c p) n -> p c n", p=128)[:, :, tt * TILE:(tt + 1) * TILE],
                        z[:], waits=(t_z2,))

            bkl, bkb = banks[0], banks[1]
            opsl = [{"out": bkl.ps[0:16, 0:512], "lhsT": lamTw[:, kc, :],
                     "rhs": z[:, kc, :],
                     "start": kc == 0, "stop": kc == NCH - 1} for kc in range(NCH)]
            opsb = [{"out": bkb.ps[0:16, 0:512], "lhsT": wbTw[:, kc, :],
                     "rhs": z[:, kc, :],
                     "start": kc == 0, "stop": kc == NCH - 1} for kc in range(NCH)]
            t_l = pe(opsl, waits=(bkl.free, t_z2))
            t_b = pe(opsb, waits=(bkb.free,))
            t_sg = ae(lambda a: a.activation(out=lamrow[:, :], in_=bkl.ps[0:16, 0:512],
                                             func=AF.Sigmoid, bias=blam[:, :]),
                      waits=(t_l,))
            t_cb = ve(lambda v: v.tensor_copy(bxrow[:, :], bkb.ps[0:16, 0:512]),
                      waits=(t_b,))
            bkl.free = t_sg
            bkb.free = t_cb
            tap("lam", lamrow[:], waits=(t_sg,))
            tap("bx", bxrow[:], waits=(t_cb,))

            def f_scan(v):
                v.tensor_tensor_scan(hslb[:, :], lamrow[:, :], bxrow[:, :],
                                     scarry[:, :], ALU.mult, ALU.add)
                v.tensor_tensor_scan(acumb[:, :], lamrow[:, :], zer16[:, :],
                                     ccarry[:, :], ALU.mult, ALU.add)
                v.tensor_copy(gtmp[:, :], hslb[:, :])
                v.tensor_copy(scarry[:, :], FA2[0:16, TILE - 1:TILE])
                return v.tensor_copy(ccarry[:, :], FA2[0:16, 2 * TILE - 1:2 * TILE])
            t_scan = ve(f_scan, waits=(t_sg, t_cb, t_misc, t_hsw))
            banks[0].free = t_scan
            banks[1].free = t_scan
            t_hsw = dmas("A", [(hsd[:, tt * TILE:(tt + 1) * TILE], hslb[:, :]),
                               (acd[:, tt * TILE:(tt + 1) * TILE], acumb[:, :])],
                         waits=(t_scan,))
            tap("hsl", hslb[:, :], waits=(t_scan,))
            tap("acum", acumb[:, :], waits=(t_scan,))

        if upto == "phaseA":
            return finish()

        t_hw = dma("G", agh_in, scarry[:, :], waits=(t_scan,))
        box2 = {}

        def f_cc2(g):
            _wait(g, t_hw)
            i = g.collective_compute("AllGather", ALU.bypass,
                                     replica_groups=[[0, 1, 2, 3, 4, 5, 6, 7]],
                                     ins=[agh_in], outs=[agh_out])
            box2["t"] = sGP.inc(i)
        block.gpsimd(f_cc2)
        t_agh = dma("S", agh_s[:], agh_out.rearrange("(r k) one -> r (k one)", k=16),
                    waits=(box2["t"],))
        t_selmm = pe([{"out": banks[0].ps[0:16, 0:1], "lhsT": agh_s[:, :],
                       "rhs": selHs[:, :]}],
                     waits=tuple(w for w in (t_agh, banks[0].free) if w))
        t_init = ve(lambda v: v.tensor_copy(initv[:, :], banks[0].ps[0:16, 0:1]),
                    waits=(t_selmm,))
        banks[0].free = t_init
        tap("hft", scarry[:], waits=(t_hw,))
        tap("initv", initv[:], waits=(t_init,))
        if upto == "phaseB":
            return finish()

        t_h1 = None
        t_hs = None
        t_odma = None
        t_y = None
        t_x3 = None
        for tt in range(n_tiles):
            t_x2r = dma("S", X[:],
                        x2d.rearrange("(c p) n -> p c n", p=128)[:, :, tt * TILE:(tt + 1) * TILE],
                        waits=tuple(w for w in (t_x2w, t_odma) if w))
            t_z2r = dma("S", z[:],
                        z2d.rearrange("(c p) n -> p c n", p=128)[:, :, tt * TILE:(tt + 1) * TILE],
                        waits=tuple(w for w in (t_z2w, t_h1) if w))
            t_hsr = dmas("S", [(hslb[:, :], hsd[:, tt * TILE:(tt + 1) * TILE]),
                               (acumb[:, :], acd[:, tt * TILE:(tt + 1) * TILE])],
                         waits=tuple(w for w in (t_hsw, t_hs) if w))
            if tt == 0:
                tap("cx2", X[:], waits=(t_x2r,))
                tap("cz2", z[:], waits=(t_z2r,))
                tap("chsl", hslb[:, :], waits=(t_hsr,))

            if SERIAL[0] or 1 in JOINS[0]:
                joinpoint()

            def ev_wg(bk, og, tok):
                def f(a):
                    return a.activation(out=bag1[:, og, :], in_=bk.ps[:, 0:TILE],
                                        func=AF.Sigmoid)
                return ae(f, waits=(tok,))
            proj(WgB, D, D, z_get(z), ev_wg, transposed=True,
                 t_src=(t_WgB, t_z2r) + tuple(w for w in (t_y, t_x3) if w))
            t_gate = _later(banks[0].free, banks[1].free)
            tap("gate1", bag1[:], waits=(t_gate,))
            if SERIAL[0] or 2 in JOINS[0]:
                joinpoint()

            def f_hs(v):
                v.scalar_tensor_tensor(gtmp[:, :], acumb[:, :], initv[:, 0:1],
                                       hslb[:, :], ALU.mult, ALU.add)
                return v.tensor_copy(hsbf[:, :], gtmp[:, :])
            t_hs = ve(f_hs, waits=tuple(w for w in (t_init, t_hsr, t_y) if w))

            t_y = None
            for oc in range(NCH):
                bk = banks[2 + oc % 2]
                wl = [bk.free] + ([t_hs] if oc == 0 else [])
                t_mm = pe([{"out": bk.ps[:, 0:512], "lhsT": wcT[:, oc, :],
                            "rhs": hsbf[:, :]}],
                          waits=tuple(w for w in wl if w))

                def f(v, bk=bk, oc=oc):
                    return v.tensor_mul(qT[:, oc, :], bk.ps[:, 0:512], bag1[:, oc, :])
                bk.free = ve(f, waits=(t_mm, t_gate))
                t_y = bk.free
            tap("yv", qT[:], waits=(t_y,))
            if SERIAL[0] or 3 in JOINS[0]:
                joinpoint()

            b7 = banks[7]
            prev = [None, None]
            tsq_mm = None
            for c in range(NCH):
                pb = c % 2
                wl = [prev[pb]] + ([t_y] if c < 2 else [])
                t_sq = ae(lambda a, c=c, pb=pb: a.activation(
                    out=sqs[pb][:, :], in_=qT[:, c, :], func=AF.Square),
                    waits=tuple(w for w in wl if w))
                tsq_mm = pe([{"out": b7.ps[0:1, 0:512], "lhsT": onesf[:, :],
                              "rhs": sqs[pb][:, :], "start": c == 0, "stop": c == NCH - 1}],
                            waits=(t_sq, b7.free) if c == 0 else (t_sq,))
                prev[pb] = tsq_mm
            t_ms = ve(lambda v: v.tensor_scalar(srow[:, 0, 0:TILE], b7.ps[0:1, 0:TILE],
                                                1.0 / D, 1e-6, ALU.mult, ALU.add),
                      waits=(tsq_mm,))
            t_msq = ae(lambda a: a.activation(out=srow[:, 0, 0:TILE], in_=srow[:, 0, 0:TILE],
                                              func=AF.Sqrt), waits=(t_ms,))

            def f_rrec(v):
                v.reciprocal(srow[:, 0, 0:TILE], srow[:, 0, 0:TILE])
                return v.tensor_copy(srowb_rms[:, 0:TILE], srow[:, 0, 0:TILE])
            t_rrec = ve(f_rrec, waits=(t_msq,))
            b7.free = t_rrec
            tap("rmsrow", srow[:, 0, :], waits=(t_rrec,))

            bk5 = banks[5]
            t_bc = pe([{"out": bk5.ps[:, 0:512], "lhsT": ones1[:, :],
                        "rhs": srowb_rms[:, 0:TILE]}],
                      waits=tuple(w for w in (bk5.free, t_rrec) if w))
            t_x3 = None
            for c in range(NCH):
                def f(v, c=c):
                    v.scalar_tensor_tensor(sqs[0][:, :], qT[:, c, :],
                                           rmsw[:, c:c + 1], bk5.ps[:, 0:512],
                                           ALU.mult, ALU.mult)
                    return v.tensor_add(X[:, c, :], X[:, c, :], sqs[0][:, :])
                wl = [t_bc] + ([t_x2r] if c == 0 else [])
                t_x3 = ve(f, waits=tuple(w for w in wl if w))
            bk5.free = t_x3
            tap("X3", X[:], waits=(t_x3,))
            if SERIAL[0] or 4 in JOINS[0]:
                joinpoint()

            t_z3 = full_ln(lambda c: X[:, c, :], lambda c: z[:, c, :], TILE, 1e-5,
                           extra=(t_x3,))
            tap("z3", z[:], waits=(t_z3,))
            if SERIAL[0] or 5 in JOINS[0]:
                joinpoint()

            def ev_g(bk, og, tok):
                def f(a):
                    return a.activation(out=big1[:, og, :], in_=bk.ps[:, 0:TILE],
                                        func=AF.Silu, scale=1.0 / 127.0)
                return ae(f, waits=(tok,))
            proj(wgB, D, 3072, z_get(z), ev_g, transposed=True,
                 t_src=(t_z3, t_y, t_gate))
            t_sg1 = _later(banks[0].free, banks[1].free)
            if SERIAL[0] or 6 in JOINS[0]:
                joinpoint()

            def ev_u(bk, og, tok):
                def f(v):
                    return v.scalar_tensor_tensor(big1[:, og, :], bk.ps[:, 0:TILE],
                                                  1.0 / 127.0, big1[:, og, :],
                                                  ALU.mult, ALU.mult)
                return ve(f, waits=(tok, t_sg1))
            proj(wuB, D, 3072, z_get(z), ev_u, transposed=True)
            t_h1 = _later(banks[0].free, banks[1].free)
            if SERIAL[0] or 7 in JOINS[0]:
                joinpoint()

            def ev_d(bk, og, tok):
                def f(v):
                    return v.scalar_tensor_tensor(X[:, og, :], bk.ps[:, 0:TILE],
                                                  1.0 / 127.0, X[:, og, :],
                                                  ALU.mult, ALU.add)
                return ve(f, waits=(tok,))
            proj(wdB, 3072, D, z_get(big1), ev_d, transposed=True,
                 t_src=(t_h1,), slab_cols=256)
            t_out = _later(banks[0].free, banks[1].free)
            t_odma = dma("S", out_d.rearrange("(c p) n -> p c n", p=128)[:, :, tt * TILE:(tt + 1) * TILE],
                         X[:], waits=(t_out,))
            if SERIAL[0] or 8 in JOINS[0]:
                joinpoint()

        return finish()


def _mk_consts(core):
    half = core % 2
    consts = {}
    consts["c_ident"] = np.eye(128, dtype=BF)
    j = np.arange(256)[None, :]
    kl = np.arange(128)[:, None]
    consts["c_mask"] = ((1 <= j - kl) & (j - kl <= 128)).astype(BF)
    consts["c_halom"] = (np.ascontiguousarray(consts["c_mask"][:, 128:256]) if half == 1
                         else np.zeros((128, 128), BF))
    cb = half * 32
    lat = np.arange(64)[:, None]
    chunk = cb + np.arange(M)[None, :] // 128
    lm = (lat < chunk)
    consts["c_latm"] = np.concatenate([lm, lm], axis=0).astype(BF)
    oh = np.zeros((128, 16, 16), BF)
    for h in range(16):
        oh[:, h, h] = 1
    consts["c_oh16"] = oh
    ohl = np.zeros((128, 2, 4), BF)
    for c in range(2):
        ohl[0:64, c, 2 * c] = 1
        ohl[64:128, c, 2 * c + 1] = 1
    consts["c_ohlat"] = ohl
    bc2 = np.zeros((64, 8, 128), BF)
    for c in range(8):
        bc2[2 * c, c, 0:64] = 1
        bc2[2 * c + 1, c, 64:128] = 1
        bc2[32 + 2 * c, c, 0:64] = 1
        bc2[32 + 2 * c + 1, c, 64:128] = 1
    consts["c_bc2"] = bc2
    bcl = np.zeros((4, 2, 128), BF)
    for c in range(2):
        bcl[2 * c, c, 0:64] = 1
        bcl[2 * c + 1, c, 64:128] = 1
    consts["c_bclat"] = bcl
    consts["c_ones1"] = np.ones((1, 128), BF)
    consts["c_onesf"] = np.ones((128, 1), np.float32)
    selH = np.zeros((8, 1), np.float32)
    if half == 1:
        selH[core - 1, 0] = 1.0
    consts["c_selH"] = selH
    selA = np.zeros((128, 8), np.float32)
    selA[:, core - 1 if half == 1 else core] = 1.0
    consts["c_selA"] = selA
    selC = np.zeros((128, 8), np.float32)
    if half == 1:
        selC[:, core] = 1.0
    consts["c_selC"] = selC
    padmv = np.zeros((128, 128), BF)
    for reg in range(4):
        if (reg % 2) == half:
            padmv[:, reg * 32:(reg + 1) * 32] = np.float32(1.0 / 128.0)
    consts["c_padm"] = padmv
    return consts


def make_in_maps(inputs):
    x = np.asarray(inputs["L0_attn_x"])
    qkv_b = np.asarray(inputs["L0_attn_qkv_b"])
    in_maps = []
    for core in range(8):
        b, half = core // 2, core % 2
        g0 = half * M
        im = {}
        im["x"] = np.ascontiguousarray(x[b, g0:g0 + M, :].T)
        im["xh"] = (np.ascontiguousarray(x[b, g0 - 128:g0, :].T) if half == 1
                    else np.zeros((D, 128), np.float32))
        im["qkv_w"] = np.asarray(inputs["L0_attn_qkv_w"])
        im["rq_w"] = np.asarray(inputs["L0_attn_rq_w"])
        im["rkv_w"] = np.asarray(inputs["L0_attn_rkv_w"])
        im["rout_w"] = np.asarray(inputs["L0_attn_rout_w"])
        im["out_w"] = np.asarray(inputs["L0_attn_out_w"])
        im["gl_w"] = np.asarray(inputs["L0_attn_gl_w"])
        im["gr_w"] = np.asarray(inputs["L0_attn_gr_w"])
        im["w1"] = np.asarray(inputs["L0_ffn_w1"])
        im["w2"] = np.asarray(inputs["L0_ffn_w2"])
        im["W_lambda"] = np.asarray(inputs["L1_attn_W_lambda"])
        im["W_B"] = np.asarray(inputs["L1_attn_W_B"])
        im["W_C"] = np.asarray(inputs["L1_attn_W_C"])
        im["W_gate"] = np.asarray(inputs["L1_attn_W_gate"])
        im["wg"] = np.asarray(inputs["L1_ffn_w_gate"])
        im["wu"] = np.asarray(inputs["L1_ffn_w_up"])
        im["wd"] = np.asarray(inputs["L1_ffn_w_down"])
        im["b_qkv"] = np.ascontiguousarray(qkv_b.reshape(24, 128).T)
        im["b_rq"] = np.ascontiguousarray(np.asarray(inputs["L0_attn_rq_b"]).reshape(2, 128).T)
        im["b_rkv"] = np.ascontiguousarray(np.asarray(inputs["L0_attn_rkv_b"]).reshape(4, 128).T)
        im["b_rout"] = np.ascontiguousarray(np.asarray(inputs["L0_attn_rout_b"]).reshape(8, 128).T)
        im["b_out"] = np.ascontiguousarray(np.asarray(inputs["L0_attn_out_b"]).reshape(8, 128).T)
        im["b_1"] = np.ascontiguousarray(np.asarray(inputs["L0_ffn_b1"]).reshape(32, 128).T)
        im["b_2"] = np.ascontiguousarray(np.asarray(inputs["L0_ffn_b2"]).reshape(8, 128).T)
        im["b_lam"] = np.ascontiguousarray(np.asarray(inputs["L1_attn_b_lambda"]).reshape(16, 1))
        im["rmsw"] = np.ascontiguousarray(np.asarray(inputs["L1_attn_rms_weight"]).reshape(8, 128).T)
        im["vb_row"] = np.ascontiguousarray(qkv_b[2 * D:].reshape(1, D))
        im["rv_row"] = np.ascontiguousarray(np.asarray(inputs["L0_attn_rkv_b"])[256:].reshape(1, 256))
        im.update(_mk_consts(core))
        in_maps.append(im)
    return in_maps


def kernel(**inputs):
    nc = build_program()
    in_maps = make_in_maps(inputs)
    res = run_bass_kernel_spmd(nc, in_maps, core_ids=list(range(8)))
    out = np.empty((4, 2 * M, D), np.float32)
    for core in range(8):
        b, half = core // 2, core % 2
        out[b, half * M:(half + 1) * M, :] = res.results[core]["out"].T
    return out


# version 04
# speedup vs baseline: 1.2141x; 1.2116x over previous
import contextlib
import numpy as np
import ml_dtypes

import concourse.bass as bass
import concourse.mybir as mybir
from concourse.bass_utils import run_bass_kernel_spmd

F32 = mybir.dt.float32
BF16 = mybir.dt.bfloat16
I32 = mybir.dt.int32
AF = mybir.ActivationFunctionType
ALU = mybir.AluOpType
AX = mybir.AxisListType
BF = ml_dtypes.bfloat16

D = 1024
NCH = 8
TILE = 512
NTILE = 8
M = TILE * NTILE
H = 16
NM = TILE // 128


class Sem:
    def __init__(self, stack, nc, name):
        self.h = stack.enter_context(nc.semaphore(name))
        self.n = 0

    def inc(self, instr, k=1):
        instr.then_inc(self.h, k)
        self.n += k
        return (self, self.n)


def _wait(eng, tok):
    if tok is None:
        return
    sem, val = tok
    if val > 0:
        eng.wait_ge(sem.h, val)


def _later(*toks):
    toks = [t for t in toks if t is not None]
    if not toks:
        return None
    sems = {id(t[0]) for t in toks}
    assert len(sems) == 1, "mixed-semaphore _later; pass tokens separately"
    return max(toks, key=lambda t: t[1])


def _latest(*toks):
    best = {}
    for t in toks:
        if t is None:
            continue
        k = id(t[0])
        if k not in best or t[1] > best[k][1]:
            best[k] = t
    return tuple(best.values())


class Bank:
    __slots__ = ("ps", "free")

    def __init__(self, ps):
        self.ps = ps
        self.free = None


SERIAL = [False]
JOINS = [set([0,1,2,3])]


def build_program(n_tiles=NTILE, taps=(), upto=None):
    nc = bass.Bass("TRN2", target_bir_lowering=False, debug=False)
    tapset = set(taps)

    def din(name, shape, dt=F32):
        return nc.dram_tensor(name, shape, dt, kind="ExternalInput").ap()

    x_d = din("x", [D, M])
    xh_d = din("xh", [D, 128])
    qkvw_d = din("qkv_w", [D, 3 * D])
    rqw_d = din("rq_w", [D, 256])
    rkvw_d = din("rkv_w", [D, 512])
    routw_d = din("rout_w", [256, D])
    outw_d = din("out_w", [D, D])
    glw_d = din("gl_w", [64, 1])
    grw_d = din("gr_w", [64, 1])
    w1_d = din("w1", [4096, D])
    w2_d = din("w2", [D, 4096])
    w1t_d = din("w1t", [D, 4096])
    w2t_d = din("w2t", [4096, D])
    wlam_d = din("W_lambda", [16, D])
    wbm_d = din("W_B", [16, D])
    wc_d = din("W_C", [D, 16])
    wgate_d = din("W_gate", [D, D])
    wgq_d = din("wg", [D, 3072], I32)
    wuq_d = din("wu", [D, 3072], I32)
    wdq_d = din("wd", [3072, D], I32)
    bqkv_d = din("b_qkv", [128, 24])
    brq_d = din("b_rq", [128, 2])
    brkv_d = din("b_rkv", [128, 4])
    brout_d = din("b_rout", [128, 8])
    bout_d = din("b_out", [128, 8])
    b1_d = din("b_1", [128, 32])
    b2_d = din("b_2", [128, 8])
    blam_d = din("b_lam", [16, 1])
    rmsw_d = din("rmsw", [128, 8])
    vbrow_d = din("vb_row", [1, D])
    rvrow_d = din("rv_row", [1, 256])
    ident_d = din("c_ident", [128, 128], BF16)
    mask_d = din("c_mask", [128, 256], BF16)
    halom_d = din("c_halom", [128, 128], BF16)
    latm_d = din("c_latm", [128, M], BF16)
    oh16_d = din("c_oh16", [128, 16, 16], BF16)
    ohlat_d = din("c_ohlat", [128, 2, 4], BF16)
    bc2_d = din("c_bc2", [64, 8, 128], BF16)
    bclat_d = din("c_bclat", [4, 2, 128], BF16)
    ones1_d = din("c_ones1", [1, 128], BF16)
    onesf_d = din("c_onesf", [128, 1])
    selH_d = din("c_selH", [8, 1])
    selA_d = din("c_selA", [128, 8])
    selC_d = din("c_selC", [128, 8])
    padm_d = din("c_padm", [128, 128], BF16)

    out_d = nc.dram_tensor("out", [D, M], F32, kind="ExternalOutput").ap()

    TAPSPECS = {
        "z0": ([128, NCH, TILE], BF16), "qT": ([128, NCH, TILE], BF16),
        "kT": ([128, NCH, TILE], BF16), "vtok": ([128, NCH, TILE], BF16),
        "localu": ([128, NCH, TILE], BF16), "lsum": ([16, TILE], F32),
        "mixed": ([128, NCH, TILE], BF16), "remote": ([128, NCH, TILE], BF16),
        "X1": ([128, NCH, TILE], F32), "X2": ([128, NCH, TILE], F32),
        "hft": ([16, 1], F32), "initv": ([16, 1], F32),
        "rkall": ([128, 2, 64], BF16), "rvlm": ([128, 256], BF16),
        "latn": ([128, 2, TILE], BF16), "grow": ([16, TILE], F32),
        "hsl": ([16, TILE], F32), "acum": ([16, TILE], F32),
        "X3": ([128, NCH, TILE], F32), "z1": ([128, NCH, TILE], BF16),
        "z2": ([128, NCH, TILE], BF16), "z3": ([128, NCH, TILE], BF16),
        "hffn": ([128, 32, TILE], BF16), "pooled": ([128, NCH, 128], BF16),
        "alpha1": ([128, 32], F32), "lam": ([16, TILE], F32),
        "bx": ([16, TILE], F32), "yv": ([128, NCH, TILE], BF16),
        "gate1": ([128, NCH, TILE], BF16), "rmsrow": ([1, TILE], F32),
        "latu": ([128, 2, TILE], BF16), "latsum": ([4, TILE], F32),
        "rqT": ([128, 2, TILE], BF16), "ktail": ([128, NCH, 128], BF16),
        "vtail": ([128, D], BF16), "hz": ([128, NCH, 128], BF16), "cx2": ([128, NCH, TILE], F32), "cz2": ([128, NCH, TILE], BF16), "chsl": ([16, TILE], F32),
    }
    tap_d = {tn: nc.dram_tensor("tap_" + tn, sh, dt, kind="ExternalOutput").ap()
             for tn, (sh, dt) in TAPSPECS.items() if tn in tapset}

    def scr(name, shape, dt=BF16):
        return nc.dram_tensor(name, shape, dt, kind="Internal").ap()

    qkvB = scr("qkvB", [D, 3 * D])
    rqB = scr("rqB", [D, 256])
    rkvB = scr("rkvB", [D, 512])
    routB = scr("routB", [256, D])
    outwB = scr("outwB", [D, D])
    w1sB = scr("w1sB", [D, 4096])
    w2sB = scr("w2sB", [4096, D])
    WgB = scr("WgB", [D, D])
    wgB = scr("wgB", [D, 3072])
    wuB = scr("wuB", [D, 3072])
    wdB = scr("wdB", [3072, D])
    x2d = scr("x2d", [D, M], F32)
    z2d = scr("z2d", [D, M], BF16)
    hsd = scr("hsd", [16, M], F32)
    acd = scr("acd", [16, M], F32)
    agkv_in = scr("agkv_in", [3, 128, 128], BF16)
    agkv_out = scr("agkv_out", [8, 3, 128, 128], BF16)
    agh_in = scr("agh_in", [16, 1], F32)
    agh_out = scr("agh_out", [128, 1], F32)

    stack = contextlib.ExitStack()
    with stack:
        def sbuf(name, shape, dt):
            return stack.enter_context(nc.sbuf_tensor(name, shape, dt))

        FA = sbuf("FA", [128, 10240], F32)
        BA = sbuf("BA", [128, 32768], BF16)

        def carve(arena, off, shape):
            n = 1
            for s_ in shape[1:]:
                n *= s_
            ap = arena[0:shape[0], off:off + n]
            if len(shape) == 3:
                ap = ap.rearrange("p (a b) -> p a b", a=shape[1])
            return ap

        X = carve(FA, 0, [128, NCH, TILE])
        sqs = [carve(FA, 4096 + i * TILE, [128, TILE]) for i in range(2)]
        srow = carve(FA, 5120, [1, 3, TILE])
        wtmp = [carve(FA, 0, [128, 4096]),
                carve(FA, 4096, [128, 4096])]
        lsum = carve(FA, 6656, [16, TILE])
        latsum = carve(FA, 7168, [4, TILE])
        grow = carve(FA, 7680, [16, TILE])
        gtmp = carve(FA, 8192, [16, TILE])
        lamrow = carve(FA, 8704, [16, TILE])
        bxrow = carve(FA, 9216, [16, TILE])
        zer16 = carve(FA, 9728, [16, TILE])
        FA2 = sbuf("FA2", [16, 2 * TILE], F32)
        hslb = FA2[0:16, 0:TILE]
        acumb = FA2[0:16, TILE:2 * TILE]

        z = carve(BA, 0, [128, NCH, TILE])
        qT = carve(BA, 4096, [128, NCH, TILE])
        kT = carve(BA, 8192, [128, NCH, TILE])
        vtok = carve(BA, 12288, [128, NCH, TILE])
        bag1 = carve(BA, 16384, [128, NCH, TILE])
        bag2 = carve(BA, 20480, [128, NCH, TILE])
        big1 = carve(BA, 4096, [128, 32, TILE])
        wsgn = [carve(BA, 4096, [128, 4096]),
                carve(BA, 8192, [128, 4096])]
        agkv_s = carve(BA, 12288, [128, 24, 128])
        rqTb = carve(BA, 24576, [128, 2, TILE])
        lexp = carve(BA, 25600, [128, 2, TILE])
        latu = carve(BA, 26624, [128, 2, TILE])
        latnb = carve(BA, 24576, [128, 2, TILE])
        etile = carve(BA, 27648, [128, 2, 256])
        hz = carve(BA, 28160, [128, NCH, 128])
        mtmp = carve(BA, 29184, [128, TILE])
        srowb = carve(BA, 29696, [1, 2, TILE])
        srowb_rms = carve(BA, 30720, [1, TILE])
        latsb = carve(BA, 31232, [4, TILE])
        ghb = carve(BA, 31744, [16, TILE])
        gm1b = carve(BA, 32256, [16, TILE])

        ident = sbuf("ident", [128, 128], BF16)
        mask01 = sbuf("mask01", [128, 256], BF16)
        halom = sbuf("halom", [128, 128], BF16)
        latm = sbuf("latm_s", [128, M], BF16)
        oh16 = sbuf("oh16", [128, 16, 16], BF16)
        ohlat = sbuf("ohlat", [128, 2, 4], BF16)
        bc2 = sbuf("bc2", [64, 8, 128], BF16)
        bclat = sbuf("bclat", [4, 2, 128], BF16)
        ones1 = sbuf("ones1", [1, 128], BF16)
        onesf = sbuf("onesf", [128, 1], F32)
        ones1b = sbuf("ones1b", [128, 1], BF16)
        selHs = sbuf("selHs", [8, 1], F32)
        selAs = sbuf("selAs", [128, 8], F32)
        selCs = sbuf("selCs", [128, 8], F32)
        padm = sbuf("padm", [128, 128], BF16)
        bqkv = sbuf("bqkv", [128, 24], F32)
        brq = sbuf("brq", [128, 2], F32)
        brkv = sbuf("brkv", [128, 4], F32)
        brout = sbuf("brout", [128, 8], F32)
        bout = sbuf("bout", [128, 8], F32)
        b1v = sbuf("b1v", [128, 32], F32)
        b2v = sbuf("b2v", [128, 8], F32)
        blam = sbuf("blam", [16, 1], F32)
        rmsw = sbuf("rmsw_s", [128, 8], F32)
        vb_bc = sbuf("vb_bc", [128, D], BF16)
        alpha1 = sbuf("alpha1", [128, 32], F32)
        alpha2 = sbuf("alpha2", [128, 8], F32)
        lamTw = sbuf("lamTw", [128, NCH, 16], BF16)
        wbTw = sbuf("wbTw", [128, NCH, 16], BF16)
        wcT = sbuf("wcT", [16, NCH, 128], BF16)
        glgr = sbuf("glgr", [128, 2, 8, 16], BF16)
        rkallT = sbuf("rkallT", [128, 2, 64], BF16)
        rvlm = sbuf("rvlm", [128, 256], BF16)
        smu0 = sbuf("smu0", [1, NTILE, TILE], BF16)
        sinv0 = sbuf("sinv0", [1, NTILE, TILE], BF16)
        hmu = sbuf("hmu", [1, 128], BF16)
        hinv = sbuf("hinv", [1, 128], BF16)
        ktail = sbuf("ktail", [128, NCH, 128], BF16)
        vtail = sbuf("vtail", [128, D], BF16)
        scarry = sbuf("scarry", [16, 1], F32)
        ccarry = sbuf("ccarry", [16, 1], F32)
        initv = sbuf("initv", [16, 1], F32)
        pooledT = sbuf("pooledT", [128, NCH, 128], BF16)
        glw_s = sbuf("glw_s", [64, 1], F32)
        grw_s = sbuf("grw_s", [64, 1], F32)
        rk_con = sbuf("rk_con", [128, 2, 64], BF16)
        rv_con = sbuf("rv_con", [128, 256], BF16)
        hsbf = sbuf("hsbf", [16, TILE], BF16)
        pooledF = sbuf("pooledF", [128, NCH, 32], F32)
        agh_s = sbuf("agh_s", [8, 16], F32)
        vbrow_s = sbuf("vbrow_s", [1, D], F32)
        vbrow_b = sbuf("vbrow_b", [1, D], BF16)
        rvrow_s = sbuf("rvrow_s", [1, 256], F32)
        rvrow_b = sbuf("rvrow_b", [1, 256], BF16)
        wlam_s = sbuf("wlam_s", [16, D], F32)
        wlam_b = sbuf("wlam_b", [16, D], BF16)
        wbm_s = sbuf("wbm_s", [16, D], F32)
        wbm_b = sbuf("wbm_b", [16, D], BF16)
        wc_s = sbuf("wc_s", [128, NCH, 16], F32)
        wc_b = sbuf("wc_b", [128, NCH, 16], BF16)
        wsl = [sbuf(f"wsl{i}", [128, 8192], BF16) for i in range(2)]

        def wslv(i, nkc, cols):
            return wsl[i][:, 0:nkc * cols].rearrange("p (a b) -> p a b", a=nkc)

        banks = [Bank(stack.enter_context(nc.psum_tensor(f"pb{i}", [128, 512], F32)))
                 for i in range(8)]

        block = stack.enter_context(nc.Block())
        qS = Sem(stack, nc, "qS")
        qG = Sem(stack, nc, "qG")
        qA = Sem(stack, nc, "qA")
        sPE = Sem(stack, nc, "sPE")
        sDVE = Sem(stack, nc, "sDVE")
        sACT = Sem(stack, nc, "sACT")
        sGP = Sem(stack, nc, "sGP")

        allsems = [qS, qG, qA, sPE, sDVE, sACT, sGP]

        def _serial_waits(e):
            if SERIAL[0] or 0 in JOINS[0]:
                for s_ in allsems:
                    if s_.n > 0:
                        e.wait_ge(s_.h, s_.n)

        def dmas(qname, pairs, waits=(), transpose=False):
            sec, sem = {"S": (block.sync, qS), "G": (block.gpsimd, qG),
                        "A": (block.scalar, qA)}[qname]
            box = {}

            def f(e):
                _serial_waits(e)
                for w in waits:
                    _wait(e, w)
                for out, in_ in pairs:
                    i = e.dma_start(out=out, in_=in_, transpose=transpose)
                    box["t"] = sem.inc(i, 16)
            sec(f)
            return box["t"]

        def dma(qname, out, in_, waits=(), transpose=False):
            return dmas(qname, [(out, in_)], waits=waits, transpose=transpose)

        def pe(ops, waits=()):
            box = {}

            def f(t):
                _serial_waits(t)
                for w in waits:
                    _wait(t, w)
                last = None
                for o in ops:
                    kw = {}
                    if o.get("tp") is not None:
                        kw["tile_position"] = o["tp"]
                    last = t.matmul(o["out"], o["lhsT"], o["rhs"],
                                    start=o.get("start", True),
                                    stop=o.get("stop", True), **kw)
                box["t"] = sPE.inc(last)
            block.tensor(f)
            return box["t"]

        def tr(out_ps, in_ap, waits=(), idn=128):
            return pe([{"out": out_ps, "lhsT": in_ap, "rhs": ident[0:idn, 0:idn]}],
                      waits=waits)

        def ve(fn, waits=()):
            box = {}

            def f(v):
                _serial_waits(v)
                for w in waits:
                    _wait(v, w)
                box["t"] = sDVE.inc(fn(v))
            block.vector(f)
            return box["t"]

        def ae(fn, waits=()):
            box = {}

            def f(a):
                _serial_waits(a)
                for w in waits:
                    _wait(a, w)
                box["t"] = sACT.inc(fn(a))
            block.scalar(f)
            return box["t"]

        def ge(fn, waits=()):
            box = {}

            def f(g):
                _serial_waits(g)
                for w in waits:
                    _wait(g, w)
                box["t"] = sGP.inc(fn(g))
            block.gpsimd(f)
            return box["t"]

        def tap(name, src_ap, waits=()):
            if name in tap_d:
                return dma("A", tap_d[name], src_ap, waits=waits)
            return None

        dummy = sbuf("dummy_jp", [1, 1], F32)

        def joinpoint():
            toks = [(s_, s_.n) for s_ in (qS, qG, qA, sPE, sDVE, sACT, sGP) if s_.n > 0]
            return ve(lambda v: v.memset(dummy[:, :], 0.0), waits=tuple(toks))

        def finish():
            def f(g):
                _wait(g, (qS, qS.n))
                _wait(g, (qA, qA.n))
                _wait(g, (qG, qG.n))
            block.gpsimd(f)
            return nc

        t_consts = dmas("S", [
            (ident[:], ident_d), (mask01[:], mask_d), (halom[:], halom_d),
            (latm[:], latm_d), (oh16[:], oh16_d), (ohlat[:], ohlat_d),
            (bc2[:], bc2_d), (bclat[:], bclat_d), (ones1[:], ones1_d),
            (onesf[:], onesf_d), (selHs[:], selH_d), (selAs[:], selA_d),
            (selCs[:], selC_d), (padm[:], padm_d), (bqkv[:], bqkv_d),
            (brq[:], brq_d), (brkv[:], brkv_d), (brout[:], brout_d),
            (bout[:], bout_d), (b1v[:], b1_d), (b2v[:], b2_d),
            (blam[:], blam_d), (rmsw[:], rmsw_d), (vbrow_s[:], vbrow_d),
            (rvrow_s[:], rvrow_d),
            (glw_s[:], glw_d), (grw_s[:], grw_d), (wlam_s[:], wlam_d),
            (wbm_s[:], wbm_d),
            (wc_s[:], wc_d.rearrange("(c p) n -> p c n", p=128)),
        ])

        t_qkvB = dma("G", qkvB, qkvw_d)
        t_rkvB = dma("G", rkvB, rkvw_d)
        t_rqB = dma("G", rqB, rqw_d)
        t_routB = dma("G", routB, routw_d)
        t_outwB = dma("G", outwB, outw_d)
        t_WgB = dma("G", WgB, wgate_d)
        t_wgB = dma("G", wgB, wgq_d)
        t_wuB = dma("G", wuB, wuq_d)
        t_wdB = dma("G", wdB, wdq_d)

        def prep_misc(v):
            v.memset(scarry[:], 0.0)
            v.memset(ccarry[:], 1.0)
            v.memset(ones1b[:], 1.0)
            v.memset(glgr[:], 0.0)
            v.tensor_copy(vbrow_b[:], vbrow_s[:])
            v.tensor_copy(rvrow_b[:], rvrow_s[:])
            v.tensor_copy(wlam_b[:], wlam_s[:])
            v.tensor_copy(wbm_b[:], wbm_s[:])
            last = v.tensor_copy(wc_b[:], wc_s[:])
            for c in range(8):
                v.tensor_copy(glgr[0:64, 0, c, 2 * c:2 * c + 1], glw_s[:, :])
                v.tensor_copy(glgr[64:128, 0, c, 2 * c + 1:2 * c + 2], glw_s[:, :])
                v.tensor_copy(glgr[0:64, 1, c, 2 * c:2 * c + 1], grw_s[:, :])
                last = v.tensor_copy(glgr[64:128, 1, c, 2 * c + 1:2 * c + 2], grw_s[:, :])
            return last
        t_misc = ve(prep_misc, waits=(t_consts,))

        tbv = pe([{"out": banks[0].ps[:, 0:512], "lhsT": ones1[:, :], "rhs": vbrow_b[:, 0:512]},
                  {"out": banks[1].ps[:, 0:512], "lhsT": ones1[:, :], "rhs": vbrow_b[:, 512:1024]}],
                 waits=(t_misc,))

        def f_vb(v):
            v.tensor_copy(vb_bc[:, 0:512], banks[0].ps[:, 0:512])
            return v.tensor_copy(vb_bc[:, 512:1024], banks[1].ps[:, 0:512])
        t_vb = ve(f_vb, waits=(tbv,))
        banks[0].free = t_vb
        banks[1].free = t_vb

        for cch in range(NCH):
            bk = banks[cch % 2]
            tr(bk.ps[:, 0:16], wlam_b[:, cch * 128:(cch + 1) * 128],
               waits=(t_misc, bk.free), idn=16)
            tb_ = tr(bk.ps[:, 16:32], wbm_b[:, cch * 128:(cch + 1) * 128], idn=16)

            def f(v, bk=bk, cch=cch):
                v.tensor_copy(lamTw[:, cch, :], bk.ps[:, 0:16])
                return v.tensor_copy(wbTw[:, cch, :], bk.ps[:, 16:32])
            bk.free = ve(f, waits=(tb_,))
        for cch in range(NCH):
            bk = banks[cch % 2]
            ta = tr(bk.ps[0:16, 0:128], wc_b[:, cch, :], waits=(t_misc, bk.free))
            bk.free = ve(lambda v, bk=bk, cch=cch:
                         v.tensor_copy(wcT[:, cch, :], bk.ps[0:16, 0:128]),
                         waits=(ta,))

        wtok = [None, None]
        rtok = [None, None]
        for it in range(32):
            pb = it % 2
            t_l = dma("S", wtmp[pb][:, 0:1024], w1_d[it * 128:(it + 1) * 128, :],
                      waits=tuple(w for w in (rtok[pb],) if w))
            rtok[pb] = ve(lambda v, pb=pb, it=it: v.reduce_sum(
                alpha1[:, it:it + 1], wtmp[pb][:, 0:1024],
                axis=AX.X, apply_absolute_value=True), waits=(t_l,))
        for it in range(8):
            pb = it % 2
            t_l = dma("S", wtmp[pb][:, :], w2_d[it * 128:(it + 1) * 128, :],
                      waits=tuple(w for w in (rtok[pb],) if w))
            rtok[pb] = ve(lambda v, pb=pb, it=it: v.reduce_sum(
                alpha2[:, it:it + 1], wtmp[pb][:, :],
                axis=AX.X, apply_absolute_value=True), waits=(t_l,))
        for it in range(8):
            pb = it % 2
            t_l = dma("S", wtmp[pb][:, :], w1t_d[it * 128:(it + 1) * 128, :],
                      waits=tuple(w for w in (wtok[pb], rtok[pb]) if w))
            rtok[pb] = None
            t_s = ae(lambda a, pb=pb: a.activation(out=wsgn[pb][:, :], in_=wtmp[pb][:, :],
                                                   func=AF.Sign),
                     waits=(t_l,))
            wtok[pb] = dma("G", w1sB[it * 128:(it + 1) * 128, :], wsgn[pb][:, :],
                           waits=(t_s,))
        wtok2 = [None, None]
        for it in range(32):
            pb = it % 2
            t_l = dma("S", wtmp[pb][:, 0:1024], w2t_d[it * 128:(it + 1) * 128, :],
                      waits=tuple(w for w in (wtok2[pb] or wtok[pb],) if w))
            t_s = ae(lambda a, pb=pb: a.activation(out=wsgn[pb][:, 0:1024],
                                                   in_=wtmp[pb][:, 0:1024], func=AF.Sign),
                     waits=(t_l,))
            wtok2[pb] = dma("G", w2sB[it * 128:(it + 1) * 128, :], wsgn[pb][:, 0:1024],
                            waits=(t_s,))

        def f_alpha(v):
            v.memset(zer16[:, :], 0.0)
            v.tensor_scalar_mul(alpha1[:], alpha1[:], 1.0 / 1024.0)
            return v.tensor_scalar_mul(alpha2[:], alpha2[:], 1.0 / 4096.0)
        t_alpha = ve(f_alpha)
        t_signs = _latest(*[t for t in (*wtok, *wtok2, t_alpha, *rtok) if t])
        tap("alpha1", alpha1[:], waits=(t_alpha,))

        def ln_stats(zget, T_, mu_out, inv_out, eps, extra=(), srcf32=True):
            b6, b7 = banks[6], banks[7]
            onesz = onesf if srcf32 else ones1b
            ops = []
            for c in range(NCH):
                ops.append({"out": b6.ps[0:1, 0:T_], "lhsT": onesz[:, :],
                            "rhs": zget(c)[:, 0:T_], "start": c == 0, "stop": c == NCH - 1})
            tsum = pe(ops, waits=(b6.free,) + tuple(extra))
            tsq_mm = None
            prev = [None, None]
            for c in range(NCH):
                pb = c % 2
                wl = [prev[pb]] + (list(extra) if c < 2 else [])
                t_sq = ae(lambda a, c=c, pb=pb: a.activation(
                    out=sqs[pb][:, 0:T_], in_=zget(c)[:, 0:T_], func=AF.Square),
                    waits=tuple(w for w in wl if w))
                tsq_mm = pe([{"out": b7.ps[0:1, 0:T_], "lhsT": onesf[:, :],
                              "rhs": sqs[pb][:, 0:T_], "start": c == 0, "stop": c == NCH - 1}],
                            waits=(t_sq, b7.free) if c == 0 else (t_sq,))
                prev[pb] = tsq_mm

            def fin(v):
                v.tensor_scalar_mul(srow[:, 0, 0:T_], b6.ps[0:1, 0:T_], 1.0 / D)
                v.tensor_scalar_mul(srow[:, 1, 0:T_], b7.ps[0:1, 0:T_], 1.0 / D)
                v.tensor_mul(srow[:, 2, 0:T_], srow[:, 0, 0:T_], srow[:, 0, 0:T_])
                v.tensor_sub(srow[:, 1, 0:T_], srow[:, 1, 0:T_], srow[:, 2, 0:T_])
                return v.tensor_scalar_add(srow[:, 1, 0:T_], srow[:, 1, 0:T_], float(eps))
            t_fin = ve(fin, waits=(tsum, tsq_mm))
            t_sq2 = ae(lambda a: a.activation(out=srow[:, 2, 0:T_], in_=srow[:, 1, 0:T_],
                                              func=AF.Sqrt),
                       waits=(t_fin,))

            def fin2(v):
                v.reciprocal(srow[:, 2, 0:T_], srow[:, 2, 0:T_])
                v.tensor_copy(mu_out, srow[:, 0, 0:T_])
                return v.tensor_copy(inv_out, srow[:, 2, 0:T_])
            t2 = ve(fin2, waits=tuple(w for w in (t_sq2, srowb_busy[0]) if w))
            b6.free = t2
            b7.free = t2
            return t2

        srowb_busy = [None]

        def ln_bcast(mu_row, inv_row, T_, extra=()):
            bkA, bkB = banks[4], banks[5]
            t = pe([{"out": bkA.ps[:, 0:T_], "lhsT": ones1[:, :], "rhs": mu_row},
                    {"out": bkB.ps[:, 0:T_], "lhsT": ones1[:, :], "rhs": inv_row}],
                   waits=tuple(w for w in (bkA.free, bkB.free) + tuple(extra) if w))
            srowb_busy[0] = t
            return t

        def ln_apply(xget, zget, T_, t_bc):
            bkA, bkB = banks[4], banks[5]
            tk = None
            for c in range(NCH):
                def f(v, c=c):
                    v.tensor_sub(zget(c)[:, 0:T_], xget(c)[:, 0:T_], bkA.ps[:, 0:T_])
                    return v.tensor_mul(zget(c)[:, 0:T_], zget(c)[:, 0:T_], bkB.ps[:, 0:T_])
                tk = ve(f, waits=(t_bc,) if c == 0 else ())
            bkA.free = tk
            bkB.free = tk
            return tk

        def full_ln(xget, zget, T_, eps, extra=(), srcf32=True,
                    mu_out=None, inv_out=None, stats=True):
            mu_out = srowb[:, 0, 0:T_] if mu_out is None else mu_out
            inv_out = srowb[:, 1, 0:T_] if inv_out is None else inv_out
            if stats:
                t_st = ln_stats(xget, T_, mu_out, inv_out, eps, extra=extra,
                                srcf32=srcf32)
                t_bc = ln_bcast(mu_out, inv_out, T_, extra=(t_st,))
            else:
                t_bc = ln_bcast(mu_out, inv_out, T_, extra=_latest(*extra))
            return ln_apply(xget, zget, T_, t_bc)

        t_pool = None
        t_z0 = None
        t_ap = None
        for tt in list(range(1, n_tiles)) + [0]:
            t_x = dma("S", X[:],
                      x_d.rearrange("(c p) n -> p c n", p=128)[:, :, tt * TILE:(tt + 1) * TILE],
                      waits=t_signs + ((t_ap,) if t_ap else ()))
            t_ap = full_ln(lambda c: X[:, c, :], lambda c: z[:, c, :], TILE, 1e-5,
                           extra=(t_x,) + t_signs, mu_out=smu0[:, tt, :],
                           inv_out=sinv0[:, tt, :])
            for c in range(NCH):
                t_pool = ve(lambda v, c=c, tt=tt: v.reduce_sum(
                    pooledF[:, c, tt * 4:(tt + 1) * 4],
                    z[:, c, :].rearrange("p (g w) -> p g w", w=128), axis=AX.X),
                    waits=(t_ap,) if c == 0 else ())
            if tt == 0:
                t_z0 = t_ap
        tap("z0", z[:], waits=(t_z0,))

        t_xh = dma("G", hz[:], xh_d.rearrange("(c p) n -> p c n", p=128),
                   waits=t_signs)
        t_hst = ln_stats(lambda c: hz[:, c, :], 128, hmu[:, :], hinv[:, :], 1e-5,
                         extra=(t_xh,), srcf32=False)
        t_hbc = ln_bcast(hmu[:, :], hinv[:, :], 128, extra=(t_hst,))
        t_hz = ln_apply(lambda c: hz[:, c, :], lambda c: hz[:, c, :], 128, t_hbc)
        tap("hz", hz[:], waits=(t_hz,))

        def f_pad(v):
            last = None
            for c in range(NCH):
                for reg in range(4):
                    v.tensor_copy(pooledT[:, c, reg * 32:(reg + 1) * 32], pooledF[:, c, :])
                last = v.tensor_mul(pooledT[:, c, :], pooledT[:, c, :], padm[:, :])
            return last
        t_pad = ve(f_pad, waits=(t_pool, t_misc))
        tap("pooled", pooledT[:], waits=(t_pad,))

        t_rkvl = dma("S", wslv(0, 8, 512)[:, :, :],
                     rkvB.rearrange("(c p) n -> p c n", p=128), waits=(t_rkvB,))
        wsl_busy = [None, None]
        for oc in range(2):
            bk = banks[oc]
            ops = [{"out": bk.ps[:, 0:64], "lhsT": wslv(0, 8, 512)[:, kc, oc * 128:(oc + 1) * 128],
                    "rhs": pooledT[:, kc, 0:64], "start": kc == 0, "stop": kc == NCH - 1}
                   for kc in range(NCH)]
            t_mm = pe(ops, waits=(t_rkvl, t_pad, bk.free))
            bk.free = ve(lambda v, bk=bk, oc=oc: v.tensor_copy(rk_con[:, oc, :], bk.ps[:, 0:64]),
                         waits=(t_mm,))
        bkv = banks[2]
        ops = [{"out": bkv.ps[:, 0:256], "lhsT": pooledT[:, kc, :],
                "rhs": wslv(0, 8, 512)[:, kc, 256:512], "start": kc == 0, "stop": kc == NCH - 1}
               for kc in range(NCH)]
        t_rvmm = pe(ops, waits=(t_rkvl, t_pad, bkv.free))
        wsl_busy[0] = t_rvmm
        bkv.free = ve(lambda v: v.tensor_copy(rv_con[:, :], bkv.ps[:, 0:256]),
                      waits=(t_rvmm,))

        t_agw = dmas("G", [(agkv_in[0], rk_con[:].rearrange("p a b -> p (a b)")),
                           (agkv_in[1], rv_con[:, 0:128]),
                           (agkv_in[2], rv_con[:, 128:256])],
                     waits=(banks[0].free, banks[1].free, bkv.free))
        box1 = {}

        def f_cc1(g):
            _wait(g, t_agw)
            i = g.collective_compute("AllGather", ALU.bypass,
                                     replica_groups=[[0, 1, 2, 3, 4, 5, 6, 7]],
                                     ins=[agkv_in], outs=[agkv_out])
            box1["t"] = sGP.inc(i)
        block.gpsimd(f_cc1)
        t_agr = dmas("S", [(agkv_s[:, s_ * 8:(s_ + 1) * 8, :],
                            agkv_out[:, s_, :, :].rearrange("r p n -> p r n"))
                           for s_ in range(3)],
                     waits=(box1["t"],))

        def f_comb(v):
            last = None
            for s_ in range(3):
                dst = (rkallT[:].rearrange("p a b -> p (a b)") if s_ == 0
                       else rvlm[:, (s_ - 1) * 128:s_ * 128])
                v.tensor_scalar(dst, agkv_s[:, s_ * 8, :], selAs[:, 0:1], None, ALU.mult)
                for r in range(1, 8):
                    v.scalar_tensor_tensor(dst, agkv_s[:, s_ * 8 + r, :], selAs[:, r:r + 1],
                                           dst, ALU.mult, ALU.add)
                for r in range(8):
                    last = v.scalar_tensor_tensor(dst, agkv_s[:, s_ * 8 + r, :], selCs[:, r:r + 1],
                                                  dst, ALU.mult, ALU.add)
            return last
        t_comb = ve(f_comb, waits=(t_agr,))

        t_rvbb = pe([{"out": banks[3].ps[:, 0:256], "lhsT": ones1[:, :],
                      "rhs": rvrow_b[:, 0:256]}], waits=(banks[3].free, t_misc))

        def f_bias(v):
            v.tensor_scalar(rkallT[:, 0, :], rkallT[:, 0, :], brkv[:, 0:1], None, ALU.add)
            v.tensor_scalar(rkallT[:, 1, :], rkallT[:, 1, :], brkv[:, 1:2], None, ALU.add)
            return v.tensor_add(rvlm[:, :], rvlm[:, :], banks[3].ps[:, 0:256])
        t_rkrv = ve(f_bias, waits=(t_comb, t_rvbb))
        banks[3].free = t_rkrv
        tap("rkall", rkallT[:], waits=(t_rkrv,))
        tap("rvlm", rvlm[:], waits=(t_rkrv,))

        if upto == "prep":
            return finish()

        wsl_busy = [None, None]

        def proj(dramB, K, NOUT, rhs_get, evac, transposed=False, t_src=(),
                 slab_cols=512):
            nkc = K // 128
            nslab = NOUT // slab_cols
            socc = slab_cols // 128
            slab_tok = [None, None]
            mm_tok = [wsl_busy[0], wsl_busy[1]]
            for j in range(nslab + 1):
                pb = j % 2
                if j < nslab:
                    w0 = tuple(w for w in (slab_tok[pb], mm_tok[pb]) if w)
                    w0 = w0 + (tuple(t_src) if j == 0 else ())
                    if not transposed:
                        slab_tok[pb] = dma(
                            "S", wslv(pb, nkc, slab_cols)[:, :, :],
                            dramB.rearrange("(c p) n -> p c n", p=128)[
                                :, :, j * slab_cols:(j + 1) * slab_cols],
                            waits=w0)
                    else:
                        prs = [(wslv(pb, nkc, slab_cols)[:, kc, :],
                                dramB[j * slab_cols:(j + 1) * slab_cols,
                                      kc * 128:(kc + 1) * 128])
                               for kc in range(nkc)]
                        slab_tok[pb] = dmas("S", prs, transpose=True, waits=w0)
                if j == 0:
                    continue
                jj = j - 1
                pbc = jj % 2
                for oc in range(socc):
                    og = jj * socc + oc
                    bk = banks[og % 2]
                    ops = [{"out": bk.ps[:, 0:TILE],
                            "lhsT": wslv(pbc, nkc, slab_cols)[:, kc, oc * 128:(oc + 1) * 128],
                            "rhs": rhs_get(kc),
                            "start": kc == 0, "stop": kc == nkc - 1}
                           for kc in range(nkc)]
                    t_mm = pe(ops, waits=tuple(w for w in (slab_tok[pbc], bk.free) if w))
                    mm_tok[pbc] = t_mm
                    bk.free = evac(bk, og, t_mm)
            wsl_busy[0] = mm_tok[0]
            wsl_busy[1] = mm_tok[1]
            return _later(banks[0].free, banks[1].free)

        def z_get(buf):
            return lambda kc: buf[:, kc, :]

        t_x2w = None
        t_z2w = None
        t_scan = None
        t_tails = None
        t_hsw = None
        for tt in range(n_tiles):
            first = tt == 0
            if not first:
                t_x = dma("S", X[:],
                          x_d.rearrange("(c p) n -> p c n", p=128)[:, :, tt * TILE:(tt + 1) * TILE],
                          waits=(t_x2w,) if t_x2w else ())
                t_z = full_ln(lambda c: X[:, c, :], lambda c: z[:, c, :], TILE, 1e-5,
                              extra=(t_x, t_scan, t_z2w), mu_out=smu0[:, tt, :],
                              inv_out=sinv0[:, tt, :], stats=False)
            else:
                t_z = t_z0

            def ev_qk(bk, og, tok):
                dstb = qT if og < 8 else kT
                c = og % 8

                def f(v):
                    return v.tensor_scalar(dstb[:, c, :], bk.ps[:, 0:TILE],
                                           bqkv[:, og:og + 1], None, ALU.add)
                return ve(f, waits=(tok,))
            proj(qkvB, D, 2048, z_get(z), ev_qk, t_src=(t_qkvB, t_z, t_comb))

            t_vw0 = dma("S", wslv(0, 8, 512)[:, :, :],
                        qkvB.rearrange("(c p) n -> p c n", p=128)[:, :, 2048:2560],
                        waits=tuple(w for w in (wsl_busy[0],) if w))
            t_vw1 = dma("S", wslv(1, 8, 512)[:, :, :],
                        qkvB.rearrange("(c p) n -> p c n", p=128)[:, :, 2560:3072],
                        waits=tuple(w for w in (wsl_busy[1],) if w))
            for tb in range(NM):
                for vf in range(2):
                    bk = banks[(tb * 2 + vf) % 2]
                    ops = [{"out": bk.ps[:, 0:512],
                            "lhsT": z[:, kc, tb * 128:(tb + 1) * 128],
                            "rhs": wslv(vf, 8, 512)[:, kc, :],
                            "start": kc == 0, "stop": kc == NCH - 1}
                           for kc in range(NCH)]
                    t_mm = pe(ops, waits=((t_vw0, t_vw1)[vf], bk.free)
                              + ((t_z,) if tb == 0 else ()))
                    wsl_busy[vf] = t_mm

                    def f(v, bk=bk, tb=tb, vf=vf):
                        return v.tensor_add(vtok[:, 2 * tb + vf, :],
                                            bk.ps[:, 0:512], vb_bc[:, vf * 512:(vf + 1) * 512])
                    bk.free = ve(f, waits=(t_mm, t_vb))
            if first:
                t_hk0 = dma("S", wslv(0, 8, 512)[:, :, :],
                            qkvB.rearrange("(c p) n -> p c n", p=128)[:, :, 1024:1536],
                            waits=tuple(w for w in (wsl_busy[0],) if w))
                t_hk1 = dma("S", wslv(1, 8, 512)[:, :, :],
                            qkvB.rearrange("(c p) n -> p c n", p=128)[:, :, 1536:2048],
                            waits=tuple(w for w in (wsl_busy[1],) if w))
                for c in range(NCH):
                    bk = banks[2 + c % 2]
                    vvi = 0 if c < 4 else 1
                    ops = [{"out": bk.ps[:, 0:128],
                            "lhsT": wslv(vvi, 8, 512)[:, kc, (c % 4) * 128:(c % 4 + 1) * 128],
                            "rhs": hz[:, kc, :], "start": kc == 0, "stop": kc == NCH - 1}
                           for kc in range(NCH)]
                    t_mm = pe(ops, waits=((t_hk0, t_hk1)[vvi], t_hz, bk.free))
                    wsl_busy[vvi] = t_mm

                    def f(v, bk=bk, c=c):
                        return v.tensor_scalar(ktail[:, c, :], bk.ps[:, 0:128],
                                               bqkv[:, 8 + c:9 + c], None, ALU.add)
                    bk.free = ve(f, waits=(t_mm,))
                t_hv0 = dma("S", wslv(0, 8, 512)[:, :, :],
                            qkvB.rearrange("(c p) n -> p c n", p=128)[:, :, 2048:2560],
                            waits=tuple(w for w in (wsl_busy[0],) if w))
                t_hv1 = dma("S", wslv(1, 8, 512)[:, :, :],
                            qkvB.rearrange("(c p) n -> p c n", p=128)[:, :, 2560:3072],
                            waits=tuple(w for w in (wsl_busy[1],) if w))
                for vf in range(2):
                    bk = banks[4 + vf]
                    ops = [{"out": bk.ps[:, 0:512], "lhsT": hz[:, kc, :],
                            "rhs": wslv(vf, 8, 512)[:, kc, :],
                            "start": kc == 0, "stop": kc == NCH - 1}
                           for kc in range(NCH)]
                    t_mm = pe(ops, waits=((t_hv0, t_hv1)[vf], t_hz, bk.free))
                    wsl_busy[vf] = t_mm

                    def f(v, bk=bk, vf=vf):
                        return v.tensor_add(vtail[:, vf * 512:(vf + 1) * 512],
                                            bk.ps[:, 0:512], vb_bc[:, vf * 512:(vf + 1) * 512])
                    bk.free = ve(f, waits=(t_mm, t_vb))
                t_tails = _later(banks[2].free, banks[3].free, banks[4].free, banks[5].free)
            t_qkv_done = _later(*[b.free for b in banks])
            tap("qT", qT[:], waits=(t_qkv_done,))
            tap("kT", kT[:], waits=(t_qkv_done,))
            tap("vtok", vtok[:], waits=(t_qkv_done,))
            tap("ktail", ktail[:], waits=(t_tails,))
            tap("vtail", vtail[:], waits=(t_tails,))
            if upto == "qkv":
                return finish()

            sgrp_open = False
            t_lsum = None
            t_av_last = None
            for cpair in range(8):
                for s in range(-1, NM):
                    q0 = max(0, s * 128)
                    q1 = min(TILE, s * 128 + 256)
                    wq = q1 - q0
                    joff = q0 - s * 128
                    t_e = [None, None]
                    for hh in range(2):
                        hb = 64 * hh
                        bk = banks[2 + hh]
                        ksrc = (ktail[hb:hb + 64, cpair, 0:128] if s < 0
                                else kT[hb:hb + 64, cpair, s * 128:(s + 1) * 128])
                        wl = [bk.free]
                        if cpair == 0 and s == -1:
                            wl += [t_qkv_done, t_tails]
                        t_sc = pe([{"out": bk.ps[0:128, 0:wq], "lhsT": ksrc,
                                    "rhs": qT[hb:hb + 64, cpair, q0:q1]}],
                                  waits=tuple(w for w in wl if w))
                        t_e[hh] = ae(lambda a, bk=bk, hh=hh, wq=wq: a.activation(
                            out=etile[:, hh, 0:wq], in_=bk.ps[0:128, 0:wq],
                            func=AF.Exp, scale=0.125),
                            waits=tuple(w for w in (t_sc, t_av_last) if w))
                        bk.free = t_e[hh]
                    mref = halom[:, 0:128] if (first and s < 0) else mask01[:, joff:joff + wq]

                    def fm(g, mref=mref, wq=wq):
                        g.tensor_mul(etile[:, 0, 0:wq], etile[:, 0, 0:wq], mref)
                        return g.tensor_mul(etile[:, 1, 0:wq], etile[:, 1, 0:wq], mref)
                    t_m = ge(fm, waits=(t_e[0], t_e[1]))
                    for mpart in (s, s + 1):
                        if mpart < 0 or mpart >= NM:
                            continue
                        mo = mpart * 128
                        bkL = banks[4]
                        bkS = banks[6]
                        eq0 = mpart * 128 - q0
                        opening = mpart == s + 1
                        closing = mpart == s
                        ops = []
                        wl = [t_m]
                        for hh in range(2):
                            h = 2 * cpair + hh
                            vs = (vtail[:, h * 64:(h + 1) * 64] if s < 0
                                  else vtok[:, 2 * s + (h * 64) // 512,
                                            (h * 64) % 512:(h * 64) % 512 + 64])
                            if opening and hh == 0 and mpart == 0:
                                wl.append(bkL.free)
                            ops.append({"out": bkL.ps[64 * hh:64 * (hh + 1), mo:mo + 128],
                                        "lhsT": vs, "rhs": etile[:, hh, eq0:eq0 + 128],
                                        "start": bool(opening),
                                        "stop": bool(closing),
                                        "tp": (0, 64) if hh == 1 else None})
                            s_open = bool(opening and hh == 0 and not sgrp_open)
                            s_close = bool(closing and hh == 1 and cpair == 7
                                           and mpart == NM - 1)
                            if s_open:
                                wl.append(bkS.free)
                                sgrp_open = True
                            ops.append({"out": bkS.ps[0:16, mo:mo + 128],
                                        "lhsT": oh16[:, h, :],
                                        "rhs": etile[:, hh, eq0:eq0 + 128],
                                        "start": s_open, "stop": s_close})
                        t_av = pe(ops, waits=tuple(w for w in wl if w))
                        t_av_last = t_av
                        if closing and mpart == NM - 1:
                            def fev(v, cpair=cpair):
                                return v.tensor_copy(bag1[:, cpair, :], banks[4].ps[:, 0:512])
                            banks[4].free = ve(fev, waits=(t_av,))
                        if closing and cpair == 7 and mpart == NM - 1:
                            def fsv(v):
                                return v.tensor_copy(lsum[:, :], banks[6].ps[0:16, 0:512])
                            t_lsum = ve(fsv, waits=(t_av,))
                            banks[6].free = t_lsum
            t_localu = banks[4].free
            tap("localu", bag1[:], waits=(t_localu,))
            tap("lsum", lsum[:], waits=(t_lsum,))

            def f_lr(v):
                v.tensor_scalar_max(lsum[:, :], lsum[:, :], 1e-30)
                return v.reciprocal(lsum[:, :], lsum[:, :])
            t_lrec = ve(f_lr, waits=(t_lsum,))
            if tt < n_tiles - 1:
                def f_tails(v):
                    for c in range(NCH):
                        v.tensor_copy(ktail[:, c, :], kT[:, c, TILE - 128:TILE])
                    v.tensor_copy(vtail[:, 0:512], vtok[:, 2 * (NM - 1), :])
                    return v.tensor_copy(vtail[:, 512:1024], vtok[:, 2 * (NM - 1) + 1, :])
                t_tails = ve(f_tails, waits=(t_qkv_done, t_localu))
            if upto == "local":
                return finish()

            def ev_rq(bk, og, tok):
                def f(v):
                    return v.tensor_scalar(rqTb[:, og, :], bk.ps[:, 0:TILE],
                                           brq[:, og:og + 1], None, ALU.add)
                return ve(f, waits=(tok,))
            proj(rqB, D, 256, z_get(z), ev_rq, t_src=(t_rqB,), slab_cols=256)
            t_rq = _later(banks[0].free, banks[1].free)
            tap("rqT", rqTb[:], waits=(t_rq,))

            lat_last = None
            lsum_lat = None
            for cpair in range(2):
                bk = banks[2 + cpair]
                ops = []
                for hh in range(2):
                    hb = 64 * hh
                    ops.append({"out": bk.ps[hb:hb + 64, 0:512],
                                "lhsT": rkallT[hb:hb + 64, cpair, :],
                                "rhs": rqTb[hb:hb + 64, cpair, :],
                                "tp": (64, 64) if hh == 1 else None})
                wl = [bk.free] + ([t_rq, t_rkrv] if cpair == 0 else [])
                t_ls = pe(ops, waits=tuple(w for w in wl if w))
                t_le = ae(lambda a, bk=bk, cpair=cpair: a.activation(
                    out=lexp[:, cpair, :], in_=bk.ps[:, 0:512],
                    func=AF.Exp, scale=0.125), waits=(t_ls,))
                bk.free = t_le

                def fm(g, cpair=cpair, tt=tt):
                    return g.tensor_mul(lexp[:, cpair, :], lexp[:, cpair, :],
                                        latm[:, tt * TILE:(tt + 1) * TILE])
                t_lm = ge(fm, waits=(t_le,))
                bkL = banks[4] if cpair == 0 else banks[5]
                ops = []
                for hh in range(2):
                    hb = 64 * hh
                    lh = 2 * cpair + hh
                    ops.append({"out": bkL.ps[hb:hb + 64, 0:512],
                                "lhsT": rvlm[hb:hb + 64, lh * 64:(lh + 1) * 64],
                                "rhs": lexp[hb:hb + 64, cpair, :],
                                "tp": (64, 64) if hh == 1 else None})
                ops.append({"out": banks[6].ps[0:4, 0:512],
                            "lhsT": ohlat[:, cpair, :],
                            "rhs": lexp[:, cpair, :],
                            "start": cpair == 0, "stop": cpair == 1})
                wl = [t_lm, bkL.free] + ([banks[6].free] if cpair == 0 else [])
                t_lv = pe(ops, waits=tuple(w for w in wl if w))

                def flv(v, bkL=bkL, cpair=cpair):
                    return v.tensor_copy(latu[:, cpair, :], bkL.ps[:, 0:512])
                bkL.free = ve(flv, waits=(t_lv,))
                lat_last = _later(lat_last, bkL.free)
                if cpair == 1:
                    def fls(v):
                        return v.tensor_copy(latsum[:, :], banks[6].ps[0:4, 0:512])
                    lsum_lat = ve(fls, waits=(t_lv,))
                    banks[6].free = lsum_lat
            tap("latu", latu[:], waits=(lat_last,))
            tap("latsum", latsum[:], waits=(lsum_lat,))

            def f_latrec(v):
                v.tensor_scalar_max(latsum[:, :], latsum[:, :], 1e-30)
                v.reciprocal(latsum[:, :], latsum[:, :])
                return v.tensor_copy(latsb[:, :], latsum[:, :])
            t_latrec = ve(f_latrec, waits=(lsum_lat,))

            for cpair in range(2):
                bk = banks[2 + cpair]
                t_bc = pe([{"out": bk.ps[:, 0:512], "lhsT": bclat[:, cpair, :],
                            "rhs": latsb[:, :]}],
                          waits=(t_latrec, bk.free))

                def f(v, bk=bk, cpair=cpair):
                    return v.tensor_mul(latnb[:, cpair, :], latu[:, cpair, :],
                                        bk.ps[:, 0:512])
                bk.free = ve(f, waits=(t_bc, lat_last))
            t_latn = _later(banks[2].free, banks[3].free)
            tap("latn", latnb[:], waits=(t_latn,))

            t_rol = dma("S", wslv(0, 2, 1024)[:, :, :],
                        routB.rearrange("(c p) n -> p c n", p=128),
                        waits=tuple(w for w in (t_routB, wsl_busy[0]) if w))
            for og in range(8):
                bk = banks[og % 2]
                ops = [{"out": bk.ps[:, 0:TILE],
                        "lhsT": wslv(0, 2, 1024)[:, kc, og * 128:(og + 1) * 128],
                        "rhs": latnb[:, kc, :],
                        "start": kc == 0, "stop": kc == 1}
                       for kc in range(2)]
                wl = [t_rol, bk.free] + ([t_latn] if og == 0 else [])
                t_mm = pe(ops, waits=tuple(w for w in wl if w))
                wsl_busy[0] = t_mm

                def f(v, bk=bk, og=og):
                    return v.tensor_scalar(bag2[:, og, :], bk.ps[:, 0:TILE],
                                           brout[:, og:og + 1], None, ALU.add)
                bk.free = ve(f, waits=(t_mm,))
            t_remote = _later(banks[0].free, banks[1].free)
            tap("remote", bag2[:], waits=(t_remote,))

            bkl, bkr = banks[2], banks[3]
            opsl = [{"out": bkl.ps[0:16, 0:512], "lhsT": glgr[:, 0, c, :],
                     "rhs": bag1[:, c, :],
                     "start": c == 0, "stop": c == NCH - 1} for c in range(NCH)]
            opsr = [{"out": bkr.ps[0:16, 0:512], "lhsT": glgr[:, 1, c, :],
                     "rhs": bag2[:, c, :],
                     "start": c == 0, "stop": c == NCH - 1} for c in range(NCH)]
            t_gl = pe(opsl, waits=(bkl.free, t_localu))
            t_gr = pe(opsr, waits=(bkr.free, t_remote))

            def f_g(v):
                v.tensor_mul(gtmp[:, :], bkl.ps[0:16, 0:512], lsum[:, :])
                return v.tensor_add(gtmp[:, :], gtmp[:, :], bkr.ps[0:16, 0:512])
            t_gd = ve(f_g, waits=(t_gl, t_gr, t_lrec))
            t_sig = ae(lambda a: a.activation(out=grow[0:16, :], in_=gtmp[:, :],
                                              func=AF.Sigmoid), waits=(t_gd,))

            def f_g2(v):
                v.tensor_mul(ghb[:, :], grow[0:16, :], lsum[:, :])
                return v.tensor_scalar(gm1b[:, :], grow[0:16, :],
                                       -1.0, 1.0, ALU.mult, ALU.add)
            t_grow = ve(f_g2, waits=(t_sig,))
            bkl.free = t_grow
            bkr.free = t_grow
            tap("grow", grow[:], waits=(t_grow,))

            t_mix = None
            for cpair in range(8):
                bka = banks[2 + cpair % 2]
                bkb = banks[4 + cpair % 2]
                wl = [bka.free] + ([t_grow] if cpair == 0 else [])
                t_b1 = pe([{"out": bka.ps[:, 0:512], "lhsT": bc2[0:16, cpair, :],
                            "rhs": ghb[:, :]}],
                          waits=tuple(w for w in wl if w))
                t_b2 = pe([{"out": bkb.ps[:, 0:512], "lhsT": bc2[0:16, cpair, :],
                            "rhs": gm1b[:, :]}],
                          waits=(bkb.free,) if bkb.free else ())

                def f(v, bka=bka, bkb=bkb, cpair=cpair):
                    v.tensor_mul(z[:, cpair, :], bag1[:, cpair, :], bka.ps[:, 0:512])
                    v.tensor_mul(mtmp[:, :], bag2[:, cpair, :], bkb.ps[:, 0:512])
                    return v.tensor_add(z[:, cpair, :], z[:, cpair, :], mtmp[:, :])
                tk = ve(f, waits=(t_b1, t_b2))
                bka.free = tk
                bkb.free = tk
                t_mix = tk
            tap("mixed", z[:], waits=(t_mix,))
            if upto == "mix":
                return finish()

            def ev_out(bk, og, tok):
                def f(v):
                    return v.scalar_tensor_tensor(X[:, og, :], bk.ps[:, 0:TILE],
                                                  bout[:, og:og + 1], X[:, og, :],
                                                  ALU.add, ALU.add)
                return ve(f, waits=(tok,))
            proj(outwB, D, D, z_get(z), ev_out, t_src=(t_outwB, t_mix))
            t_x1 = _later(banks[0].free, banks[1].free)
            tap("X1", X[:], waits=(t_x1,))

            t_z1 = full_ln(lambda c: X[:, c, :], lambda c: z[:, c, :], TILE, 1e-5,
                           extra=(t_x1,))
            tap("z1", z[:], waits=(t_z1,))

            def ev_w1(bk, og, tok):
                def f(a):
                    return a.activation(out=big1[:, og, :], in_=bk.ps[:, 0:TILE],
                                        func=AF.Gelu, bias=b1v[:, og:og + 1],
                                        scale=alpha1[:, og:og + 1])
                return ae(f, waits=(tok,))
            proj(w1sB, D, 4096, z_get(z), ev_w1,
                 t_src=(t_z1, t_mix, t_grow, t_localu, t_qkv_done, t_tails))
            t_h0 = _later(banks[0].free, banks[1].free)
            tap("hffn", big1[:], waits=(t_h0,))

            def ev_w2(bk, og, tok):
                def f(v):
                    v.tensor_scalar(sqs[0][:, :], bk.ps[:, 0:TILE],
                                    alpha2[:, og:og + 1], b2v[:, og:og + 1],
                                    ALU.mult, ALU.add)
                    return v.tensor_add(X[:, og, :], X[:, og, :], sqs[0][:, :])
                return ve(f, waits=(tok,))
            proj(w2sB, 4096, D, z_get(big1), ev_w2,
                 t_src=(t_h0,), slab_cols=128)
            t_x2 = _later(banks[0].free, banks[1].free)
            tap("X2", X[:], waits=(t_x2,))

            t_z2 = full_ln(lambda c: X[:, c, :], lambda c: z[:, c, :], TILE, 1e-5,
                           extra=(t_x2,))
            tap("z2", z[:], waits=(t_z2,))
            t_x2w = dma("A", x2d.rearrange("(c p) n -> p c n", p=128)[:, :, tt * TILE:(tt + 1) * TILE],
                        X[:], waits=(t_x2,))
            t_z2w = dma("A", z2d.rearrange("(c p) n -> p c n", p=128)[:, :, tt * TILE:(tt + 1) * TILE],
                        z[:], waits=(t_z2,))

            bkl, bkb = banks[0], banks[1]
            opsl = [{"out": bkl.ps[0:16, 0:512], "lhsT": lamTw[:, kc, :],
                     "rhs": z[:, kc, :],
                     "start": kc == 0, "stop": kc == NCH - 1} for kc in range(NCH)]
            opsb = [{"out": bkb.ps[0:16, 0:512], "lhsT": wbTw[:, kc, :],
                     "rhs": z[:, kc, :],
                     "start": kc == 0, "stop": kc == NCH - 1} for kc in range(NCH)]
            t_l = pe(opsl, waits=(bkl.free, t_z2))
            t_b = pe(opsb, waits=(bkb.free,))
            t_sg = ae(lambda a: a.activation(out=lamrow[:, :], in_=bkl.ps[0:16, 0:512],
                                             func=AF.Sigmoid, bias=blam[:, :]),
                      waits=(t_l,))
            t_cb = ve(lambda v: v.tensor_copy(bxrow[:, :], bkb.ps[0:16, 0:512]),
                      waits=(t_b,))
            bkl.free = t_sg
            bkb.free = t_cb
            tap("lam", lamrow[:], waits=(t_sg,))
            tap("bx", bxrow[:], waits=(t_cb,))

            def f_scan(v):
                v.tensor_tensor_scan(hslb[:, :], lamrow[:, :], bxrow[:, :],
                                     scarry[:, :], ALU.mult, ALU.add)
                v.tensor_tensor_scan(acumb[:, :], lamrow[:, :], zer16[:, :],
                                     ccarry[:, :], ALU.mult, ALU.add)
                v.tensor_copy(gtmp[:, :], hslb[:, :])
                v.tensor_copy(scarry[:, :], FA2[0:16, TILE - 1:TILE])
                return v.tensor_copy(ccarry[:, :], FA2[0:16, 2 * TILE - 1:2 * TILE])
            t_scan = ve(f_scan, waits=(t_sg, t_cb, t_misc, t_hsw))
            banks[0].free = t_scan
            banks[1].free = t_scan
            t_hsw = dmas("A", [(hsd[:, tt * TILE:(tt + 1) * TILE], hslb[:, :]),
                               (acd[:, tt * TILE:(tt + 1) * TILE], acumb[:, :])],
                         waits=(t_scan,))
            tap("hsl", hslb[:, :], waits=(t_scan,))
            tap("acum", acumb[:, :], waits=(t_scan,))

        if upto == "phaseA":
            return finish()

        t_hw = dma("G", agh_in, scarry[:, :], waits=(t_scan,))
        box2 = {}

        def f_cc2(g):
            _wait(g, t_hw)
            i = g.collective_compute("AllGather", ALU.bypass,
                                     replica_groups=[[0, 1, 2, 3, 4, 5, 6, 7]],
                                     ins=[agh_in], outs=[agh_out])
            box2["t"] = sGP.inc(i)
        block.gpsimd(f_cc2)
        t_agh = dma("S", agh_s[:], agh_out.rearrange("(r k) one -> r (k one)", k=16),
                    waits=(box2["t"],))
        t_selmm = pe([{"out": banks[0].ps[0:16, 0:1], "lhsT": agh_s[:, :],
                       "rhs": selHs[:, :]}],
                     waits=tuple(w for w in (t_agh, banks[0].free) if w))
        t_init = ve(lambda v: v.tensor_copy(initv[:, :], banks[0].ps[0:16, 0:1]),
                    waits=(t_selmm,))
        banks[0].free = t_init
        tap("hft", scarry[:], waits=(t_hw,))
        tap("initv", initv[:], waits=(t_init,))
        if upto == "phaseB":
            return finish()

        t_h1 = None
        t_hs = None
        t_odma = None
        t_y = None
        t_x3 = None
        for tt in range(n_tiles):
            t_x2r = dma("S", X[:],
                        x2d.rearrange("(c p) n -> p c n", p=128)[:, :, tt * TILE:(tt + 1) * TILE],
                        waits=tuple(w for w in (t_x2w, t_odma) if w))
            t_z2r = dma("S", z[:],
                        z2d.rearrange("(c p) n -> p c n", p=128)[:, :, tt * TILE:(tt + 1) * TILE],
                        waits=tuple(w for w in (t_z2w, t_h1) if w))
            t_hsr = dmas("S", [(hslb[:, :], hsd[:, tt * TILE:(tt + 1) * TILE]),
                               (acumb[:, :], acd[:, tt * TILE:(tt + 1) * TILE])],
                         waits=tuple(w for w in (t_hsw, t_hs) if w))
            if tt == 0:
                tap("cx2", X[:], waits=(t_x2r,))
                tap("cz2", z[:], waits=(t_z2r,))
                tap("chsl", hslb[:, :], waits=(t_hsr,))

            if SERIAL[0] or 1 in JOINS[0]:
                joinpoint()

            def ev_wg(bk, og, tok):
                def f(a):
                    return a.activation(out=bag1[:, og, :], in_=bk.ps[:, 0:TILE],
                                        func=AF.Sigmoid)
                return ae(f, waits=(tok,))
            proj(WgB, D, D, z_get(z), ev_wg,
                 t_src=(t_WgB, t_z2r) + tuple(w for w in (t_y, t_x3) if w))
            t_gate = _later(banks[0].free, banks[1].free)
            tap("gate1", bag1[:], waits=(t_gate,))
            if SERIAL[0] or 2 in JOINS[0]:
                joinpoint()

            def f_hs(v):
                v.scalar_tensor_tensor(gtmp[:, :], acumb[:, :], initv[:, 0:1],
                                       hslb[:, :], ALU.mult, ALU.add)
                return v.tensor_copy(hsbf[:, :], gtmp[:, :])
            t_hs = ve(f_hs, waits=tuple(w for w in (t_init, t_hsr, t_y) if w))

            t_y = None
            for oc in range(NCH):
                bk = banks[2 + oc % 2]
                wl = [bk.free] + ([t_hs] if oc == 0 else [])
                t_mm = pe([{"out": bk.ps[:, 0:512], "lhsT": wcT[:, oc, :],
                            "rhs": hsbf[:, :]}],
                          waits=tuple(w for w in wl if w))

                def f(v, bk=bk, oc=oc):
                    return v.tensor_mul(qT[:, oc, :], bk.ps[:, 0:512], bag1[:, oc, :])
                bk.free = ve(f, waits=(t_mm, t_gate))
                t_y = bk.free
            tap("yv", qT[:], waits=(t_y,))
            if SERIAL[0] or 3 in JOINS[0]:
                joinpoint()

            b7 = banks[7]
            prev = [None, None]
            tsq_mm = None
            for c in range(NCH):
                pb = c % 2
                wl = [prev[pb]] + ([t_y] if c < 2 else [])
                t_sq = ae(lambda a, c=c, pb=pb: a.activation(
                    out=sqs[pb][:, :], in_=qT[:, c, :], func=AF.Square),
                    waits=tuple(w for w in wl if w))
                tsq_mm = pe([{"out": b7.ps[0:1, 0:512], "lhsT": onesf[:, :],
                              "rhs": sqs[pb][:, :], "start": c == 0, "stop": c == NCH - 1}],
                            waits=(t_sq, b7.free) if c == 0 else (t_sq,))
                prev[pb] = tsq_mm
            t_ms = ve(lambda v: v.tensor_scalar(srow[:, 0, 0:TILE], b7.ps[0:1, 0:TILE],
                                                1.0 / D, 1e-6, ALU.mult, ALU.add),
                      waits=(tsq_mm,))
            t_msq = ae(lambda a: a.activation(out=srow[:, 0, 0:TILE], in_=srow[:, 0, 0:TILE],
                                              func=AF.Sqrt), waits=(t_ms,))

            def f_rrec(v):
                v.reciprocal(srow[:, 0, 0:TILE], srow[:, 0, 0:TILE])
                return v.tensor_copy(srowb_rms[:, 0:TILE], srow[:, 0, 0:TILE])
            t_rrec = ve(f_rrec, waits=(t_msq,))
            b7.free = t_rrec
            tap("rmsrow", srow[:, 0, :], waits=(t_rrec,))

            bk5 = banks[5]
            t_bc = pe([{"out": bk5.ps[:, 0:512], "lhsT": ones1[:, :],
                        "rhs": srowb_rms[:, 0:TILE]}],
                      waits=tuple(w for w in (bk5.free, t_rrec) if w))
            t_x3 = None
            for c in range(NCH):
                def f(v, c=c):
                    v.scalar_tensor_tensor(sqs[0][:, :], qT[:, c, :],
                                           rmsw[:, c:c + 1], bk5.ps[:, 0:512],
                                           ALU.mult, ALU.mult)
                    return v.tensor_add(X[:, c, :], X[:, c, :], sqs[0][:, :])
                wl = [t_bc] + ([t_x2r] if c == 0 else [])
                t_x3 = ve(f, waits=tuple(w for w in wl if w))
            bk5.free = t_x3
            tap("X3", X[:], waits=(t_x3,))
            if SERIAL[0] or 4 in JOINS[0]:
                joinpoint()

            t_z3 = full_ln(lambda c: X[:, c, :], lambda c: z[:, c, :], TILE, 1e-5,
                           extra=(t_x3,))
            tap("z3", z[:], waits=(t_z3,))
            if SERIAL[0] or 5 in JOINS[0]:
                joinpoint()

            def ev_g(bk, og, tok):
                def f(a):
                    return a.activation(out=big1[:, og, :], in_=bk.ps[:, 0:TILE],
                                        func=AF.Silu, scale=1.0 / 127.0)
                return ae(f, waits=(tok,))
            proj(wgB, D, 3072, z_get(z), ev_g,
                 t_src=(t_z3, t_y, t_gate))
            t_sg1 = _later(banks[0].free, banks[1].free)
            if SERIAL[0] or 6 in JOINS[0]:
                joinpoint()

            def ev_u(bk, og, tok):
                def f(v):
                    return v.scalar_tensor_tensor(big1[:, og, :], bk.ps[:, 0:TILE],
                                                  1.0 / 127.0, big1[:, og, :],
                                                  ALU.mult, ALU.mult)
                return ve(f, waits=(tok, t_sg1))
            proj(wuB, D, 3072, z_get(z), ev_u)
            t_h1 = _later(banks[0].free, banks[1].free)
            if SERIAL[0] or 7 in JOINS[0]:
                joinpoint()

            def ev_d(bk, og, tok):
                def f(v):
                    return v.scalar_tensor_tensor(X[:, og, :], bk.ps[:, 0:TILE],
                                                  1.0 / 127.0, X[:, og, :],
                                                  ALU.mult, ALU.add)
                return ve(f, waits=(tok,))
            proj(wdB, 3072, D, z_get(big1), ev_d,
                 t_src=(t_h1,), slab_cols=256)
            t_out = _later(banks[0].free, banks[1].free)
            t_odma = dma("A", out_d.rearrange("(c p) n -> p c n", p=128)[:, :, tt * TILE:(tt + 1) * TILE],
                         X[:], waits=(t_out,))
            if SERIAL[0] or 8 in JOINS[0]:
                joinpoint()

        return finish()


def _mk_consts(core):
    half = core % 2
    consts = {}
    consts["c_ident"] = np.eye(128, dtype=BF)
    j = np.arange(256)[None, :]
    kl = np.arange(128)[:, None]
    consts["c_mask"] = ((1 <= j - kl) & (j - kl <= 128)).astype(BF)
    consts["c_halom"] = (np.ascontiguousarray(consts["c_mask"][:, 128:256]) if half == 1
                         else np.zeros((128, 128), BF))
    cb = half * 32
    lat = np.arange(64)[:, None]
    chunk = cb + np.arange(M)[None, :] // 128
    lm = (lat < chunk)
    consts["c_latm"] = np.concatenate([lm, lm], axis=0).astype(BF)
    oh = np.zeros((128, 16, 16), BF)
    for h in range(16):
        oh[:, h, h] = 1
    consts["c_oh16"] = oh
    ohl = np.zeros((128, 2, 4), BF)
    for c in range(2):
        ohl[0:64, c, 2 * c] = 1
        ohl[64:128, c, 2 * c + 1] = 1
    consts["c_ohlat"] = ohl
    bc2 = np.zeros((64, 8, 128), BF)
    for c in range(8):
        bc2[2 * c, c, 0:64] = 1
        bc2[2 * c + 1, c, 64:128] = 1
        bc2[32 + 2 * c, c, 0:64] = 1
        bc2[32 + 2 * c + 1, c, 64:128] = 1
    consts["c_bc2"] = bc2
    bcl = np.zeros((4, 2, 128), BF)
    for c in range(2):
        bcl[2 * c, c, 0:64] = 1
        bcl[2 * c + 1, c, 64:128] = 1
    consts["c_bclat"] = bcl
    consts["c_ones1"] = np.ones((1, 128), BF)
    consts["c_onesf"] = np.ones((128, 1), np.float32)
    selH = np.zeros((8, 1), np.float32)
    if half == 1:
        selH[core - 1, 0] = 1.0
    consts["c_selH"] = selH
    selA = np.zeros((128, 8), np.float32)
    selA[:, core - 1 if half == 1 else core] = 1.0
    consts["c_selA"] = selA
    selC = np.zeros((128, 8), np.float32)
    if half == 1:
        selC[:, core] = 1.0
    consts["c_selC"] = selC
    padmv = np.zeros((128, 128), BF)
    for reg in range(4):
        if (reg % 2) == half:
            padmv[:, reg * 32:(reg + 1) * 32] = np.float32(1.0 / 128.0)
    consts["c_padm"] = padmv
    return consts


def make_in_maps(inputs):
    x = np.asarray(inputs["L0_attn_x"])
    qkv_b = np.asarray(inputs["L0_attn_qkv_b"])
    in_maps = []
    for core in range(8):
        b, half = core // 2, core % 2
        g0 = half * M
        im = {}
        im["x"] = np.ascontiguousarray(x[b, g0:g0 + M, :].T)
        im["xh"] = (np.ascontiguousarray(x[b, g0 - 128:g0, :].T) if half == 1
                    else np.zeros((D, 128), np.float32))
        im["qkv_w"] = np.asarray(inputs["L0_attn_qkv_w"])
        im["rq_w"] = np.asarray(inputs["L0_attn_rq_w"])
        im["rkv_w"] = np.asarray(inputs["L0_attn_rkv_w"])
        im["rout_w"] = np.asarray(inputs["L0_attn_rout_w"])
        im["out_w"] = np.asarray(inputs["L0_attn_out_w"])
        im["gl_w"] = np.asarray(inputs["L0_attn_gl_w"])
        im["gr_w"] = np.asarray(inputs["L0_attn_gr_w"])
        im["w1"] = np.asarray(inputs["L0_ffn_w1"])
        im["w2"] = np.asarray(inputs["L0_ffn_w2"])
        im["w1t"] = np.ascontiguousarray(np.asarray(inputs["L0_ffn_w1"]).T)
        im["w2t"] = np.ascontiguousarray(np.asarray(inputs["L0_ffn_w2"]).T)
        im["W_lambda"] = np.asarray(inputs["L1_attn_W_lambda"])
        im["W_B"] = np.asarray(inputs["L1_attn_W_B"])
        im["W_C"] = np.asarray(inputs["L1_attn_W_C"])
        im["W_gate"] = np.ascontiguousarray(np.asarray(inputs["L1_attn_W_gate"]).T)
        im["wg"] = np.ascontiguousarray(np.asarray(inputs["L1_ffn_w_gate"]).T)
        im["wu"] = np.ascontiguousarray(np.asarray(inputs["L1_ffn_w_up"]).T)
        im["wd"] = np.ascontiguousarray(np.asarray(inputs["L1_ffn_w_down"]).T)
        im["b_qkv"] = np.ascontiguousarray(qkv_b.reshape(24, 128).T)
        im["b_rq"] = np.ascontiguousarray(np.asarray(inputs["L0_attn_rq_b"]).reshape(2, 128).T)
        im["b_rkv"] = np.ascontiguousarray(np.asarray(inputs["L0_attn_rkv_b"]).reshape(4, 128).T)
        im["b_rout"] = np.ascontiguousarray(np.asarray(inputs["L0_attn_rout_b"]).reshape(8, 128).T)
        im["b_out"] = np.ascontiguousarray(np.asarray(inputs["L0_attn_out_b"]).reshape(8, 128).T)
        im["b_1"] = np.ascontiguousarray(np.asarray(inputs["L0_ffn_b1"]).reshape(32, 128).T)
        im["b_2"] = np.ascontiguousarray(np.asarray(inputs["L0_ffn_b2"]).reshape(8, 128).T)
        im["b_lam"] = np.ascontiguousarray(np.asarray(inputs["L1_attn_b_lambda"]).reshape(16, 1))
        im["rmsw"] = np.ascontiguousarray(np.asarray(inputs["L1_attn_rms_weight"]).reshape(8, 128).T)
        im["vb_row"] = np.ascontiguousarray(qkv_b[2 * D:].reshape(1, D))
        im["rv_row"] = np.ascontiguousarray(np.asarray(inputs["L0_attn_rkv_b"])[256:].reshape(1, 256))
        im.update(_mk_consts(core))
        in_maps.append(im)
    return in_maps


def kernel(**inputs):
    nc = build_program()
    in_maps = make_in_maps(inputs)
    res = run_bass_kernel_spmd(nc, in_maps, core_ids=list(range(8)))
    out = np.empty((4, 2 * M, D), np.float32)
    for core in range(8):
        b, half = core // 2, core % 2
        out[b, half * M:(half + 1) * M, :] = res.results[core]["out"].T
    return out


# version 09
# speedup vs baseline: 2.2977x; 1.8925x over previous
import contextlib
import numpy as np
import ml_dtypes

import concourse.bass as bass
import concourse.mybir as mybir
from concourse.bass_utils import run_bass_kernel_spmd

F32 = mybir.dt.float32
BF16 = mybir.dt.bfloat16
I32 = mybir.dt.int32
AF = mybir.ActivationFunctionType
ALU = mybir.AluOpType
AX = mybir.AxisListType
BF = ml_dtypes.bfloat16

D = 1024
NCH = 8
TILE = 512
NTILE = 8
M = TILE * NTILE
H = 16
NM = TILE // 128


class Sem:
    def __init__(self, stack, nc, name):
        self.h = stack.enter_context(nc.semaphore(name))
        self.n = 0

    def inc(self, instr, k=1):
        instr.then_inc(self.h, k)
        self.n += k
        return (self, self.n)


def _wait(eng, tok):
    if tok is None:
        return
    sem, val = tok
    if val > 0:
        eng.wait_ge(sem.h, val)


def _later(*toks):
    toks = [t for t in toks if t is not None]
    if not toks:
        return None
    sems = {id(t[0]) for t in toks}
    assert len(sems) == 1, "mixed-semaphore _later; pass tokens separately"
    return max(toks, key=lambda t: t[1])


def _latest(*toks):
    best = {}
    for t in toks:
        if t is None:
            continue
        k = id(t[0])
        if k not in best or t[1] > best[k][1]:
            best[k] = t
    return tuple(best.values())


class Bank:
    __slots__ = ("ps", "free")

    def __init__(self, ps):
        self.ps = ps
        self.free = None


SERIAL = [False]
JOINS = [set()]
SERJ = [set(["P"])]
PHZ = ["P"]


def build_program(n_tiles=NTILE, taps=(), upto=None):
    nc = bass.Bass("TRN2", target_bir_lowering=False, debug=False)
    tapset = set(taps)

    def din(name, shape, dt=F32):
        return nc.dram_tensor(name, shape, dt, kind="ExternalInput").ap()

    x_d = din("x", [D, M])
    xh_d = din("xh", [D, 128])
    qkvw_d = din("qkv_w", [D, 3 * D])
    rqw_d = din("rq_w", [D, 256])
    rkvw_d = din("rkv_w", [D, 512])
    routw_d = din("rout_w", [256, D])
    outw_d = din("out_w", [D, D])
    glw_d = din("gl_w", [64, 1])
    grw_d = din("gr_w", [64, 1])
    w1_d = din("w1", [4096, D])
    w2_d = din("w2", [D, 4096])
    w1t_d = din("w1t", [D, 4096])
    w2t_d = din("w2t", [4096, D])
    wlam_d = din("W_lambda", [16, D])
    wbm_d = din("W_B", [16, D])
    wc_d = din("W_C", [D, 16])
    wgate_d = din("W_gate", [D, D])
    wgq_d = din("wg", [D, 3072], I32)
    wuq_d = din("wu", [D, 3072], I32)
    wdq_d = din("wd", [3072, D], I32)
    bqkv_d = din("b_qkv", [128, 24])
    brq_d = din("b_rq", [128, 2])
    brkv_d = din("b_rkv", [128, 4])
    brout_d = din("b_rout", [128, 8])
    bout_d = din("b_out", [128, 8])
    b1_d = din("b_1", [128, 32])
    b2_d = din("b_2", [128, 8])
    blam_d = din("b_lam", [16, 1])
    rmsw_d = din("rmsw", [128, 8])
    vbrow_d = din("vb_row", [1, D])
    rvrow_d = din("rv_row", [1, 256])
    ident_d = din("c_ident", [128, 128], BF16)
    mask_d = din("c_mask", [128, 256], BF16)
    halom_d = din("c_halom", [128, 128], BF16)
    latm_d = din("c_latm", [128, M], BF16)
    oh16_d = din("c_oh16", [128, 16, 16], BF16)
    ohlat_d = din("c_ohlat", [128, 2, 4], BF16)
    bc2_d = din("c_bc2", [64, 8, 128], BF16)
    bclat_d = din("c_bclat", [4, 2, 128], BF16)
    ones1_d = din("c_ones1", [1, 128], BF16)
    onesf_d = din("c_onesf", [128, 1])
    selH_d = din("c_selH", [8, 1])
    selA_d = din("c_selA", [128, 8])
    selC_d = din("c_selC", [128, 8])
    padm_d = din("c_padm", [128, 128], BF16)

    out_d = nc.dram_tensor("out", [D, M], F32, kind="ExternalOutput").ap()

    TAPSPECS = {
        "z0": ([128, NCH, TILE], BF16), "qT": ([128, NCH, TILE], BF16),
        "kT": ([128, NCH, TILE], BF16), "vtok": ([128, NCH, TILE], BF16),
        "localu": ([128, NCH, TILE], BF16), "lsum": ([16, TILE], F32),
        "mixed": ([128, NCH, TILE], BF16), "remote": ([128, NCH, TILE], BF16),
        "X1": ([128, NCH, TILE], F32), "X2": ([128, NCH, TILE], F32),
        "hft": ([16, 1], F32), "initv": ([16, 1], F32),
        "rkall": ([128, 2, 64], BF16), "rvlm": ([128, 256], BF16),
        "latn": ([128, 2, TILE], BF16), "grow": ([16, TILE], F32),
        "hsl": ([16, TILE], F32), "acum": ([16, TILE], F32),
        "X3": ([128, NCH, TILE], F32), "z1": ([128, NCH, TILE], BF16),
        "z2": ([128, NCH, TILE], BF16), "z3": ([128, NCH, TILE], BF16),
        "hffn": ([128, 32, TILE], BF16), "pooled": ([128, NCH, 128], BF16),
        "alpha1": ([128, 32], F32), "lam": ([16, TILE], F32),
        "bx": ([16, TILE], F32), "yv": ([128, NCH, TILE], BF16),
        "gate1": ([128, NCH, TILE], BF16), "rmsrow": ([1, TILE], F32),
        "latu": ([128, 2, TILE], BF16), "latsum": ([4, TILE], F32),
        "rqT": ([128, 2, TILE], BF16), "ktail": ([128, NCH, 128], BF16),
        "vtail": ([128, D], BF16), "hz": ([128, NCH, 128], BF16), "cx2": ([128, NCH, TILE], F32), "cz2": ([128, NCH, TILE], BF16), "chsl": ([16, TILE], F32),
    }
    tap_d = {tn: nc.dram_tensor("tap_" + tn, sh, dt, kind="ExternalOutput").ap()
             for tn, (sh, dt) in TAPSPECS.items() if tn in tapset}

    def scr(name, shape, dt=BF16):
        return nc.dram_tensor(name, shape, dt, kind="Internal").ap()

    qkvB = scr("qkvB", [D, 3 * D])
    rqB = scr("rqB", [D, 256])
    rkvB = scr("rkvB", [D, 512])
    routB = scr("routB", [256, D])
    outwB = scr("outwB", [D, D])
    w1sB = scr("w1sB", [D, 4096])
    w2sB = scr("w2sB", [4096, D])
    WgB = scr("WgB", [D, D])
    wgB = scr("wgB", [D, 3072])
    wuB = scr("wuB", [D, 3072])
    wdB = scr("wdB", [3072, D])
    x2d = scr("x2d", [D, M], F32)
    z2d = scr("z2d", [D, M], BF16)
    hsd = scr("hsd", [16, M], F32)
    acd = scr("acd", [16, M], F32)
    agkv_in = scr("agkv_in", [3, 128, 128], BF16)
    agkv_out = scr("agkv_out", [8, 3, 128, 128], BF16)
    agh_in = scr("agh_in", [16, 1], F32)
    agh_out = scr("agh_out", [128, 1], F32)

    stack = contextlib.ExitStack()
    with stack:
        def sbuf(name, shape, dt):
            return stack.enter_context(nc.sbuf_tensor(name, shape, dt))

        FA = sbuf("FA", [128, 10240], F32)
        BA = sbuf("BA", [128, 32768], BF16)

        def carve(arena, off, shape):
            n = 1
            for s_ in shape[1:]:
                n *= s_
            ap = arena[0:shape[0], off:off + n]
            if len(shape) == 3:
                ap = ap.rearrange("p (a b) -> p a b", a=shape[1])
            return ap

        X = carve(FA, 0, [128, NCH, TILE])
        sqs = [carve(FA, 4096 + i * TILE, [128, TILE]) for i in range(2)]
        srow = carve(FA, 5120, [1, 3, TILE])
        wtmp = [carve(FA, 0, [128, 4096]),
                carve(FA, 4096, [128, 4096])]
        lsum = carve(FA, 6656, [16, TILE])
        latsum = carve(FA, 7168, [4, TILE])
        grow = carve(FA, 7680, [16, TILE])
        gtmp = carve(FA, 8192, [16, TILE])
        lamrow = carve(FA, 8704, [16, TILE])
        bxrow = carve(FA, 9216, [16, TILE])
        zer16 = carve(FA, 9728, [16, TILE])
        FA2 = sbuf("FA2", [16, 2 * TILE], F32)
        hslb = FA2[0:16, 0:TILE]
        acumb = FA2[0:16, TILE:2 * TILE]

        z = carve(BA, 0, [128, NCH, TILE])
        qT = carve(BA, 4096, [128, NCH, TILE])
        kT = carve(BA, 8192, [128, NCH, TILE])
        vtok = carve(BA, 12288, [128, NCH, TILE])
        bag1 = carve(BA, 16384, [128, NCH, TILE])
        bag2 = carve(BA, 20480, [128, NCH, TILE])
        big1 = carve(BA, 4096, [128, 32, TILE])
        wsgn = [carve(BA, 4096, [128, 4096]),
                carve(BA, 8192, [128, 4096])]
        agkv_s = carve(BA, 12288, [128, 24, 128])
        rqTb = carve(BA, 24576, [128, 2, TILE])
        lexp = carve(BA, 25600, [128, 2, TILE])
        latu = carve(BA, 26624, [128, 2, TILE])
        latnb = carve(BA, 24576, [128, 2, TILE])
        etile = carve(BA, 27648, [128, 2, 256])
        hz = carve(BA, 28160, [128, NCH, 128])
        mtmp = carve(BA, 29184, [128, TILE])
        srowb = carve(BA, 29696, [1, 2, TILE])
        srowb_rms = carve(BA, 30720, [1, TILE])
        latsb = carve(BA, 31232, [4, TILE])
        ghb = carve(BA, 31744, [16, TILE])
        gm1b = carve(BA, 32256, [16, TILE])

        ident = sbuf("ident", [128, 128], BF16)
        mask01 = sbuf("mask01", [128, 256], BF16)
        halom = sbuf("halom", [128, 128], BF16)
        latm = sbuf("latm_s", [128, M], BF16)
        oh16 = sbuf("oh16", [128, 16, 16], BF16)
        ohlat = sbuf("ohlat", [128, 2, 4], BF16)
        bc2 = sbuf("bc2", [64, 8, 128], BF16)
        bclat = sbuf("bclat", [4, 2, 128], BF16)
        ones1 = sbuf("ones1", [1, 128], BF16)
        onesf = sbuf("onesf", [128, 1], F32)
        ones1b = sbuf("ones1b", [128, 1], BF16)
        selHs = sbuf("selHs", [8, 1], F32)
        selAs = sbuf("selAs", [128, 8], F32)
        selCs = sbuf("selCs", [128, 8], F32)
        padm = sbuf("padm", [128, 128], BF16)
        bqkv = sbuf("bqkv", [128, 24], F32)
        brq = sbuf("brq", [128, 2], F32)
        brkv = sbuf("brkv", [128, 4], F32)
        brout = sbuf("brout", [128, 8], F32)
        bout = sbuf("bout", [128, 8], F32)
        b1v = sbuf("b1v", [128, 32], F32)
        b2v = sbuf("b2v", [128, 8], F32)
        blam = sbuf("blam", [16, 1], F32)
        rmsw = sbuf("rmsw_s", [128, 8], F32)
        vb_bc = sbuf("vb_bc", [128, D], BF16)
        alpha1 = sbuf("alpha1", [128, 32], F32)
        alpha2 = sbuf("alpha2", [128, 8], F32)
        lamTw = sbuf("lamTw", [128, NCH, 16], BF16)
        wbTw = sbuf("wbTw", [128, NCH, 16], BF16)
        wcT = sbuf("wcT", [16, NCH, 128], BF16)
        glgr = sbuf("glgr", [128, 2, 8, 16], BF16)
        rkallT = sbuf("rkallT", [128, 2, 64], BF16)
        rvlm = sbuf("rvlm", [128, 256], BF16)
        smu0 = sbuf("smu0", [1, NTILE, TILE], BF16)
        sinv0 = sbuf("sinv0", [1, NTILE, TILE], BF16)
        hmu = sbuf("hmu", [1, 128], BF16)
        hinv = sbuf("hinv", [1, 128], BF16)
        ktail = sbuf("ktail", [128, NCH, 128], BF16)
        vtail = sbuf("vtail", [128, D], BF16)
        scarry = sbuf("scarry", [16, 1], F32)
        ccarry = sbuf("ccarry", [16, 1], F32)
        initv = sbuf("initv", [16, 1], F32)
        pooledT = sbuf("pooledT", [128, NCH, 128], BF16)
        glw_s = sbuf("glw_s", [64, 1], F32)
        grw_s = sbuf("grw_s", [64, 1], F32)
        rk_con = sbuf("rk_con", [128, 2, 64], BF16)
        rv_con = sbuf("rv_con", [128, 256], BF16)
        hsbf = sbuf("hsbf", [16, TILE], BF16)
        pooledF = sbuf("pooledF", [128, NCH, 32], F32)
        agh_s = sbuf("agh_s", [8, 16], F32)
        vbrow_s = sbuf("vbrow_s", [1, D], F32)
        vbrow_b = sbuf("vbrow_b", [1, D], BF16)
        rvrow_s = sbuf("rvrow_s", [1, 256], F32)
        rvrow_b = sbuf("rvrow_b", [1, 256], BF16)
        wlam_s = sbuf("wlam_s", [16, D], F32)
        wlam_b = sbuf("wlam_b", [16, D], BF16)
        wbm_s = sbuf("wbm_s", [16, D], F32)
        wbm_b = sbuf("wbm_b", [16, D], BF16)
        wc_s = sbuf("wc_s", [128, NCH, 16], F32)
        wc_b = sbuf("wc_b", [128, NCH, 16], BF16)
        wsl = [sbuf(f"wsl{i}", [128, 8192], BF16) for i in range(2)]

        def wslv(i, nkc, cols):
            return wsl[i][:, 0:nkc * cols].rearrange("p (a b) -> p a b", a=nkc)

        banks = [Bank(stack.enter_context(nc.psum_tensor(f"pb{i}", [128, 512], F32)))
                 for i in range(8)]

        block = stack.enter_context(nc.Block())
        qS = Sem(stack, nc, "qS")
        qG = Sem(stack, nc, "qG")
        qA = Sem(stack, nc, "qA")
        sPE = Sem(stack, nc, "sPE")
        sDVE = Sem(stack, nc, "sDVE")
        sACT = Sem(stack, nc, "sACT")
        sGP = Sem(stack, nc, "sGP")

        allsems = [qS, qG, qA, sPE, sDVE, sACT, sGP]

        def _serial_waits(e):
            if SERIAL[0] or 0 in JOINS[0] or PHZ[0] in SERJ[0]:
                for s_ in allsems:
                    if s_.n > 0:
                        e.wait_ge(s_.h, s_.n)

        def dmas(qname, pairs, waits=(), transpose=False):
            sec, sem = {"S": (block.sync, qS), "G": (block.gpsimd, qG),
                        "A": (block.scalar, qA)}[qname]
            box = {}

            def f(e):
                _serial_waits(e)
                for w in waits:
                    _wait(e, w)
                for out, in_ in pairs:
                    i = e.dma_start(out=out, in_=in_, transpose=transpose)
                    box["t"] = sem.inc(i, 16)
            sec(f)
            return box["t"]

        def dma(qname, out, in_, waits=(), transpose=False):
            return dmas(qname, [(out, in_)], waits=waits, transpose=transpose)

        def pe(ops, waits=()):
            box = {}

            def f(t):
                _serial_waits(t)
                for w in waits:
                    _wait(t, w)
                last = None
                for o in ops:
                    kw = {}
                    if o.get("tp") is not None:
                        kw["tile_position"] = o["tp"]
                    last = t.matmul(o["out"], o["lhsT"], o["rhs"],
                                    start=o.get("start", True),
                                    stop=o.get("stop", True), **kw)
                box["t"] = sPE.inc(last)
            block.tensor(f)
            return box["t"]

        def tr(out_ps, in_ap, waits=(), idn=128):
            return pe([{"out": out_ps, "lhsT": in_ap, "rhs": ident[0:idn, 0:idn]}],
                      waits=waits)

        def ve(fn, waits=()):
            box = {}

            def f(v):
                _serial_waits(v)
                for w in waits:
                    _wait(v, w)
                box["t"] = sDVE.inc(fn(v))
            block.vector(f)
            return box["t"]

        def ae(fn, waits=()):
            box = {}

            def f(a):
                _serial_waits(a)
                for w in waits:
                    _wait(a, w)
                box["t"] = sACT.inc(fn(a))
            block.scalar(f)
            return box["t"]

        def ae_multi(items):
            toks = []

            def f(a):
                _serial_waits(a)
                for waits, fn in items:
                    for w in waits:
                        _wait(a, w)
                    toks.append(sACT.inc(fn(a)))
            block.scalar(f)
            return toks

        def ge(fn, waits=()):
            box = {}

            def f(g):
                _serial_waits(g)
                for w in waits:
                    _wait(g, w)
                box["t"] = sGP.inc(fn(g))
            block.gpsimd(f)
            return box["t"]

        def tap(name, src_ap, waits=()):
            if name in tap_d:
                return dma("A", tap_d[name], src_ap, waits=waits)
            return None

        dummy = sbuf("dummy_jp", [1, 1], F32)

        def joinpoint():
            toks = [(s_, s_.n) for s_ in (qS, qG, qA, sPE, sDVE, sACT, sGP) if s_.n > 0]
            return ve(lambda v: v.memset(dummy[:, :], 0.0), waits=tuple(toks))

        def finish():
            def f(g):
                _wait(g, (qS, qS.n))
                _wait(g, (qA, qA.n))
                _wait(g, (qG, qG.n))
            block.gpsimd(f)
            return nc

        t_consts = dmas("S", [
            (ident[:], ident_d), (mask01[:], mask_d), (halom[:], halom_d),
            (latm[:], latm_d), (oh16[:], oh16_d), (ohlat[:], ohlat_d),
            (bc2[:], bc2_d), (bclat[:], bclat_d), (ones1[:], ones1_d),
            (onesf[:], onesf_d), (selHs[:], selH_d), (selAs[:], selA_d),
            (selCs[:], selC_d), (padm[:], padm_d), (bqkv[:], bqkv_d),
            (brq[:], brq_d), (brkv[:], brkv_d), (brout[:], brout_d),
            (bout[:], bout_d), (b1v[:], b1_d), (b2v[:], b2_d),
            (blam[:], blam_d), (rmsw[:], rmsw_d), (vbrow_s[:], vbrow_d),
            (rvrow_s[:], rvrow_d),
            (glw_s[:], glw_d), (grw_s[:], grw_d), (wlam_s[:], wlam_d),
            (wbm_s[:], wbm_d),
            (wc_s[:], wc_d.rearrange("(c p) n -> p c n", p=128)),
        ])

        t_qkvB = dma("G", qkvB, qkvw_d)
        t_rkvB = dma("G", rkvB, rkvw_d)
        t_rqB = dma("G", rqB, rqw_d)
        t_routB = dma("G", routB, routw_d)
        t_outwB = dma("G", outwB, outw_d)
        t_WgB = dma("G", WgB, wgate_d)
        t_wgB = dma("G", wgB, wgq_d)
        t_wuB = dma("G", wuB, wuq_d)
        t_wdB = dma("G", wdB, wdq_d)

        def prep_misc(v):
            v.memset(scarry[:], 0.0)
            v.memset(ccarry[:], 1.0)
            v.memset(ones1b[:], 1.0)
            v.memset(glgr[:], 0.0)
            v.tensor_copy(vbrow_b[:], vbrow_s[:])
            v.tensor_copy(rvrow_b[:], rvrow_s[:])
            v.tensor_copy(wlam_b[:], wlam_s[:])
            v.tensor_copy(wbm_b[:], wbm_s[:])
            last = v.tensor_copy(wc_b[:], wc_s[:])
            for c in range(8):
                v.tensor_copy(glgr[0:64, 0, c, 2 * c:2 * c + 1], glw_s[:, :])
                v.tensor_copy(glgr[64:128, 0, c, 2 * c + 1:2 * c + 2], glw_s[:, :])
                v.tensor_copy(glgr[0:64, 1, c, 2 * c:2 * c + 1], grw_s[:, :])
                last = v.tensor_copy(glgr[64:128, 1, c, 2 * c + 1:2 * c + 2], grw_s[:, :])
            return last
        t_misc = ve(prep_misc, waits=(t_consts,))

        tbv = pe([{"out": banks[0].ps[:, 0:512], "lhsT": ones1[:, :], "rhs": vbrow_b[:, 0:512]},
                  {"out": banks[1].ps[:, 0:512], "lhsT": ones1[:, :], "rhs": vbrow_b[:, 512:1024]}],
                 waits=(t_misc,))

        def f_vb(v):
            v.tensor_copy(vb_bc[:, 0:512], banks[0].ps[:, 0:512])
            return v.tensor_copy(vb_bc[:, 512:1024], banks[1].ps[:, 0:512])
        t_vb = ve(f_vb, waits=(tbv,))
        banks[0].free = t_vb
        banks[1].free = t_vb

        for cch in range(NCH):
            bk = banks[cch % 2]
            tr(bk.ps[:, 0:16], wlam_b[:, cch * 128:(cch + 1) * 128],
               waits=(t_misc, bk.free), idn=16)
            tb_ = tr(bk.ps[:, 16:32], wbm_b[:, cch * 128:(cch + 1) * 128], idn=16)

            def f(v, bk=bk, cch=cch):
                v.tensor_copy(lamTw[:, cch, :], bk.ps[:, 0:16])
                return v.tensor_copy(wbTw[:, cch, :], bk.ps[:, 16:32])
            bk.free = ve(f, waits=(tb_,))
        for cch in range(NCH):
            bk = banks[cch % 2]
            ta = tr(bk.ps[0:16, 0:128], wc_b[:, cch, :], waits=(t_misc, bk.free))
            bk.free = ve(lambda v, bk=bk, cch=cch:
                         v.tensor_copy(wcT[:, cch, :], bk.ps[0:16, 0:128]),
                         waits=(ta,))

        wtok = [None, None]
        rtok = [None, None]
        for it in range(32):
            pb = it % 2
            t_l = dma("S", wtmp[pb][:, 0:1024], w1_d[it * 128:(it + 1) * 128, :],
                      waits=tuple(w for w in (rtok[pb],) if w))
            rtok[pb] = ve(lambda v, pb=pb, it=it: v.reduce_sum(
                alpha1[:, it:it + 1], wtmp[pb][:, 0:1024],
                axis=AX.X, apply_absolute_value=True), waits=(t_l,))
        for it in range(8):
            pb = it % 2
            t_l = dma("S", wtmp[pb][:, :], w2_d[it * 128:(it + 1) * 128, :],
                      waits=tuple(w for w in (rtok[pb],) if w))
            rtok[pb] = ve(lambda v, pb=pb, it=it: v.reduce_sum(
                alpha2[:, it:it + 1], wtmp[pb][:, :],
                axis=AX.X, apply_absolute_value=True), waits=(t_l,))
        for it in range(8):
            pb = it % 2
            t_l = dma("S", wtmp[pb][:, :], w1t_d[it * 128:(it + 1) * 128, :],
                      waits=tuple(w for w in (wtok[pb], rtok[pb]) if w))
            rtok[pb] = None
            t_s = ae(lambda a, pb=pb: a.activation(out=wsgn[pb][:, :], in_=wtmp[pb][:, :],
                                                   func=AF.Sign),
                     waits=(t_l,))
            wtok[pb] = dma("G", w1sB[it * 128:(it + 1) * 128, :], wsgn[pb][:, :],
                           waits=(t_s,))
        wtok2 = [None, None]
        for it in range(32):
            pb = it % 2
            t_l = dma("S", wtmp[pb][:, 0:1024], w2t_d[it * 128:(it + 1) * 128, :],
                      waits=tuple(w for w in (wtok2[pb] or wtok[pb],) if w))
            t_s = ae(lambda a, pb=pb: a.activation(out=wsgn[pb][:, 0:1024],
                                                   in_=wtmp[pb][:, 0:1024], func=AF.Sign),
                     waits=(t_l,))
            wtok2[pb] = dma("G", w2sB[it * 128:(it + 1) * 128, :], wsgn[pb][:, 0:1024],
                            waits=(t_s,))

        def f_alpha(v):
            v.memset(zer16[:, :], 0.0)
            v.tensor_scalar_mul(alpha1[:], alpha1[:], 1.0 / 1024.0)
            return v.tensor_scalar_mul(alpha2[:], alpha2[:], 1.0 / 4096.0)
        t_alpha = ve(f_alpha)
        t_signs = _latest(*[t for t in (*wtok, *wtok2, t_alpha, *rtok) if t])
        tap("alpha1", alpha1[:], waits=(t_alpha,))

        def ln_stats(zget, T_, mu_out, inv_out, eps, extra=(), srcf32=True):
            b6, b7 = banks[6], banks[7]
            onesz = onesf if srcf32 else ones1b
            ops = []
            for c in range(NCH):
                ops.append({"out": b6.ps[0:1, 0:T_], "lhsT": onesz[:, :],
                            "rhs": zget(c)[:, 0:T_], "start": c == 0, "stop": c == NCH - 1})
            tsum = pe(ops, waits=(b6.free,) + tuple(extra))
            tsq_mm = None
            prev = [None, None]
            for c0 in range(0, NCH, 2):
                wl = [prev[0], prev[1]] + (list(extra) if c0 == 0 else [])
                tp = ae_multi([
                    (tuple(w for w in wl if w),
                     lambda a, c=c0: a.activation(
                         out=sqs[0][:, 0:T_], in_=zget(c)[:, 0:T_], func=AF.Square)),
                    ((),
                     lambda a, c=c0 + 1: a.activation(
                         out=sqs[1][:, 0:T_], in_=zget(c)[:, 0:T_], func=AF.Square)),
                ])
                for i, c in enumerate((c0, c0 + 1)):
                    tsq_mm = pe([{"out": b7.ps[0:1, 0:T_], "lhsT": onesf[:, :],
                                  "rhs": sqs[c % 2][:, 0:T_],
                                  "start": c == 0, "stop": c == NCH - 1}],
                                waits=(tp[i], b7.free) if c == 0 else (tp[i],))
                    prev[c % 2] = tsq_mm

            def fin(v):
                v.tensor_scalar_mul(srow[:, 0, 0:T_], b6.ps[0:1, 0:T_], 1.0 / D)
                v.tensor_scalar_mul(srow[:, 1, 0:T_], b7.ps[0:1, 0:T_], 1.0 / D)
                v.tensor_mul(srow[:, 2, 0:T_], srow[:, 0, 0:T_], srow[:, 0, 0:T_])
                v.tensor_sub(srow[:, 1, 0:T_], srow[:, 1, 0:T_], srow[:, 2, 0:T_])
                return v.tensor_scalar_add(srow[:, 1, 0:T_], srow[:, 1, 0:T_], float(eps))
            t_fin = ve(fin, waits=(tsum, tsq_mm))
            t_sq2 = ae(lambda a: a.activation(out=srow[:, 2, 0:T_], in_=srow[:, 1, 0:T_],
                                              func=AF.Sqrt),
                       waits=(t_fin,))

            def fin2(v):
                v.reciprocal(srow[:, 2, 0:T_], srow[:, 2, 0:T_])
                v.tensor_copy(mu_out, srow[:, 0, 0:T_])
                return v.tensor_copy(inv_out, srow[:, 2, 0:T_])
            t2 = ve(fin2, waits=tuple(w for w in (t_sq2, srowb_busy[0]) if w))
            b6.free = t2
            b7.free = t2
            return t2

        srowb_busy = [None]

        def ln_bcast(mu_row, inv_row, T_, extra=()):
            bkA, bkB = banks[4], banks[5]
            t = pe([{"out": bkA.ps[:, 0:T_], "lhsT": ones1[:, :], "rhs": mu_row},
                    {"out": bkB.ps[:, 0:T_], "lhsT": ones1[:, :], "rhs": inv_row}],
                   waits=tuple(w for w in (bkA.free, bkB.free) + tuple(extra) if w))
            srowb_busy[0] = t
            return t

        def ln_apply(xget, zget, T_, t_bc):
            bkA, bkB = banks[4], banks[5]
            tk = None
            for c in range(NCH):
                def f(v, c=c):
                    v.tensor_sub(zget(c)[:, 0:T_], xget(c)[:, 0:T_], bkA.ps[:, 0:T_])
                    return v.tensor_mul(zget(c)[:, 0:T_], zget(c)[:, 0:T_], bkB.ps[:, 0:T_])
                tk = ve(f, waits=(t_bc,) if c == 0 else ())
            bkA.free = tk
            bkB.free = tk
            return tk

        def full_ln(xget, zget, T_, eps, extra=(), srcf32=True,
                    mu_out=None, inv_out=None, stats=True):
            mu_out = srowb[:, 0, 0:T_] if mu_out is None else mu_out
            inv_out = srowb[:, 1, 0:T_] if inv_out is None else inv_out
            if stats:
                t_st = ln_stats(xget, T_, mu_out, inv_out, eps, extra=extra,
                                srcf32=srcf32)
                t_bc = ln_bcast(mu_out, inv_out, T_, extra=(t_st,))
            else:
                t_bc = ln_bcast(mu_out, inv_out, T_, extra=_latest(*extra))
            return ln_apply(xget, zget, T_, t_bc)

        t_pool = None
        t_z0 = None
        t_ap = None
        for tt in list(range(1, n_tiles)) + [0]:
            t_x = dma("S", X[:],
                      x_d.rearrange("(c p) n -> p c n", p=128)[:, :, tt * TILE:(tt + 1) * TILE],
                      waits=t_signs + ((t_ap,) if t_ap else ()))
            t_ap = full_ln(lambda c: X[:, c, :], lambda c: z[:, c, :], TILE, 1e-5,
                           extra=(t_x,) + t_signs, mu_out=smu0[:, tt, :],
                           inv_out=sinv0[:, tt, :])
            for c in range(NCH):
                t_pool = ve(lambda v, c=c, tt=tt: v.reduce_sum(
                    pooledF[:, c, tt * 4:(tt + 1) * 4],
                    z[:, c, :].rearrange("p (g w) -> p g w", w=128), axis=AX.X),
                    waits=(t_ap,) if c == 0 else ())
            if tt == 0:
                t_z0 = t_ap
        tap("z0", z[:], waits=(t_z0,))

        t_xh = dma("G", hz[:], xh_d.rearrange("(c p) n -> p c n", p=128),
                   waits=t_signs)
        t_hst = ln_stats(lambda c: hz[:, c, :], 128, hmu[:, :], hinv[:, :], 1e-5,
                         extra=(t_xh,), srcf32=False)
        t_hbc = ln_bcast(hmu[:, :], hinv[:, :], 128, extra=(t_hst,))
        t_hz = ln_apply(lambda c: hz[:, c, :], lambda c: hz[:, c, :], 128, t_hbc)
        tap("hz", hz[:], waits=(t_hz,))

        def f_pad(v):
            last = None
            for c in range(NCH):
                for reg in range(4):
                    v.tensor_copy(pooledT[:, c, reg * 32:(reg + 1) * 32], pooledF[:, c, :])
                last = v.tensor_mul(pooledT[:, c, :], pooledT[:, c, :], padm[:, :])
            return last
        t_pad = ve(f_pad, waits=(t_pool, t_misc))
        tap("pooled", pooledT[:], waits=(t_pad,))

        t_rkvl = dma("S", wslv(0, 8, 512)[:, :, :],
                     rkvB.rearrange("(c p) n -> p c n", p=128), waits=(t_rkvB,))
        wsl_busy = [None, None]
        for oc in range(2):
            bk = banks[oc]
            ops = [{"out": bk.ps[:, 0:64], "lhsT": wslv(0, 8, 512)[:, kc, oc * 128:(oc + 1) * 128],
                    "rhs": pooledT[:, kc, 0:64], "start": kc == 0, "stop": kc == NCH - 1}
                   for kc in range(NCH)]
            t_mm = pe(ops, waits=(t_rkvl, t_pad, bk.free))
            bk.free = ve(lambda v, bk=bk, oc=oc: v.tensor_copy(rk_con[:, oc, :], bk.ps[:, 0:64]),
                         waits=(t_mm,))
        bkv = banks[2]
        ops = [{"out": bkv.ps[:, 0:256], "lhsT": pooledT[:, kc, :],
                "rhs": wslv(0, 8, 512)[:, kc, 256:512], "start": kc == 0, "stop": kc == NCH - 1}
               for kc in range(NCH)]
        t_rvmm = pe(ops, waits=(t_rkvl, t_pad, bkv.free))
        wsl_busy[0] = t_rvmm
        bkv.free = ve(lambda v: v.tensor_copy(rv_con[:, :], bkv.ps[:, 0:256]),
                      waits=(t_rvmm,))

        t_agw = dmas("G", [(agkv_in[0], rk_con[:].rearrange("p a b -> p (a b)")),
                           (agkv_in[1], rv_con[:, 0:128]),
                           (agkv_in[2], rv_con[:, 128:256])],
                     waits=(banks[0].free, banks[1].free, bkv.free))
        box1 = {}

        def f_cc1(g):
            _wait(g, t_agw)
            i = g.collective_compute("AllGather", ALU.bypass,
                                     replica_groups=[[0, 1, 2, 3, 4, 5, 6, 7]],
                                     ins=[agkv_in], outs=[agkv_out])
            box1["t"] = sGP.inc(i)
        block.gpsimd(f_cc1)
        t_agr = dmas("S", [(agkv_s[:, s_ * 8:(s_ + 1) * 8, :],
                            agkv_out[:, s_, :, :].rearrange("r p n -> p r n"))
                           for s_ in range(3)],
                     waits=(box1["t"],))

        def f_comb(v):
            last = None
            for s_ in range(3):
                dst = (rkallT[:].rearrange("p a b -> p (a b)") if s_ == 0
                       else rvlm[:, (s_ - 1) * 128:s_ * 128])
                v.tensor_scalar(dst, agkv_s[:, s_ * 8, :], selAs[:, 0:1], None, ALU.mult)
                for r in range(1, 8):
                    v.scalar_tensor_tensor(dst, agkv_s[:, s_ * 8 + r, :], selAs[:, r:r + 1],
                                           dst, ALU.mult, ALU.add)
                for r in range(8):
                    last = v.scalar_tensor_tensor(dst, agkv_s[:, s_ * 8 + r, :], selCs[:, r:r + 1],
                                                  dst, ALU.mult, ALU.add)
            return last
        t_comb = ve(f_comb, waits=(t_agr,))

        t_rvbb = pe([{"out": banks[3].ps[:, 0:256], "lhsT": ones1[:, :],
                      "rhs": rvrow_b[:, 0:256]}], waits=(banks[3].free, t_misc))

        def f_bias(v):
            v.tensor_scalar(rkallT[:, 0, :], rkallT[:, 0, :], brkv[:, 0:1], None, ALU.add)
            v.tensor_scalar(rkallT[:, 1, :], rkallT[:, 1, :], brkv[:, 1:2], None, ALU.add)
            return v.tensor_add(rvlm[:, :], rvlm[:, :], banks[3].ps[:, 0:256])
        t_rkrv = ve(f_bias, waits=(t_comb, t_rvbb))
        banks[3].free = t_rkrv
        tap("rkall", rkallT[:], waits=(t_rkrv,))
        tap("rvlm", rvlm[:], waits=(t_rkrv,))

        if upto == "prep":
            return finish()

        wsl_busy = [None, None]

        def proj(dramB, K, NOUT, rhs_get, evac, transposed=False, t_src=(),
                 slab_cols=512, evac2=None):
            nkc = K // 128
            nslab = NOUT // slab_cols
            socc = slab_cols // 128
            slab_tok = [None, None]
            mm_tok = [wsl_busy[0], wsl_busy[1]]
            pend = []

            def flush():
                if not pend:
                    return
                if evac2 is not None and len(pend) == 2:
                    (b0, o0, k0), (b1, o1, k1) = pend
                    f0, f1 = evac2(b0, o0, k0, b1, o1, k1)
                    b0.free = f0
                    b1.free = f1
                else:
                    for b_, o_, k_ in pend:
                        b_.free = evac(b_, o_, k_)
                pend.clear()
            for j in range(nslab + 1):
                pb = j % 2
                if j < nslab:
                    w0 = tuple(w for w in (slab_tok[pb], mm_tok[pb]) if w)
                    w0 = w0 + (tuple(t_src) if j == 0 else ())
                    if not transposed:
                        slab_tok[pb] = dma(
                            "S", wslv(pb, nkc, slab_cols)[:, :, :],
                            dramB.rearrange("(c p) n -> p c n", p=128)[
                                :, :, j * slab_cols:(j + 1) * slab_cols],
                            waits=w0)
                    else:
                        prs = [(wslv(pb, nkc, slab_cols)[:, kc, :],
                                dramB[j * slab_cols:(j + 1) * slab_cols,
                                      kc * 128:(kc + 1) * 128])
                               for kc in range(nkc)]
                        slab_tok[pb] = dmas("S", prs, transpose=True, waits=w0)
                if j == 0:
                    continue
                jj = j - 1
                pbc = jj % 2
                for oc in range(socc):
                    og = jj * socc + oc
                    bk = banks[og % 2]
                    ops = [{"out": bk.ps[:, 0:TILE],
                            "lhsT": wslv(pbc, nkc, slab_cols)[:, kc, oc * 128:(oc + 1) * 128],
                            "rhs": rhs_get(kc),
                            "start": kc == 0, "stop": kc == nkc - 1}
                           for kc in range(nkc)]
                    t_mm = pe(ops, waits=tuple(w for w in (slab_tok[pbc], bk.free) if w))
                    mm_tok[pbc] = t_mm
                    pend.append((bk, og, t_mm))
                    if len(pend) == 2:
                        flush()
            flush()
            wsl_busy[0] = mm_tok[0]
            wsl_busy[1] = mm_tok[1]
            return _later(banks[0].free, banks[1].free)

        def z_get(buf):
            return lambda kc: buf[:, kc, :]

        t_x2w = None
        t_z2w = None
        t_scan = None
        t_tails = None
        t_hsw = None
        PHZ[0] = "A"
        for tt in range(n_tiles):
            first = tt == 0
            if not first:
                t_x = dma("S", X[:],
                          x_d.rearrange("(c p) n -> p c n", p=128)[:, :, tt * TILE:(tt + 1) * TILE],
                          waits=tuple(w for w in (t_x2w, t_z2, t_scan) if w))
                t_z = full_ln(lambda c: X[:, c, :], lambda c: z[:, c, :], TILE, 1e-5,
                              extra=(t_x, t_scan, t_z2w), mu_out=smu0[:, tt, :],
                              inv_out=sinv0[:, tt, :], stats=False)
            else:
                t_z = t_z0

            def ev_qk(bk, og, tok):
                dstb = qT if og < 8 else kT
                c = og % 8

                def f(v):
                    return v.tensor_scalar(dstb[:, c, :], bk.ps[:, 0:TILE],
                                           bqkv[:, og:og + 1], None, ALU.add)
                return ve(f, waits=(tok,))
            proj(qkvB, D, 2048, z_get(z), ev_qk, t_src=(t_qkvB, t_z, t_comb))

            t_vw0 = dma("S", wslv(0, 8, 512)[:, :, :],
                        qkvB.rearrange("(c p) n -> p c n", p=128)[:, :, 2048:2560],
                        waits=tuple(w for w in (wsl_busy[0],) if w))
            t_vw1 = dma("S", wslv(1, 8, 512)[:, :, :],
                        qkvB.rearrange("(c p) n -> p c n", p=128)[:, :, 2560:3072],
                        waits=tuple(w for w in (wsl_busy[1],) if w))
            for tb in range(NM):
                for vf in range(2):
                    bk = banks[(tb * 2 + vf) % 2]
                    ops = [{"out": bk.ps[:, 0:512],
                            "lhsT": z[:, kc, tb * 128:(tb + 1) * 128],
                            "rhs": wslv(vf, 8, 512)[:, kc, :],
                            "start": kc == 0, "stop": kc == NCH - 1}
                           for kc in range(NCH)]
                    t_mm = pe(ops, waits=((t_vw0, t_vw1)[vf], bk.free)
                              + ((t_z,) if tb == 0 else ()))
                    wsl_busy[vf] = t_mm

                    def f(v, bk=bk, tb=tb, vf=vf):
                        return v.tensor_add(vtok[:, 2 * tb + vf, :],
                                            bk.ps[:, 0:512], vb_bc[:, vf * 512:(vf + 1) * 512])
                    bk.free = ve(f, waits=(t_mm, t_vb))
            if first:
                t_hk0 = dma("S", wslv(0, 8, 512)[:, :, :],
                            qkvB.rearrange("(c p) n -> p c n", p=128)[:, :, 1024:1536],
                            waits=tuple(w for w in (wsl_busy[0],) if w))
                t_hk1 = dma("S", wslv(1, 8, 512)[:, :, :],
                            qkvB.rearrange("(c p) n -> p c n", p=128)[:, :, 1536:2048],
                            waits=tuple(w for w in (wsl_busy[1],) if w))
                for c in range(NCH):
                    bk = banks[2 + c % 2]
                    vvi = 0 if c < 4 else 1
                    ops = [{"out": bk.ps[:, 0:128],
                            "lhsT": wslv(vvi, 8, 512)[:, kc, (c % 4) * 128:(c % 4 + 1) * 128],
                            "rhs": hz[:, kc, :], "start": kc == 0, "stop": kc == NCH - 1}
                           for kc in range(NCH)]
                    t_mm = pe(ops, waits=((t_hk0, t_hk1)[vvi], t_hz, bk.free))
                    wsl_busy[vvi] = t_mm

                    def f(v, bk=bk, c=c):
                        return v.tensor_scalar(ktail[:, c, :], bk.ps[:, 0:128],
                                               bqkv[:, 8 + c:9 + c], None, ALU.add)
                    bk.free = ve(f, waits=(t_mm,))
                t_hv0 = dma("S", wslv(0, 8, 512)[:, :, :],
                            qkvB.rearrange("(c p) n -> p c n", p=128)[:, :, 2048:2560],
                            waits=tuple(w for w in (wsl_busy[0],) if w))
                t_hv1 = dma("S", wslv(1, 8, 512)[:, :, :],
                            qkvB.rearrange("(c p) n -> p c n", p=128)[:, :, 2560:3072],
                            waits=tuple(w for w in (wsl_busy[1],) if w))
                for vf in range(2):
                    bk = banks[4 + vf]
                    ops = [{"out": bk.ps[:, 0:512], "lhsT": hz[:, kc, :],
                            "rhs": wslv(vf, 8, 512)[:, kc, :],
                            "start": kc == 0, "stop": kc == NCH - 1}
                           for kc in range(NCH)]
                    t_mm = pe(ops, waits=((t_hv0, t_hv1)[vf], t_hz, bk.free))
                    wsl_busy[vf] = t_mm

                    def f(v, bk=bk, vf=vf):
                        return v.tensor_add(vtail[:, vf * 512:(vf + 1) * 512],
                                            bk.ps[:, 0:512], vb_bc[:, vf * 512:(vf + 1) * 512])
                    bk.free = ve(f, waits=(t_mm, t_vb))
                t_tails = _later(banks[2].free, banks[3].free, banks[4].free, banks[5].free)
            t_qkv_done = _later(*[b.free for b in banks])
            tap("qT", qT[:], waits=(t_qkv_done,))
            tap("kT", kT[:], waits=(t_qkv_done,))
            tap("vtok", vtok[:], waits=(t_qkv_done,))
            tap("ktail", ktail[:], waits=(t_tails,))
            tap("vtail", vtail[:], waits=(t_tails,))
            if upto == "qkv":
                return finish()

            sgrp_open = False
            t_lsum = None
            t_av_last = None
            for cpair in range(8):
                for s in range(-1, NM):
                    q0 = max(0, s * 128)
                    q1 = min(TILE, s * 128 + 256)
                    wq = q1 - q0
                    joff = q0 - s * 128
                    t_sc2 = [None, None]
                    for hh in range(2):
                        hb = 64 * hh
                        bk = banks[2 + hh]
                        ksrc = (ktail[hb:hb + 64, cpair, 0:128] if s < 0
                                else kT[hb:hb + 64, cpair, s * 128:(s + 1) * 128])
                        wl = [bk.free]
                        if cpair == 0 and s == -1:
                            wl += [t_qkv_done, t_tails]
                        t_sc2[hh] = pe([{"out": bk.ps[0:128, 0:wq], "lhsT": ksrc,
                                         "rhs": qT[hb:hb + 64, cpair, q0:q1]}],
                                       waits=tuple(w for w in wl if w))
                    t_e = ae_multi([
                        (tuple(w for w in (t_sc2[0], t_av_last) if w),
                         lambda a, wq=wq: a.activation(
                             out=etile[:, 0, 0:wq], in_=banks[2].ps[0:128, 0:wq],
                             func=AF.Exp, scale=0.125)),
                        ((t_sc2[1],),
                         lambda a, wq=wq: a.activation(
                             out=etile[:, 1, 0:wq], in_=banks[3].ps[0:128, 0:wq],
                             func=AF.Exp, scale=0.125)),
                    ])
                    banks[2].free = t_e[0]
                    banks[3].free = t_e[1]
                    mref = halom[:, 0:128] if (first and s < 0) else mask01[:, joff:joff + wq]

                    def fm(g, mref=mref, wq=wq):
                        g.tensor_mul(etile[:, 0, 0:wq], etile[:, 0, 0:wq], mref)
                        return g.tensor_mul(etile[:, 1, 0:wq], etile[:, 1, 0:wq], mref)
                    t_m = ge(fm, waits=(t_e[0], t_e[1]))
                    for mpart in (s, s + 1):
                        if mpart < 0 or mpart >= NM:
                            continue
                        mo = mpart * 128
                        bkL = banks[4]
                        bkS = banks[6]
                        eq0 = mpart * 128 - q0
                        opening = mpart == s + 1
                        closing = mpart == s
                        ops = []
                        wl = [t_m]
                        for hh in range(2):
                            h = 2 * cpair + hh
                            vs = (vtail[:, h * 64:(h + 1) * 64] if s < 0
                                  else vtok[:, 2 * s + (h * 64) // 512,
                                            (h * 64) % 512:(h * 64) % 512 + 64])
                            if opening and hh == 0 and mpart == 0:
                                wl.append(bkL.free)
                            ops.append({"out": bkL.ps[64 * hh:64 * (hh + 1), mo:mo + 128],
                                        "lhsT": vs, "rhs": etile[:, hh, eq0:eq0 + 128],
                                        "start": bool(opening),
                                        "stop": bool(closing),
                                        "tp": (0, 64) if hh == 1 else None})
                            s_open = bool(opening and hh == 0 and not sgrp_open)
                            s_close = bool(closing and hh == 1 and cpair == 7
                                           and mpart == NM - 1)
                            if s_open:
                                wl.append(bkS.free)
                                sgrp_open = True
                            ops.append({"out": bkS.ps[0:16, mo:mo + 128],
                                        "lhsT": oh16[:, h, :],
                                        "rhs": etile[:, hh, eq0:eq0 + 128],
                                        "start": s_open, "stop": s_close})
                        t_av = pe(ops, waits=tuple(w for w in wl if w))
                        t_av_last = t_av
                        if closing and mpart == NM - 1:
                            def fev(v, cpair=cpair):
                                return v.tensor_copy(bag1[:, cpair, :], banks[4].ps[:, 0:512])
                            banks[4].free = ve(fev, waits=(t_av,))
                        if closing and cpair == 7 and mpart == NM - 1:
                            def fsv(v):
                                return v.tensor_copy(lsum[:, :], banks[6].ps[0:16, 0:512])
                            t_lsum = ve(fsv, waits=(t_av,))
                            banks[6].free = t_lsum
            t_localu = banks[4].free
            tap("localu", bag1[:], waits=(t_localu,))
            tap("lsum", lsum[:], waits=(t_lsum,))

            def f_lr(v):
                v.tensor_scalar_max(lsum[:, :], lsum[:, :], 1e-30)
                return v.reciprocal(lsum[:, :], lsum[:, :])
            t_lrec = ve(f_lr, waits=(t_lsum,))
            if tt < n_tiles - 1:
                def f_tails(v):
                    for c in range(NCH):
                        v.tensor_copy(ktail[:, c, :], kT[:, c, TILE - 128:TILE])
                    v.tensor_copy(vtail[:, 0:512], vtok[:, 2 * (NM - 1), :])
                    return v.tensor_copy(vtail[:, 512:1024], vtok[:, 2 * (NM - 1) + 1, :])
                t_tails = ve(f_tails, waits=(t_qkv_done, t_localu))
            if upto == "local":
                return finish()

            def ev_rq(bk, og, tok):
                def f(v):
                    return v.tensor_scalar(rqTb[:, og, :], bk.ps[:, 0:TILE],
                                           brq[:, og:og + 1], None, ALU.add)
                return ve(f, waits=(tok,))
            proj(rqB, D, 256, z_get(z), ev_rq, t_src=(t_rqB,), slab_cols=256)
            t_rq = _later(banks[0].free, banks[1].free)
            tap("rqT", rqTb[:], waits=(t_rq,))

            lat_last = None
            lsum_lat = None
            for cpair in range(2):
                bk = banks[2 + cpair]
                ops = []
                for hh in range(2):
                    hb = 64 * hh
                    ops.append({"out": bk.ps[hb:hb + 64, 0:512],
                                "lhsT": rkallT[hb:hb + 64, cpair, :],
                                "rhs": rqTb[hb:hb + 64, cpair, :],
                                "tp": (64, 64) if hh == 1 else None})
                wl = [bk.free] + ([t_rq, t_rkrv] if cpair == 0 else [])
                t_ls = pe(ops, waits=tuple(w for w in wl if w))
                t_le = ae(lambda a, bk=bk, cpair=cpair: a.activation(
                    out=lexp[:, cpair, :], in_=bk.ps[:, 0:512],
                    func=AF.Exp, scale=0.125), waits=(t_ls,))
                bk.free = t_le

                def fm(g, cpair=cpair, tt=tt):
                    return g.tensor_mul(lexp[:, cpair, :], lexp[:, cpair, :],
                                        latm[:, tt * TILE:(tt + 1) * TILE])
                t_lm = ge(fm, waits=(t_le,))
                bkL = banks[4] if cpair == 0 else banks[5]
                ops = []
                for hh in range(2):
                    hb = 64 * hh
                    lh = 2 * cpair + hh
                    ops.append({"out": bkL.ps[hb:hb + 64, 0:512],
                                "lhsT": rvlm[hb:hb + 64, lh * 64:(lh + 1) * 64],
                                "rhs": lexp[hb:hb + 64, cpair, :],
                                "tp": (64, 64) if hh == 1 else None})
                ops.append({"out": banks[6].ps[0:4, 0:512],
                            "lhsT": ohlat[:, cpair, :],
                            "rhs": lexp[:, cpair, :],
                            "start": cpair == 0, "stop": cpair == 1})
                wl = [t_lm, bkL.free] + ([banks[6].free] if cpair == 0 else [])
                t_lv = pe(ops, waits=tuple(w for w in wl if w))

                def flv(v, bkL=bkL, cpair=cpair):
                    return v.tensor_copy(latu[:, cpair, :], bkL.ps[:, 0:512])
                bkL.free = ve(flv, waits=(t_lv,))
                lat_last = _later(lat_last, bkL.free)
                if cpair == 1:
                    def fls(v):
                        return v.tensor_copy(latsum[:, :], banks[6].ps[0:4, 0:512])
                    lsum_lat = ve(fls, waits=(t_lv,))
                    banks[6].free = lsum_lat
            tap("latu", latu[:], waits=(lat_last,))
            tap("latsum", latsum[:], waits=(lsum_lat,))

            def f_latrec(v):
                v.tensor_scalar_max(latsum[:, :], latsum[:, :], 1e-30)
                v.reciprocal(latsum[:, :], latsum[:, :])
                return v.tensor_copy(latsb[:, :], latsum[:, :])
            t_latrec = ve(f_latrec, waits=(lsum_lat,))

            for cpair in range(2):
                bk = banks[2 + cpair]
                t_bc = pe([{"out": bk.ps[:, 0:512], "lhsT": bclat[:, cpair, :],
                            "rhs": latsb[:, :]}],
                          waits=(t_latrec, bk.free))

                def f(v, bk=bk, cpair=cpair):
                    return v.tensor_mul(latnb[:, cpair, :], latu[:, cpair, :],
                                        bk.ps[:, 0:512])
                bk.free = ve(f, waits=(t_bc, lat_last))
            t_latn = _later(banks[2].free, banks[3].free)
            tap("latn", latnb[:], waits=(t_latn,))

            t_rol = dma("S", wslv(0, 2, 1024)[:, :, :],
                        routB.rearrange("(c p) n -> p c n", p=128),
                        waits=tuple(w for w in (t_routB, wsl_busy[0]) if w))
            for og in range(8):
                bk = banks[og % 2]
                ops = [{"out": bk.ps[:, 0:TILE],
                        "lhsT": wslv(0, 2, 1024)[:, kc, og * 128:(og + 1) * 128],
                        "rhs": latnb[:, kc, :],
                        "start": kc == 0, "stop": kc == 1}
                       for kc in range(2)]
                wl = [t_rol, bk.free] + ([t_latn] if og == 0 else [])
                t_mm = pe(ops, waits=tuple(w for w in wl if w))
                wsl_busy[0] = t_mm

                def f(v, bk=bk, og=og):
                    return v.tensor_scalar(bag2[:, og, :], bk.ps[:, 0:TILE],
                                           brout[:, og:og + 1], None, ALU.add)
                bk.free = ve(f, waits=(t_mm,))
            t_remote = _later(banks[0].free, banks[1].free)
            tap("remote", bag2[:], waits=(t_remote,))

            bkl, bkr = banks[2], banks[3]
            opsl = [{"out": bkl.ps[0:16, 0:512], "lhsT": glgr[:, 0, c, :],
                     "rhs": bag1[:, c, :],
                     "start": c == 0, "stop": c == NCH - 1} for c in range(NCH)]
            opsr = [{"out": bkr.ps[0:16, 0:512], "lhsT": glgr[:, 1, c, :],
                     "rhs": bag2[:, c, :],
                     "start": c == 0, "stop": c == NCH - 1} for c in range(NCH)]
            t_gl = pe(opsl, waits=(bkl.free, t_localu))
            t_gr = pe(opsr, waits=(bkr.free, t_remote))

            def f_g(v):
                v.tensor_mul(gtmp[:, :], bkl.ps[0:16, 0:512], lsum[:, :])
                return v.tensor_add(gtmp[:, :], gtmp[:, :], bkr.ps[0:16, 0:512])
            t_gd = ve(f_g, waits=(t_gl, t_gr, t_lrec))
            t_sig = ae(lambda a: a.activation(out=grow[0:16, :], in_=gtmp[:, :],
                                              func=AF.Sigmoid), waits=(t_gd,))

            def f_g2(v):
                v.tensor_mul(ghb[:, :], grow[0:16, :], lsum[:, :])
                return v.tensor_scalar(gm1b[:, :], grow[0:16, :],
                                       -1.0, 1.0, ALU.mult, ALU.add)
            t_grow = ve(f_g2, waits=(t_sig,))
            bkl.free = t_grow
            bkr.free = t_grow
            tap("grow", grow[:], waits=(t_grow,))

            t_mix = None
            for cpair in range(8):
                bka = banks[2 + cpair % 2]
                bkb = banks[4 + cpair % 2]
                wl = [bka.free] + ([t_grow] if cpair == 0 else [])
                t_b1 = pe([{"out": bka.ps[:, 0:512], "lhsT": bc2[0:16, cpair, :],
                            "rhs": ghb[:, :]}],
                          waits=tuple(w for w in wl if w))
                t_b2 = pe([{"out": bkb.ps[:, 0:512], "lhsT": bc2[0:16, cpair, :],
                            "rhs": gm1b[:, :]}],
                          waits=(bkb.free,) if bkb.free else ())

                def f(v, bka=bka, bkb=bkb, cpair=cpair):
                    v.tensor_mul(z[:, cpair, :], bag1[:, cpair, :], bka.ps[:, 0:512])
                    v.tensor_mul(mtmp[:, :], bag2[:, cpair, :], bkb.ps[:, 0:512])
                    return v.tensor_add(z[:, cpair, :], z[:, cpair, :], mtmp[:, :])
                tk = ve(f, waits=(t_b1, t_b2))
                bka.free = tk
                bkb.free = tk
                t_mix = tk
            tap("mixed", z[:], waits=(t_mix,))
            if upto == "mix":
                return finish()

            def ev_out(bk, og, tok):
                def f(v):
                    return v.scalar_tensor_tensor(X[:, og, :], bk.ps[:, 0:TILE],
                                                  bout[:, og:og + 1], X[:, og, :],
                                                  ALU.add, ALU.add)
                return ve(f, waits=(tok,))
            proj(outwB, D, D, z_get(z), ev_out, t_src=(t_outwB, t_mix))
            t_x1 = _later(banks[0].free, banks[1].free)
            tap("X1", X[:], waits=(t_x1,))

            t_z1 = full_ln(lambda c: X[:, c, :], lambda c: z[:, c, :], TILE, 1e-5,
                           extra=(t_x1,))
            tap("z1", z[:], waits=(t_z1,))

            def ev_w1(bk, og, tok):
                def f(a):
                    return a.activation(out=big1[:, og, :], in_=bk.ps[:, 0:TILE],
                                        func=AF.Gelu, bias=b1v[:, og:og + 1],
                                        scale=alpha1[:, og:og + 1])
                return ae(f, waits=(tok,))
            def ev_w1_2(b0, o0, k0, b1, o1, k1):
                return ae_multi([
                    ((k0,), lambda a: a.activation(out=big1[:, o0, :], in_=b0.ps[:, 0:TILE],
                                                   func=AF.Gelu, bias=b1v[:, o0:o0 + 1],
                                                   scale=alpha1[:, o0:o0 + 1])),
                    ((k1,), lambda a: a.activation(out=big1[:, o1, :], in_=b1.ps[:, 0:TILE],
                                                   func=AF.Gelu, bias=b1v[:, o1:o1 + 1],
                                                   scale=alpha1[:, o1:o1 + 1])),
                ])
            proj(w1sB, D, 4096, z_get(z), ev_w1,
                 t_src=(t_z1, t_mix, t_grow, t_localu, t_qkv_done, t_tails),
                 evac2=ev_w1_2)
            t_h0 = _later(banks[0].free, banks[1].free)
            tap("hffn", big1[:], waits=(t_h0,))

            def ev_w2(bk, og, tok):
                def f(v):
                    v.tensor_scalar(sqs[0][:, :], bk.ps[:, 0:TILE],
                                    alpha2[:, og:og + 1], b2v[:, og:og + 1],
                                    ALU.mult, ALU.add)
                    return v.tensor_add(X[:, og, :], X[:, og, :], sqs[0][:, :])
                return ve(f, waits=(tok,))
            proj(w2sB, 4096, D, z_get(big1), ev_w2,
                 t_src=(t_h0,), slab_cols=128)
            t_x2 = _later(banks[0].free, banks[1].free)
            tap("X2", X[:], waits=(t_x2,))

            t_z2 = full_ln(lambda c: X[:, c, :], lambda c: z[:, c, :], TILE, 1e-5,
                           extra=(t_x2,))
            tap("z2", z[:], waits=(t_z2,))
            t_x2w = dma("A", x2d.rearrange("(c p) n -> p c n", p=128)[:, :, tt * TILE:(tt + 1) * TILE],
                        X[:], waits=(t_x2,))
            t_z2w = dma("A", z2d.rearrange("(c p) n -> p c n", p=128)[:, :, tt * TILE:(tt + 1) * TILE],
                        z[:], waits=(t_z2,))

            bkl, bkb = banks[0], banks[1]
            opsl = [{"out": bkl.ps[0:16, 0:512], "lhsT": lamTw[:, kc, :],
                     "rhs": z[:, kc, :],
                     "start": kc == 0, "stop": kc == NCH - 1} for kc in range(NCH)]
            opsb = [{"out": bkb.ps[0:16, 0:512], "lhsT": wbTw[:, kc, :],
                     "rhs": z[:, kc, :],
                     "start": kc == 0, "stop": kc == NCH - 1} for kc in range(NCH)]
            t_l = pe(opsl, waits=(bkl.free, t_z2))
            t_b = pe(opsb, waits=(bkb.free,))
            t_sg = ae(lambda a: a.activation(out=lamrow[:, :], in_=bkl.ps[0:16, 0:512],
                                             func=AF.Sigmoid, bias=blam[:, :]),
                      waits=(t_l,))
            t_cb = ve(lambda v: v.tensor_copy(bxrow[:, :], bkb.ps[0:16, 0:512]),
                      waits=(t_b,))
            bkl.free = t_sg
            bkb.free = t_cb
            tap("lam", lamrow[:], waits=(t_sg,))
            tap("bx", bxrow[:], waits=(t_cb,))

            def f_scan(v):
                v.tensor_tensor_scan(hslb[:, :], lamrow[:, :], bxrow[:, :],
                                     scarry[:, :], ALU.mult, ALU.add)
                v.tensor_tensor_scan(acumb[:, :], lamrow[:, :], zer16[:, :],
                                     ccarry[:, :], ALU.mult, ALU.add)
                v.tensor_copy(gtmp[:, :], hslb[:, :])
                v.tensor_copy(scarry[:, :], FA2[0:16, TILE - 1:TILE])
                return v.tensor_copy(ccarry[:, :], FA2[0:16, 2 * TILE - 1:2 * TILE])
            t_scan = ve(f_scan, waits=(t_sg, t_cb, t_misc, t_hsw))
            banks[0].free = t_scan
            banks[1].free = t_scan
            t_hsw = dmas("A", [(hsd[:, tt * TILE:(tt + 1) * TILE], hslb[:, :]),
                               (acd[:, tt * TILE:(tt + 1) * TILE], acumb[:, :])],
                         waits=(t_scan,))
            tap("hsl", hslb[:, :], waits=(t_scan,))
            tap("acum", acumb[:, :], waits=(t_scan,))

        if upto == "phaseA":
            return finish()

        PHZ[0] = "B"
        t_hw = dma("G", agh_in, scarry[:, :], waits=(t_scan,))
        box2 = {}

        def f_cc2(g):
            _wait(g, t_hw)
            i = g.collective_compute("AllGather", ALU.bypass,
                                     replica_groups=[[0, 1, 2, 3, 4, 5, 6, 7]],
                                     ins=[agh_in], outs=[agh_out])
            box2["t"] = sGP.inc(i)
        block.gpsimd(f_cc2)
        t_agh = dma("S", agh_s[:], agh_out.rearrange("(r k) one -> r (k one)", k=16),
                    waits=(box2["t"],))
        t_selmm = pe([{"out": banks[0].ps[0:16, 0:1], "lhsT": agh_s[:, :],
                       "rhs": selHs[:, :]}],
                     waits=tuple(w for w in (t_agh, banks[0].free) if w))
        t_init = ve(lambda v: v.tensor_copy(initv[:, :], banks[0].ps[0:16, 0:1]),
                    waits=(t_selmm,))
        banks[0].free = t_init
        tap("hft", scarry[:], waits=(t_hw,))
        tap("initv", initv[:], waits=(t_init,))
        if upto == "phaseB":
            return finish()

        t_h1 = None
        t_hs = None
        t_odma = None
        t_y = None
        t_x3 = None
        PHZ[0] = "C"
        for tt in range(n_tiles):
            t_x2r = dma("S", X[:],
                        x2d.rearrange("(c p) n -> p c n", p=128)[:, :, tt * TILE:(tt + 1) * TILE],
                        waits=tuple(w for w in (t_x2w, t_odma) if w))
            t_z2r = dma("S", z[:],
                        z2d.rearrange("(c p) n -> p c n", p=128)[:, :, tt * TILE:(tt + 1) * TILE],
                        waits=tuple(w for w in (t_z2w, t_h1) if w))
            t_hsr = dmas("S", [(hslb[:, :], hsd[:, tt * TILE:(tt + 1) * TILE]),
                               (acumb[:, :], acd[:, tt * TILE:(tt + 1) * TILE])],
                         waits=tuple(w for w in (t_hsw, t_hs) if w))
            if tt == 0:
                tap("cx2", X[:], waits=(t_x2r,))
                tap("cz2", z[:], waits=(t_z2r,))
                tap("chsl", hslb[:, :], waits=(t_hsr,))

            if SERIAL[0] or 1 in JOINS[0]:
                joinpoint()

            def ev_wg(bk, og, tok):
                def f(a):
                    return a.activation(out=bag1[:, og, :], in_=bk.ps[:, 0:TILE],
                                        func=AF.Sigmoid)
                return ae(f, waits=(tok,))

            def ev_wg2(b0, o0, k0, b1, o1, k1):
                return ae_multi([
                    ((k0,), lambda a: a.activation(out=bag1[:, o0, :], in_=b0.ps[:, 0:TILE],
                                                   func=AF.Sigmoid)),
                    ((k1,), lambda a: a.activation(out=bag1[:, o1, :], in_=b1.ps[:, 0:TILE],
                                                   func=AF.Sigmoid)),
                ])
            proj(WgB, D, D, z_get(z), ev_wg,
                 t_src=(t_WgB, t_z2r) + tuple(w for w in (t_y, t_x3) if w), evac2=ev_wg2)
            t_gate = _later(banks[0].free, banks[1].free)
            tap("gate1", bag1[:], waits=(t_gate,))
            if SERIAL[0] or 2 in JOINS[0]:
                joinpoint()

            def f_hs(v):
                v.scalar_tensor_tensor(gtmp[:, :], acumb[:, :], initv[:, 0:1],
                                       hslb[:, :], ALU.mult, ALU.add)
                return v.tensor_copy(hsbf[:, :], gtmp[:, :])
            t_hs = ve(f_hs, waits=tuple(w for w in (t_init, t_hsr, t_y) if w))

            t_y = None
            for oc in range(NCH):
                bk = banks[2 + oc % 2]
                wl = [bk.free] + ([t_hs] if oc == 0 else [])
                t_mm = pe([{"out": bk.ps[:, 0:512], "lhsT": wcT[:, oc, :],
                            "rhs": hsbf[:, :]}],
                          waits=tuple(w for w in wl if w))

                def f(v, bk=bk, oc=oc):
                    return v.tensor_mul(qT[:, oc, :], bk.ps[:, 0:512], bag1[:, oc, :])
                bk.free = ve(f, waits=(t_mm, t_gate))
                t_y = bk.free
            tap("yv", qT[:], waits=(t_y,))
            if SERIAL[0] or 3 in JOINS[0]:
                joinpoint()

            b7 = banks[7]
            prev = [None, None]
            tsq_mm = None
            for c0 in range(0, NCH, 2):
                wl = [prev[0], prev[1]] + ([t_y] if c0 == 0 else [])
                tp = ae_multi([
                    (tuple(w for w in wl if w),
                     lambda a, c=c0: a.activation(
                         out=sqs[0][:, :], in_=qT[:, c, :], func=AF.Square)),
                    ((),
                     lambda a, c=c0 + 1: a.activation(
                         out=sqs[1][:, :], in_=qT[:, c, :], func=AF.Square)),
                ])
                for i, c in enumerate((c0, c0 + 1)):
                    tsq_mm = pe([{"out": b7.ps[0:1, 0:512], "lhsT": onesf[:, :],
                                  "rhs": sqs[c % 2][:, :],
                                  "start": c == 0, "stop": c == NCH - 1}],
                                waits=(tp[i], b7.free) if c == 0 else (tp[i],))
                    prev[c % 2] = tsq_mm
            t_ms = ve(lambda v: v.tensor_scalar(srow[:, 0, 0:TILE], b7.ps[0:1, 0:TILE],
                                                1.0 / D, 1e-6, ALU.mult, ALU.add),
                      waits=(tsq_mm,))
            t_msq = ae(lambda a: a.activation(out=srow[:, 0, 0:TILE], in_=srow[:, 0, 0:TILE],
                                              func=AF.Sqrt), waits=(t_ms,))

            def f_rrec(v):
                v.reciprocal(srow[:, 0, 0:TILE], srow[:, 0, 0:TILE])
                return v.tensor_copy(srowb_rms[:, 0:TILE], srow[:, 0, 0:TILE])
            t_rrec = ve(f_rrec, waits=(t_msq,))
            b7.free = t_rrec
            tap("rmsrow", srow[:, 0, :], waits=(t_rrec,))

            bk5 = banks[5]
            t_bc = pe([{"out": bk5.ps[:, 0:512], "lhsT": ones1[:, :],
                        "rhs": srowb_rms[:, 0:TILE]}],
                      waits=tuple(w for w in (bk5.free, t_rrec) if w))
            t_x3 = None
            for c in range(NCH):
                def f(v, c=c):
                    v.scalar_tensor_tensor(sqs[0][:, :], qT[:, c, :],
                                           rmsw[:, c:c + 1], bk5.ps[:, 0:512],
                                           ALU.mult, ALU.mult)
                    return v.tensor_add(X[:, c, :], X[:, c, :], sqs[0][:, :])
                wl = [t_bc] + ([t_x2r] if c == 0 else [])
                t_x3 = ve(f, waits=tuple(w for w in wl if w))
            bk5.free = t_x3
            tap("X3", X[:], waits=(t_x3,))
            if SERIAL[0] or 4 in JOINS[0]:
                joinpoint()

            t_z3 = full_ln(lambda c: X[:, c, :], lambda c: z[:, c, :], TILE, 1e-5,
                           extra=(t_x3,))
            tap("z3", z[:], waits=(t_z3,))
            if SERIAL[0] or 5 in JOINS[0]:
                joinpoint()

            def ev_g(bk, og, tok):
                def f(a):
                    return a.activation(out=big1[:, og, :], in_=bk.ps[:, 0:TILE],
                                        func=AF.Silu, scale=1.0 / 127.0)
                return ae(f, waits=(tok,))

            def ev_g2(b0, o0, k0, b1, o1, k1):
                return ae_multi([
                    ((k0,), lambda a: a.activation(out=big1[:, o0, :], in_=b0.ps[:, 0:TILE],
                                                   func=AF.Silu, scale=1.0 / 127.0)),
                    ((k1,), lambda a: a.activation(out=big1[:, o1, :], in_=b1.ps[:, 0:TILE],
                                                   func=AF.Silu, scale=1.0 / 127.0)),
                ])
            proj(wgB, D, 3072, z_get(z), ev_g,
                 t_src=(t_z3, t_y, t_gate), evac2=ev_g2)
            t_sg1 = _later(banks[0].free, banks[1].free)
            if SERIAL[0] or 6 in JOINS[0]:
                joinpoint()

            def ev_u(bk, og, tok):
                def f(v):
                    return v.scalar_tensor_tensor(big1[:, og, :], bk.ps[:, 0:TILE],
                                                  1.0 / 127.0, big1[:, og, :],
                                                  ALU.mult, ALU.mult)
                return ve(f, waits=(tok, t_sg1))
            proj(wuB, D, 3072, z_get(z), ev_u)
            t_h1 = _later(banks[0].free, banks[1].free)
            if SERIAL[0] or 7 in JOINS[0]:
                joinpoint()

            def ev_d(bk, og, tok):
                def f(v):
                    return v.scalar_tensor_tensor(X[:, og, :], bk.ps[:, 0:TILE],
                                                  1.0 / 127.0, X[:, og, :],
                                                  ALU.mult, ALU.add)
                return ve(f, waits=(tok,))
            proj(wdB, 3072, D, z_get(big1), ev_d,
                 t_src=(t_h1,), slab_cols=256)
            t_out = _later(banks[0].free, banks[1].free)
            t_odma = dma("A", out_d.rearrange("(c p) n -> p c n", p=128)[:, :, tt * TILE:(tt + 1) * TILE],
                         X[:], waits=(t_out,))
            if SERIAL[0] or 8 in JOINS[0]:
                joinpoint()

        return finish()


def _mk_consts(core):
    half = core % 2
    consts = {}
    consts["c_ident"] = np.eye(128, dtype=BF)
    j = np.arange(256)[None, :]
    kl = np.arange(128)[:, None]
    consts["c_mask"] = ((1 <= j - kl) & (j - kl <= 128)).astype(BF)
    consts["c_halom"] = (np.ascontiguousarray(consts["c_mask"][:, 128:256]) if half == 1
                         else np.zeros((128, 128), BF))
    cb = half * 32
    lat = np.arange(64)[:, None]
    chunk = cb + np.arange(M)[None, :] // 128
    lm = (lat < chunk)
    consts["c_latm"] = np.concatenate([lm, lm], axis=0).astype(BF)
    oh = np.zeros((128, 16, 16), BF)
    for h in range(16):
        oh[:, h, h] = 1
    consts["c_oh16"] = oh
    ohl = np.zeros((128, 2, 4), BF)
    for c in range(2):
        ohl[0:64, c, 2 * c] = 1
        ohl[64:128, c, 2 * c + 1] = 1
    consts["c_ohlat"] = ohl
    bc2 = np.zeros((64, 8, 128), BF)
    for c in range(8):
        bc2[2 * c, c, 0:64] = 1
        bc2[2 * c + 1, c, 64:128] = 1
        bc2[32 + 2 * c, c, 0:64] = 1
        bc2[32 + 2 * c + 1, c, 64:128] = 1
    consts["c_bc2"] = bc2
    bcl = np.zeros((4, 2, 128), BF)
    for c in range(2):
        bcl[2 * c, c, 0:64] = 1
        bcl[2 * c + 1, c, 64:128] = 1
    consts["c_bclat"] = bcl
    consts["c_ones1"] = np.ones((1, 128), BF)
    consts["c_onesf"] = np.ones((128, 1), np.float32)
    selH = np.zeros((8, 1), np.float32)
    if half == 1:
        selH[core - 1, 0] = 1.0
    consts["c_selH"] = selH
    selA = np.zeros((128, 8), np.float32)
    selA[:, core - 1 if half == 1 else core] = 1.0
    consts["c_selA"] = selA
    selC = np.zeros((128, 8), np.float32)
    if half == 1:
        selC[:, core] = 1.0
    consts["c_selC"] = selC
    padmv = np.zeros((128, 128), BF)
    for reg in range(4):
        if (reg % 2) == half:
            padmv[:, reg * 32:(reg + 1) * 32] = np.float32(1.0 / 128.0)
    consts["c_padm"] = padmv
    return consts


def make_in_maps(inputs):
    x = np.asarray(inputs["L0_attn_x"])
    qkv_b = np.asarray(inputs["L0_attn_qkv_b"])
    in_maps = []
    for core in range(8):
        b, half = core // 2, core % 2
        g0 = half * M
        im = {}
        im["x"] = np.ascontiguousarray(x[b, g0:g0 + M, :].T)
        im["xh"] = (np.ascontiguousarray(x[b, g0 - 128:g0, :].T) if half == 1
                    else np.zeros((D, 128), np.float32))
        im["qkv_w"] = np.asarray(inputs["L0_attn_qkv_w"])
        im["rq_w"] = np.asarray(inputs["L0_attn_rq_w"])
        im["rkv_w"] = np.asarray(inputs["L0_attn_rkv_w"])
        im["rout_w"] = np.asarray(inputs["L0_attn_rout_w"])
        im["out_w"] = np.asarray(inputs["L0_attn_out_w"])
        im["gl_w"] = np.asarray(inputs["L0_attn_gl_w"])
        im["gr_w"] = np.asarray(inputs["L0_attn_gr_w"])
        im["w1"] = np.asarray(inputs["L0_ffn_w1"])
        im["w2"] = np.asarray(inputs["L0_ffn_w2"])
        im["w1t"] = np.ascontiguousarray(np.asarray(inputs["L0_ffn_w1"]).T)
        im["w2t"] = np.ascontiguousarray(np.asarray(inputs["L0_ffn_w2"]).T)
        im["W_lambda"] = np.asarray(inputs["L1_attn_W_lambda"])
        im["W_B"] = np.asarray(inputs["L1_attn_W_B"])
        im["W_C"] = np.asarray(inputs["L1_attn_W_C"])
        im["W_gate"] = np.ascontiguousarray(np.asarray(inputs["L1_attn_W_gate"]).T)
        im["wg"] = np.ascontiguousarray(np.asarray(inputs["L1_ffn_w_gate"]).T)
        im["wu"] = np.ascontiguousarray(np.asarray(inputs["L1_ffn_w_up"]).T)
        im["wd"] = np.ascontiguousarray(np.asarray(inputs["L1_ffn_w_down"]).T)
        im["b_qkv"] = np.ascontiguousarray(qkv_b.reshape(24, 128).T)
        im["b_rq"] = np.ascontiguousarray(np.asarray(inputs["L0_attn_rq_b"]).reshape(2, 128).T)
        im["b_rkv"] = np.ascontiguousarray(np.asarray(inputs["L0_attn_rkv_b"]).reshape(4, 128).T)
        im["b_rout"] = np.ascontiguousarray(np.asarray(inputs["L0_attn_rout_b"]).reshape(8, 128).T)
        im["b_out"] = np.ascontiguousarray(np.asarray(inputs["L0_attn_out_b"]).reshape(8, 128).T)
        im["b_1"] = np.ascontiguousarray(np.asarray(inputs["L0_ffn_b1"]).reshape(32, 128).T)
        im["b_2"] = np.ascontiguousarray(np.asarray(inputs["L0_ffn_b2"]).reshape(8, 128).T)
        im["b_lam"] = np.ascontiguousarray(np.asarray(inputs["L1_attn_b_lambda"]).reshape(16, 1))
        im["rmsw"] = np.ascontiguousarray(np.asarray(inputs["L1_attn_rms_weight"]).reshape(8, 128).T)
        im["vb_row"] = np.ascontiguousarray(qkv_b[2 * D:].reshape(1, D))
        im["rv_row"] = np.ascontiguousarray(np.asarray(inputs["L0_attn_rkv_b"])[256:].reshape(1, 256))
        im.update(_mk_consts(core))
        in_maps.append(im)
    return in_maps


def kernel(**inputs):
    nc = build_program()
    in_maps = make_in_maps(inputs)
    res = run_bass_kernel_spmd(nc, in_maps, core_ids=list(range(8)))
    out = np.empty((4, 2 * M, D), np.float32)
    for core in range(8):
        b, half = core // 2, core % 2
        out[b, half * M:(half + 1) * M, :] = res.results[core]["out"].T
    return out
